# version 12
# baseline (speedup 1.0000x reference)
import sys
import os

for _p in ("/opt/trn_rl_repo",):
    if _p not in sys.path and os.path.isdir(_p):
        sys.path.insert(0, _p)

import numpy as np
import concourse.bass as bass
import concourse.mybir as mybir
from concourse.bass_utils import run_bass_kernel_spmd
from contextlib import ExitStack

F32 = mybir.dt.float32
BF16 = mybir.dt.bfloat16
I32 = mybir.dt.int32
AF = mybir.ActivationFunctionType
ALU = mybir.AluOpType
AX = mybir.AxisListType

D = 2048
NF = 16
EPS = 1e-5
PI = float(np.pi)
TWO_PI = float(2 * np.pi)
E_SLOT = [2, 4, 6, 8, 10, 12, 14, 16]
GBLK = {0: [0, 3, 4, 7, 8, 11, 12, 15],
        1: [1, 2, 5, 6, 9, 10, 13, 14]}
QSCALE = float(1.0 / np.sqrt(128.0))
f32 = np.float32


class Sem:
    def __init__(self, nc, ctx, name):
        self.h = ctx.enter_context(nc.semaphore(name))
        self.n = 0

    def inc(self, instr, k=1):
        instr.then_inc(self.h, k)
        self.n += k
        return self.n


class SbufMap:
    def __init__(self, nc, base, top):
        self.nc = nc
        self.base = base
        self.top = top
        self.placed = []

    def alloc(self, name, cols, dtype, phases):
        size = cols * mybir.dt.size(dtype)
        ph = set(phases)
        placed = sorted([p for p in self.placed if p[3] & ph], key=lambda p: p[1])
        cands = sorted(set([self.base] + [o + s for (_, o, s, _) in placed]))
        off = None
        for c in cands:
            c = (c + 63) & ~63
            if c + size > self.top:
                continue
            if all(c + size <= o or c >= o + s for (_, o, s, _) in placed):
                off = c
                break
        if off is None:
            lines = [f"{n}@{o}+{sz}" for (n, o, sz, p) in placed]
            raise AssertionError(
                f"SBUF alloc failed: {name} ({size}B) phases={sorted(ph)}\n"
                + "\n".join(lines))
        self.placed.append((name, off, size, ph))
        return self.nc.alloc_sbuf_tensor_at(name, [128, cols], dtype, offset=off)


def view3(ap_tensor, inner):
    return ap_tensor[:].rearrange("p (f c) -> p f c", c=inner)


def build_program():
    nc = bass.Bass("TRN2", target_bir_lowering=False, debug=False)

    dx_own = nc.dram_tensor("x_own", [8, 128, D], F32, kind="ExternalInput").ap()
    dx_seq = nc.dram_tensor("x_seq", [16, 128, D], F32, kind="ExternalInput").ap()
    dqpos = nc.dram_tensor("qpos", [128, 8], F32, kind="ExternalInput").ap()
    dqrow = nc.dram_tensor("qposrow", [128, 1024], F32, kind="ExternalInput").ap()
    dpvec = nc.dram_tensor("pvec", [4, 128, 128], F32, kind="ExternalInput").ap()
    dwq = nc.dram_tensor("wq_s", [16, 2048, 128], F32, kind="ExternalInput").ap()
    dwk = nc.dram_tensor("wk", [2048, 512], F32, kind="ExternalInput").ap()
    dwv = nc.dram_tensor("wv", [2048, 512], F32, kind="ExternalInput").ap()
    dwo = nc.dram_tensor("wo_s", [16, 2048, 128], F32, kind="ExternalInput").ap()
    dwamp = nc.dram_tensor("ampw_s", [16, 2048, 128], F32, kind="ExternalInput").ap()
    dwph = nc.dram_tensor("phw_s", [16, 2048, 128], F32, kind="ExternalInput").ap()
    dwe = nc.dram_tensor("entw_s", [2, 64, 2048, 128], F32, kind="ExternalInput").ap()
    dwpw = nc.dram_tensor("pw_s", [2, 16, 8192, 128], F32, kind="ExternalInput").ap()
    dwpa = nc.dram_tensor("path_s", [2, 2, 16, 2048, 128], F32,
                          kind="ExternalInput").ap()
    dout = nc.dram_tensor("out", [8, 128, D], F32, kind="ExternalOutput").ap()
    dx2 = nc.dram_tensor("dbg_x2", [8, 128, D], F32, kind="ExternalOutput").ap()

    with ExitStack() as ctx:
        ps_s = ctx.enter_context(nc.psum_tensor("ps_s", [128, 2048], F32))
        ps_y = ctx.enter_context(nc.psum_tensor("ps_y", [128, 1024], F32))
        ps_p = ctx.enter_context(nc.psum_tensor("ps_p", [128, 512], F32))
        psT = ctx.enter_context(nc.psum_tensor("psT", [128, 1024], BF16))

        M = SbufMap(nc, base=35840, top=229340)
        ALL = list(range(12))
        params = M.alloc("params", 1024, F32, ALL)
        statsb = M.alloc("statsb", 128, F32, ALL)
        ident = M.alloc("ident", 128, BF16, ALL)
        ident32 = M.alloc("ident32", 128, F32, [0])
        ones32 = M.alloc("ones32", 128, F32, ALL)
        tmpA = M.alloc("tmpA", 2048, F32, [0, 1, 2, 3, 5, 6, 7, 8, 9, 10, 11])
        tmpB = M.alloc("tmpB", 2048, F32, [0, 1, 2, 3, 5, 6, 7, 8, 9, 10, 11])
        lnout2_bufs = [M.alloc("lnout", 2048, BF16, [1, 2, 3, 5, 11]),
                       M.alloc("lnoutB", 2048, BF16, [1, 2, 3, 5, 11])]
        lnout = lnout2_bufs[0]
        dumpb = M.alloc("dumpb", 2048, BF16, [1, 2, 3, 5])
        xst = [M.alloc("xstage0", 2048, F32, [0, 1, 2, 3, 5, 11]),
               M.alloc("xstage1", 2048, F32, [0, 1, 2, 3, 5, 11])]
        s_iota_i = M.alloc("s_iota_i", 2048, I32, [0])
        s_iota = M.alloc("s_iota", 2048, F32, list(range(5)))
        qposrow = M.alloc("qposrow", 1024, F32, [0])
        qpos = M.alloc("qpos", 8, F32, list(range(5)))
        ii0 = M.alloc("ii0", 128, I32, [0])
        ii1 = M.alloc("ii1", 128, I32, [0])
        pvst = [M.alloc("pvst0", 128, F32, [0]), M.alloc("pvst1", 128, F32, [0])]
        tbs = [M.alloc("tbs0", 2048, F32, [0]), M.alloc("tbs1", 2048, F32, [0])]
        qsin = M.alloc("qsin", 1024, F32, [0, 1, 2])
        qcos = M.alloc("qcos", 1024, F32, [0, 1, 2])
        ksin = M.alloc("ksin", 2048, F32, [0, 1, 2, 3])
        kcos = M.alloc("kcos", 2048, F32, [0, 1, 2, 3])
        ln1ownT_p = [M.alloc(f"ln1ownT{i}", 8192, BF16, [1, 2]) for i in range(2)]
        wqr = [M.alloc("wqr0", 2048, BF16, [0, 1, 2]), M.alloc("wqr1", 2048, BF16, [0, 1, 2])]
        wk_sb = M.alloc("wk_sb", 8192, BF16, [0, 1, 2, 3])
        wv_sb = M.alloc("wv_sb", 8192, BF16, [0, 1, 2, 3])
        lnT = M.alloc("lnT", 8192, BF16, [3])
        Q_p = [M.alloc(f"Q{i}", 4096, BF16, [2, 3, 4]) for i in range(4)]
        K_p = [M.alloc(f"K{i}", 2048, BF16, [3, 4]) for i in range(4)]
        V_p = [M.alloc(f"V{i}", 2048, BF16, [3, 4]) for i in range(4)]
        masks = M.alloc("masks", 4096, BF16, [4])
        PbE = [M.alloc(f"PbE{i}", 2048, BF16, [4]) for i in range(4)]
        PbM = M.alloc("PbM", 2048, BF16, [4])
        PT_ap = [M.alloc(f"PTa{i}", 4096, BF16, [4]) for i in range(2)]
        PT_bp = [M.alloc(f"PTb{i}", 4096, BF16, [4]) for i in range(2)]
        y_p = [M.alloc(f"y{i}", 4096, BF16, [4, 5]) for i in range(4)]
        wor = [M.alloc("wor0", 2048, BF16, [4, 5]),
               M.alloc("wor1", 2048, BF16, [4, 5])]
        ao_tm_p = [M.alloc(f"ao_tm{i}", 8192, BF16, [5]) for i in range(2)]
        ao_fm = M.alloc("ao_fm", 1024, BF16, [5])
        ln2xT_p = [M.alloc(f"ln2xT{i}", 4096, BF16, [5, 6]) for i in range(4)]
        war = [M.alloc("war0", 2048, BF16, [5, 6]), M.alloc("war1", 2048, BF16, [5, 6])]
        hbuf = M.alloc("hbuf", 16384, BF16, list(range(6, 12)))
        ewr = [M.alloc(f"ewr{i}", 2048, BF16, list(range(6, 11))) for i in range(3)]
        pwr = [M.alloc(f"pwr{i}", 2048, BF16, list(range(7, 11))) for i in range(2)]
        par = [M.alloc(f"par{i}", 2048, BF16, list(range(7, 11))) for i in range(3)]
        ebuf = [M.alloc(f"ebuf{i}", 8192, BF16, list(range(7, 11)))
                for i in range(4)]
        hint_p = [M.alloc(f"hint{i}", 2048, F32, list(range(7, 11)))
                  for i in range(4)]
        peak = max(o + s for (_, o, s, _) in M.placed)
        assert peak <= 229340, f"SBUF peak {peak}"

        def ln1ownT(fc, c0, c1):
            t = ln1ownT_p[fc // 8]
            base = (fc % 8) * 1024
            return t[:, base + c0:base + c1]

        def Qsl(qh, c0, c1):
            t = Q_p[qh // 4]
            base = (qh % 4) * 1024
            return t[:, base + c0:base + c1]

        def Ksl(g, c0, c1):
            return K_p[g][:, c0:c1]

        def Vsl(sc):
            return V_p[sc // 4][:, (sc % 4) * 512:(sc % 4 + 1) * 512]

        def Vtile(sc, g):
            return V_p[sc // 4][:, (sc % 4) * 512 + g * 128:
                                (sc % 4) * 512 + (g + 1) * 128]

        def PTsl(which, sc_piece):
            return (PT_ap if which == 0 else PT_bp)[sc_piece]

        def ysl(qh_group):
            return y_p[qh_group]

        def hintsl(fc):
            return hint_p[fc // 4][:, (fc % 4) * 512:(fc % 4 + 1) * 512]

        def etile(ec):
            t = ebuf[ec // 16]
            c = (ec % 16) * 512
            return t[:, c:c + 512]

        s_x = [Sem(nc, ctx, f"s_x{k}") for k in range(2)]
        s_qp = Sem(nc, ctx, "s_qp")
        s_sp = [Sem(nc, ctx, f"s_sp{k}") for k in range(2)]
        s_o = [Sem(nc, ctx, f"s_o{k}") for k in range(2)]
        s_g = Sem(nc, ctx, "s_g")
        s_wm = [Sem(nc, ctx, f"s_wm{k}") for k in range(2)]
        s_kv = Sem(nc, ctx, "s_kv")
        s_t = Sem(nc, ctx, "s_t")
        s_v = Sem(nc, ctx, "s_v")
        s_sc = Sem(nc, ctx, "s_sc")
        s_wq = [Sem(nc, ctx, f"s_wq{k}") for k in range(2)]
        s_wo = [Sem(nc, ctx, f"s_wo{k}") for k in range(2)]
        s_wa = [Sem(nc, ctx, f"s_wa{k}") for k in range(2)]
        s_we = [Sem(nc, ctx, f"s_we{k}") for k in range(3)]
        s_wp = [Sem(nc, ctx, f"s_wp{k}") for k in range(2)]
        s_wh = [Sem(nc, ctx, f"s_wh{k}") for k in range(3)]

        free_tok = {"wq": [], "wo": [], "wa": [], "we": [], "wp": [], "wh": []}
        ffn_order = []

        blk = ctx.enter_context(nc.Block())

        @blk.gpsimd
        def _(g):
            g.iota(s_iota_i[:], pattern=[[1, 2048]], base=0, channel_multiplier=0)
            g.iota(ii0[:], pattern=[[0, 128]], base=0, channel_multiplier=1)
            last = g.iota(ii1[:], pattern=[[1, 128]], base=0, channel_multiplier=0)
            s_g.inc(last)
            for i in range(4):
                if i >= 2:
                    g.wait_ge(s_t.h, i - 1)
                s_wm[i % 2].inc(g.dma_start(out=pvst[i % 2][:], in_=dpvec[i]), 16)
            s_kv.inc(g.dma_start(
                out=view3(wk_sb, 512), in_=dwk.rearrange("(k p) e -> p k e", p=128)), 16)
            s_kv.inc(g.dma_start(
                out=view3(wv_sb, 512), in_=dwv.rearrange("(k p) e -> p k e", p=128)), 16)
        tok_g = s_g.n

        @blk.vector
        def _(v):
            v.wait_ge(s_g.h, tok_g)
            v.tensor_copy(s_iota[:], s_iota_i[:])
            v.tensor_tensor(ident[:], ii0[:], ii1[:], ALU.is_equal)
            v.tensor_tensor(ident32[:], ii0[:], ii1[:], ALU.is_equal)
            v.memset(ones32[:], 1.0)
            i0 = v.tensor_copy(statsb[:, 0:1], ii0[:, 0:1])
            tw = s_v.inc(i0)
            v.wait_ge(s_v.h, tw)
            iv = v.tensor_scalar_mul(statsb[:, 1:2], statsb[:, 0:1],
                                     float(-np.log(10000.0) / 64.0))
            s_v.inc(iv)
        tok_v0 = s_v.n

        @blk.scalar
        def _(sc):
            sc.wait_ge(s_v.h, tok_v0)
            ii = sc.activation(statsb[:, 2:3], statsb[:, 1:2], AF.Exp)
            s_sc.inc(ii)
        tok_if = s_sc.n

        pv_ev = []
        for i in range(4):
            @blk.tensor
            def _(t, i=i):
                t.wait_ge(s_wm[i % 2].h, 16 * (i // 2 + 1))
                if i == 0:
                    t.wait_ge(s_v.h, tok_v0)
                if i >= 1:
                    t.wait_ge(s_v.h, pv_ev[i - 1])
                tr = t.transpose(ps_p[:, 0:128], pvst[i % 2][:], ident32[:])
                s_t.inc(tr)
            tt = s_t.n

            @blk.vector
            def _(v, i=i, tt=tt):
                v.wait_ge(s_t.h, tt)
                iv = v.tensor_copy(params[:, i * 128:(i + 1) * 128], ps_p[:, 0:128])
                pv_ev.append(s_v.inc(iv))
        tok_params = pv_ev[-1]

        def table_pipeline(src_rows64_ap, width, dst_sin, dst_cos, scale):
            for (dst, shift) in ((dst_sin, 0.0), (dst_cos, PI / 2)):
                sA = tmpA[0:64, 0:width]
                sB = tmpB[0:64, 0:width]
                sC = tbs[0][0:64, 0:width]
                sD = tbs[1][0:64, 0:width]
                i32s = s_iota_i[0:64, 0:width]

                @blk.vector
                def _(v, dst=dst, shift=shift, sA=sA, sB=sB, sC=sC, sD=sD, i32s=i32s):
                    v.wait_ge(s_sc.h, tok_if)
                    v.tensor_scalar_mul(sA, src_rows64_ap, statsb[0:64, 2:3])
                    v.tensor_scalar(sB, sA, float(1.0), float(shift + 16 * PI),
                                    ALU.mult, ALU.add)
                    v.tensor_scalar_mul(sC, sB, float(1.0 / TWO_PI))
                    v.tensor_copy(i32s, sC)
                    v.tensor_copy(sC, i32s)
                    v.scalar_tensor_tensor(sD, sC, float(-TWO_PI), sB, ALU.mult, ALU.add)
                    v.tensor_single_scalar(sC, sD, float(PI), ALU.is_gt)
                    iv = v.scalar_tensor_tensor(sB, sC, float(-TWO_PI), sD,
                                                ALU.mult, ALU.add)
                    s_v.inc(iv)
                tv = s_v.n

                @blk.scalar
                def _(sc, sA=sA, sB=sB, tv=tv):
                    sc.wait_ge(s_v.h, tv)
                    ii = sc.activation(sA, sB, AF.Sin)
                    s_sc.inc(ii)
                ts_ = s_sc.n

                @blk.vector
                def _(v, dst=dst, sA=sA, ts_=ts_):
                    v.wait_ge(s_sc.h, ts_)
                    v.tensor_scalar_mul(dst[0:64, 0:width], sA, float(scale))
                    iv = v.tensor_copy(dst[64:128, 0:width], dst[0:64, 0:width])
                    s_v.inc(iv)

        table_pipeline(s_iota[0:64, :], 2048, ksin, kcos, 1.0)
        @blk.sync
        def _(sync):
            s_qp.inc(sync.dma_start(out=qposrow[:], in_=dqrow), 16)
            s_qp.inc(sync.dma_start(out=qpos[:], in_=dqpos), 16)
        tok_qr = s_qp.n

        @blk.vector
        def _(v):
            v.wait_ge(s_qp.h, tok_qr)
        table_pipeline(qposrow[0:64, :], 1024, qsin, qcos, QSCALE)

        @blk.vector
        def _(v):
            v.wait_ge(s_v.h, tok_params)
            ph_ap = params[:, 256:320]
            sA = tmpA[:, 0:64]
            sB = tmpB[:, 0:64]
            sC = tmpA[:, 64:128]
            sD = tmpB[:, 64:128]
            i32s = s_iota_i[:, 0:64]
            v.tensor_scalar(sB, ph_ap, float(1.0), float(PI / 4 + 16 * PI),
                            ALU.mult, ALU.add)
            v.tensor_scalar_mul(sC, sB, float(1.0 / TWO_PI))
            v.tensor_copy(i32s, sC)
            v.tensor_copy(sC, i32s)
            v.scalar_tensor_tensor(sD, sC, float(-TWO_PI), sB, ALU.mult, ALU.add)
            v.tensor_single_scalar(sC, sD, float(PI), ALU.is_gt)
            iv = v.scalar_tensor_tensor(sB, sC, float(-TWO_PI), sD, ALU.mult, ALU.add)
            s_v.inc(iv)
        tvm = s_v.n

        @blk.scalar
        def _(sc):
            sc.wait_ge(s_v.h, tvm)
            ii = sc.activation(tmpA[:, 0:64], tmpB[:, 0:64], AF.Sin)
            s_sc.inc(ii)
        tsm = s_sc.n

        @blk.vector
        def _(v):
            v.wait_ge(s_sc.h, tsm)
            im = v.tensor_scalar_mul(params[:, 512:576], tmpA[:, 0:64],
                                     float(np.sqrt(2.0)))
            tw = s_v.inc(im)
            v.wait_ge(s_v.h, tw)
            for l in range(2):
                pb0 = params[:, 192 + 32 * l:208 + 32 * l]
                pb1 = params[:, 208 + 32 * l:224 + 32 * l]
                m0 = params[:, 512 + 32 * l:528 + 32 * l]
                m1 = params[:, 528 + 32 * l:544 + 32 * l]
                v.tensor_tensor(tmpA[:, 0:16], pb0, m0, ALU.mult)
                i1 = v.tensor_tensor(tmpA[:, 16:32], pb1, m1, ALU.mult)
                tw = s_v.inc(i1)
                v.wait_ge(s_v.h, tw)
                iv = v.tensor_add(params[:, 576 + 16 * l:592 + 16 * l],
                                  tmpA[:, 0:16], tmpA[:, 16:32])
            s_v.inc(iv)
        tok_tables = s_v.n

        psT_free = [0]
        lnout_free = [0, 0]

        def heat(t, n):
            for _ in range(n):
                t.matmul(ps_p[:, 0:128], ident[:], ident[:], start=True, stop=True)

        lnpar = {"i": 0}

        def emit_ln_chunk(x_src_ap, dst_writer, pre_v=None, pre_x=None,
                          heat_ok=True):
            par = lnpar["i"] % 2
            lnpar["i"] += 1
            C = 8 + 32 * par
            lnout = lnout2_bufs[par]

            def sb_(k):
                return statsb[:, C + k:C + k + 1]
            """LN over free dim of staged f32 [128,2048]; bf16 result transposed
            into dstT tiles at [f, within_off:within_off+128], f=0..15."""
            @blk.vector
            def _(v):
                if pre_x is not None:
                    v.wait_ge(s_x[pre_x[0]].h, pre_x[1])
                if pre_v is not None:
                    v.wait_ge(s_v.h, pre_v)
                iv = v.reduce_sum(sb_(0), x_src_ap, axis=AX.X)
                s_v.inc(iv)
            t1 = s_v.n

            @blk.scalar
            def _(sc):
                if pre_x is not None:
                    sc.wait_ge(s_x[pre_x[0]].h, pre_x[1])
                if pre_v is not None:
                    sc.wait_ge(s_v.h, pre_v)
                ii = sc.activation(dumpb[:], x_src_ap, AF.Square,
                                   accum_out=sb_(1))
                s_sc.inc(ii)
            t2 = s_sc.n

            @blk.vector
            def _(v):
                v.wait_ge(s_sc.h, t2)
                i0 = v.tensor_scalar_mul(sb_(2), sb_(0),
                                         float(1.0 / D))
                tw = s_v.inc(i0)
                v.wait_ge(s_v.h, tw)
                i1 = v.tensor_tensor(sb_(3), sb_(2),
                                     sb_(2), ALU.mult)
                tw = s_v.inc(i1)
                v.wait_ge(s_v.h, tw)
                iv = v.scalar_tensor_tensor(sb_(4), sb_(1),
                                            float(1.0 / D), sb_(3),
                                            ALU.mult, ALU.subtract)
                s_v.inc(iv)
            t3 = s_v.n

            @blk.scalar
            def _(sc):
                sc.wait_ge(s_v.h, t3)
                ii = sc.activation(sb_(5), sb_(4), AF.Ln,
                                   bias=np.float32(EPS))
                s_sc.inc(ii)
            t4 = s_sc.n

            @blk.vector
            def _(v):
                v.wait_ge(s_sc.h, t4)
                iv = v.tensor_scalar_mul(sb_(6), sb_(5),
                                         float(-0.5))
                s_v.inc(iv)
            t5 = s_v.n

            @blk.scalar
            def _(sc):
                sc.wait_ge(s_v.h, t5)
                ii = sc.activation(sb_(7), sb_(6), AF.Exp)
                s_sc.inc(ii)
            t6 = s_sc.n

            @blk.vector
            def _(v):
                v.wait_ge(s_sc.h, t6)
                i0 = v.tensor_tensor(sb_(8), sb_(2),
                                     sb_(7), ALU.mult)
                tw = s_v.inc(i0)
                v.wait_ge(s_v.h, tw)
                iv = v.tensor_scalar_mul(sb_(9), sb_(8),
                                         float(-1.0))
                s_v.inc(iv)
            t7 = s_v.n

            @blk.scalar
            def _(sc):
                sc.wait_ge(s_v.h, t7)
                if lnout_free[par]:
                    sc.wait_ge(s_t.h, lnout_free[par])
                ii = sc.activation(lnout[:], x_src_ap, AF.Identity,
                                   bias=sb_(9), scale=sb_(7))
                s_sc.inc(ii)
            t8 = s_sc.n

            for r in range(2):
                @blk.tensor
                def _(t, r=r, t8=t8, t3=t3, t6=t6):
                    if r == 0 and heat_ok:
                        t.wait_ge(s_v.h, t3)
                        heat(t, 12)
                        t.wait_ge(s_sc.h, t6)
                        heat(t, 12)
                    t.wait_ge(s_sc.h, t8)
                    if psT_free[0]:
                        t.wait_ge(s_v.h, psT_free[0])
                    for f in range(8):
                        tr = t.transpose(
                            psT[:, f * 128:(f + 1) * 128],
                            lnout[:, (8 * r + f) * 128:(8 * r + f + 1) * 128],
                            ident[:])
                    s_t.inc(tr)
                tt = s_t.n
                lnout_free[par] = tt

                @blk.vector
                def _(v, r=r, tt=tt):
                    v.wait_ge(s_t.h, tt)
                    o = dst_writer(r)
                    if isinstance(o, tuple):
                        v.tensor_copy(o[0], view3(psT, 128)[:, 0:4, :])
                        iv = v.tensor_copy(o[1], view3(psT, 128)[:, 4:8, :])
                    else:
                        iv = v.tensor_copy(o, view3(psT, 128))
                    psT_free[0] = s_v.inc(iv)
            return t8

        xtok = []
        apply_done = []
        for tch in range(8):
            @blk.sync
            def _(sync, tch=tch):
                if tch >= 2:
                    sync.wait_ge(s_sc.h, apply_done[tch - 2])
                xtok.append((tch % 2,
                             s_x[tch % 2].inc(sync.dma_start(out=xst[tch % 2][:],
                                                             in_=dx_own[tch]), 16)))
            def _w1(r, tch=tch):
                t = ln1ownT_p[r]
                return view3(t, 1024)[:, 0:8, tch * 128:(tch + 1) * 128]
            emit_ln_chunk(xst[tch % 2][:], _w1, pre_x=xtok[tch])
            apply_done.append(s_sc.n)
        ln1_done_v = psT_free[0]

        psq = [ps_y[:, 0:1024], ps_s[:, 0:1024]]
        rope_done = []
        q_mm_tok = []
        for h in range(16):
            @blk.tensor
            def _(t, h=h):
                t.wait_ge(s_wq[h % 2].h, 16 * (h // 2 + 1))
                if h == 0:
                    t.wait_ge(s_v.h, ln1_done_v)
                if h >= 2:
                    t.wait_ge(s_v.h, rope_done[h - 2])
                pq = psq[h % 2]
                for fc in range(16):
                    for hf in range(2):
                        mm = t.matmul(pq[:, hf * 512:(hf + 1) * 512],
                                      wqr[h % 2][:, fc * 128:(fc + 1) * 128],
                                      ln1ownT(fc, hf * 512, (hf + 1) * 512),
                                      start=(fc == 0), stop=(fc == 15))
                s_t.inc(mm)
            q_mm_tok.append(s_t.n)

            @blk.vector
            def _(v, h=h):
                v.wait_ge(s_t.h, q_mm_tok[h])
                pq = psq[h % 2]
                lo, hi = slice(0, 64), slice(64, 128)
                a = tmpA[0:64, 0:1024]
                b = tmpB[0:64, 0:1024]
                qc = Qsl(h, 0, 1024)
                v.tensor_tensor(a, pq[lo, :], qcos[lo, :], ALU.mult)
                v.tensor_tensor(b, pq[hi, :], qsin[hi, :], ALU.mult)
                v.tensor_sub(qc[lo, :], a, b)
                v.tensor_tensor(tmpA[64:128, 0:1024], pq[lo, :], qsin[lo, :],
                                ALU.mult)
                v.tensor_tensor(tmpB[64:128, 0:1024], pq[hi, :], qcos[hi, :],
                                ALU.mult)
                iv = v.tensor_add(qc[hi, :], tmpA[64:128, 0:1024],
                                  tmpB[64:128, 0:1024])
                rope_done.append(s_v.inc(iv))
            free_tok["wq"].append(("v", rope_done[h]))
        q_done = rope_done[-1]

        psk = [ps_p[:, 0:512], ps_s[:, 0:512]]
        psv = [ps_y[:, 0:512], ps_s[:, 512:1024]]
        kpipe = {"k": 0, "v": 0}
        kv_free = {}
        for pc in range(4):
            for tsb in range(4):
                gi = 8 + pc * 4 + tsb

                @blk.sync
                def _(sync, gi=gi, pc=pc, tsb=tsb):
                    sync.wait_ge(s_sc.h, apply_done[gi - 2])
                    xtok.append((gi % 2,
                                 s_x[gi % 2].inc(sync.dma_start(
                                     out=xst[gi % 2][:],
                                     in_=dx_seq[pc * 4 + tsb]), 16)))
                def _w3(r, tsb=tsb):
                    return view3(lnT, 512)[:, 8 * r:8 * r + 8,
                                           tsb * 128:(tsb + 1) * 128]
                emit_ln_chunk(xst[gi % 2][:], _w3, pre_x=xtok[gi], heat_ok=False)
                apply_done.append(s_sc.n)
            lnT_tok = psT_free[0]

            for g4 in range(4):
                i = kpipe["k"]

                @blk.tensor
                def _(t, pc=pc, g4=g4, i=i, lnT_tok=lnT_tok):
                    t.wait_ge(s_v.h, lnT_tok)
                    if pc == 0 and g4 == 0:
                        t.wait_ge(s_kv.h, 32)
                        t.wait_ge(s_v.h, q_done)
                    if i >= 2:
                        t.wait_ge(s_v.h, kv_free[("k", i - 2)])
                    pk = psk[i % 2]
                    for fc in range(16):
                        mm = t.matmul(
                            pk,
                            wk_sb[:, fc * 512 + g4 * 128:fc * 512 + (g4 + 1) * 128],
                            lnT[:, fc * 512:(fc + 1) * 512],
                            start=(fc == 0), stop=(fc == 15))
                    s_t.inc(mm)
                tk = s_t.n

                @blk.vector
                def _(v, pc=pc, g4=g4, i=i, tk=tk):
                    v.wait_ge(s_t.h, tk)
                    pk = psk[i % 2]
                    lo, hi = slice(0, 64), slice(64, 128)
                    cw = slice(pc * 512, (pc + 1) * 512)
                    a = tmpA[0:64, 0:512]
                    b = tmpB[0:64, 0:512]
                    kc = Ksl(g4, pc * 512, (pc + 1) * 512)
                    v.tensor_tensor(a, pk[lo, :], kcos[lo, cw], ALU.mult)
                    v.tensor_tensor(b, pk[hi, :], ksin[hi, cw], ALU.mult)
                    v.tensor_sub(kc[lo, :], a, b)
                    v.tensor_tensor(tmpA[64:128, 0:512], pk[lo, :], ksin[lo, cw],
                                    ALU.mult)
                    v.tensor_tensor(tmpB[64:128, 0:512], pk[hi, :], kcos[hi, cw],
                                    ALU.mult)
                    iv = v.tensor_add(kc[hi, :], tmpA[64:128, 0:512],
                                      tmpB[64:128, 0:512])
                    kv_free[("k", i)] = s_v.inc(iv)
                kpipe["k"] += 1

            for tsb in range(4):
                i = kpipe["v"]

                @blk.tensor
                def _(t, pc=pc, tsb=tsb, i=i, lnT_tok=lnT_tok):
                    t.wait_ge(s_v.h, lnT_tok)
                    if i >= 2:
                        t.wait_ge(s_v.h, kv_free[("v", i - 2)])
                    pv = psv[i % 2]
                    for fc in range(16):
                        mm = t.matmul(
                            pv,
                            lnT[:, fc * 512 + tsb * 128:fc * 512 + (tsb + 1) * 128],
                            wv_sb[:, fc * 512:(fc + 1) * 512],
                            start=(fc == 0), stop=(fc == 15))
                    s_t.inc(mm)
                tv2 = s_t.n

                @blk.vector
                def _(v, pc=pc, tsb=tsb, i=i, tv2=tv2):
                    v.wait_ge(s_t.h, tv2)
                    sc_ = pc * 4 + tsb
                    iv = v.tensor_copy(Vsl(sc_), psv[i % 2])
                    kv_free[("v", i)] = s_v.inc(iv)
                kpipe["v"] += 1
        kv_done = s_v.n

        EXPC = 24.0
        S0 = s_sc.n
        git = [0]
        pread_grp = [0]
        ptbuf_free = {0: 0, 1: 0}
        psy_free = {0: 0, 1: 0}
        for sp in range(4):
            ja, jb = 2 * sp, 2 * sp + 1
            Ea, Eb = E_SLOT[ja], E_SLOT[jb]

            @blk.vector
            def _(v, sp=sp, ja=ja, jb=jb, Ea=Ea, Eb=Eb):
                if sp == 0:
                    v.wait_ge(s_v.h, kv_done)
                v.tensor_single_scalar(masks[:, 0:Ea * 128], s_iota[:, 0:Ea * 128],
                                       qpos[:, ja:ja + 1], ALU.is_le)
                iv = v.tensor_single_scalar(masks[:, 2048:2048 + Eb * 128],
                                            s_iota[:, 0:Eb * 128],
                                            qpos[:, jb:jb + 1], ALU.is_le)
                s_v.inc(iv)
            mask_tok = s_v.n

            for g4 in range(4):
                for (sub, j, E, moff) in ((0, ja, Ea, 0),
                                          (1, jb, Eb, 2048)):
                    yoff = sub * 512
                    cols = E * 128
                    twobuf = cols <= 1024
                    base_it = git[0]
                    tqk = []
                    for qi in range(4):
                        qh = 4 * g4 + qi
                        it = base_it + qi
                        if twobuf:
                            base = (it % 2) * 1024
                            freedep = it - 2
                        else:
                            base = 0
                            freedep = it - 1

                        @blk.tensor
                        def _(t, j=j, qh=qh, cols=cols, g4=g4, base=base,
                              freedep=freedep, mask_tok=mask_tok):
                            t.wait_ge(s_v.h, mask_tok)
                            if freedep >= 0:
                                t.wait_ge(s_sc.h, S0 + freedep + 1)
                            c = 0
                            while c < cols:
                                w = min(512, cols - c)
                                mm = t.matmul(
                                    ps_s[:, base + c:base + c + w],
                                    Qsl(qh, j * 128, (j + 1) * 128),
                                    Ksl(g4, c, c + w),
                                    start=True, stop=True)
                                c += w
                            s_t.inc(mm)
                        tqk.append(s_t.n)

                    @blk.scalar
                    def _(sc, cols=cols, base_it=base_it, tqk=tuple(tqk),
                          prg=pread_grp[0]):
                        if prg:
                            sc.wait_ge(s_t.h, prg)
                        for qi in range(4):
                            it = base_it + qi
                            base = (it % 2) * 1024 if cols <= 1024 else 0
                            sc.wait_ge(s_t.h, tqk[qi])
                            ii = sc.activation(PbE[qi][:, 0:cols],
                                               ps_s[:, base:base + cols],
                                               AF.Exp, bias=np.float32(-EXPC))
                            s_sc.inc(ii)
                    assert s_sc.n == S0 + base_it + 4, (s_sc.n, S0, base_it)

                    tnorm = []

                    @blk.vector
                    def _(v, cols=cols, moff=moff, base_it=base_it):
                        for qi in range(4):
                            SC = 20 + 3 * (qi % 2)
                            v.wait_ge(s_sc.h, S0 + base_it + qi + 1)
                            i0 = v.scalar_tensor_tensor(
                                PbM[:, 0:cols], PbE[qi][:, 0:cols], float(1.0),
                                masks[:, moff:moff + cols],
                                ALU.mult, ALU.mult,
                                accum_out=statsb[:, SC + 1:SC + 2])
                            t0 = s_v.inc(i0)
                            v.wait_ge(s_v.h, t0)
                            i1 = v.reciprocal(statsb[:, SC + 2:SC + 3],
                                              statsb[:, SC + 1:SC + 2])
                            t1 = s_v.inc(i1)
                            v.wait_ge(s_v.h, t1)
                            iv = v.tensor_scalar_mul(PbE[qi][:, 0:cols],
                                                     PbM[:, 0:cols],
                                                     statsb[:, SC + 2:SC + 3])
                            tnorm.append(s_v.inc(iv))

                    for qi in range(4):
                        nrounds = (E + 7) // 8
                        for r in range(nrounds):
                            n_in_r = min(8, E - 8 * r)

                            @blk.tensor
                            def _(t, qi=qi, r=r, n_in_r=n_in_r, sub=sub,
                                  tn=tnorm[qi] if False else None):
                                t.wait_ge(s_v.h, tnorm[qi])
                                if psT_free[0]:
                                    t.wait_ge(s_v.h, psT_free[0])
                                if ptbuf_free[sub]:
                                    t.wait_ge(s_t.h, ptbuf_free[sub])
                                for k in range(n_in_r):
                                    sc_ = 8 * r + k
                                    tr = t.transpose(
                                        psT[:, k * 128:(k + 1) * 128],
                                        PbE[qi][:, sc_ * 128:(sc_ + 1) * 128],
                                        ident[:])
                                s_t.inc(tr)
                            ttr = s_t.n

                            @blk.vector
                            def _(v, r=r, n_in_r=n_in_r, qi=qi, sub=sub, ttr=ttr):
                                v.wait_ge(s_t.h, ttr)
                                o = view3(PTsl(sub, r), 512)[:, 0:n_in_r,
                                                             qi * 128:(qi + 1) * 128]
                                iv = v.tensor_copy(
                                    o, view3(psT, 128)[:, 0:n_in_r, :])
                                psT_free[0] = s_v.inc(iv)
                    pread_grp[0] = s_t.n
                    git[0] += 4
                    pt_tok = psT_free[0]

                    @blk.tensor
                    def _(t, g4=g4, E=E, yoff=yoff, sub=sub, pt_tok=pt_tok):
                        t.wait_ge(s_v.h, pt_tok)
                        if psy_free[sub]:
                            t.wait_ge(s_v.h, psy_free[sub])
                        for sc_ in range(E):
                            pt = PTsl(sub, sc_ // 8)
                            mm = t.matmul(
                                ps_y[:, yoff:yoff + 512],
                                Vtile(sc_, g4),
                                pt[:, (sc_ % 8) * 512:(sc_ % 8 + 1) * 512],
                                start=(sc_ == 0), stop=(sc_ == E - 1))
                        ptbuf_free[sub] = s_t.inc(mm)
                    tav = s_t.n

                    @blk.vector
                    def _(v, g4=g4, j=j, yoff=yoff, sub=sub, tav=tav):
                        v.wait_ge(s_t.h, tav)
                        o = view3(ysl(g4), 1024)[:, 0:4, j * 128:(j + 1) * 128]
                        i_ = view3(ps_y, 128)[:, 4 * sub:4 * sub + 4, :]
                        iv = v.tensor_copy(o, i_)
                        psy_free[sub] = s_v.inc(iv)
        y_done = s_v.n

        pso = [ps_y[:, 0:1024], ps_s[:, 0:1024]]
        wo_psum_free = {}
        aofm_free = [0]
        wo_mm_tok = []
        for fo in range(16):
            @blk.tensor
            def _(t, fo=fo):
                t.wait_ge(s_wo[fo % 2].h, 16 * (fo // 2 + 1))
                if fo == 0:
                    t.wait_ge(s_v.h, y_done)
                if fo >= 2:
                    t.wait_ge(s_v.h, wo_psum_free[fo % 2])
                po = pso[fo % 2]
                for dc in range(16):
                    yb = y_p[dc // 4]
                    for hf in range(2):
                        mm = t.matmul(po[:, hf * 512:(hf + 1) * 512],
                                      wor[fo % 2][:, dc * 128:(dc + 1) * 128],
                                      yb[:, (dc % 4) * 1024 + hf * 512:
                                         (dc % 4) * 1024 + (hf + 1) * 512],
                                      start=(dc == 0), stop=(dc == 15))
                s_t.inc(mm)
            wo_mm_tok.append(s_t.n)
            free_tok["wo"].append(("t", wo_mm_tok[fo]))

            @blk.vector
            def _(v, fo=fo):
                v.wait_ge(s_t.h, wo_mm_tok[fo])
                if aofm_free[0]:
                    v.wait_ge(s_t.h, aofm_free[0])
                iv = v.tensor_copy(ao_fm[:], pso[fo % 2][:, 0:1024])
                wo_psum_free[fo % 2] = s_v.inc(iv)
            tev = s_v.n

            @blk.tensor
            def _(t, fo=fo, tev=tev):
                t.wait_ge(s_v.h, tev)
                if psT_free[0]:
                    t.wait_ge(s_v.h, psT_free[0])
                for k in range(8):
                    tr = t.transpose(psT[:, k * 128:(k + 1) * 128],
                                     ao_fm[:, k * 128:(k + 1) * 128], ident[:])
                aofm_free[0] = s_t.inc(tr)
            ttr2 = s_t.n

            @blk.vector
            def _(v, fo=fo, ttr2=ttr2):
                v.wait_ge(s_t.h, ttr2)
                o0 = view3(ao_tm_p[0], 2048)[:, 0:4, fo * 128:(fo + 1) * 128]
                o1 = view3(ao_tm_p[1], 2048)[:, 0:4, fo * 128:(fo + 1) * 128]
                v.tensor_copy(o0, view3(psT, 128)[:, 0:4, :])
                iv = v.tensor_copy(o1, view3(psT, 128)[:, 4:8, :])
                psT_free[0] = s_v.inc(iv)
        ao_done = psT_free[0]

        spill_tok = []
        add_v_tok = []
        for tch in range(8):
            @blk.sync
            def _(sync, tch=tch):
                if tch < 2:
                    sync.wait_ge(s_v.h, ao_done)
                else:
                    sync.wait_ge(s_sc.h, apply_done[24 + tch - 2])
                xtok.append((tch % 2,
                             s_x[tch % 2].inc(sync.dma_start(out=xst[tch % 2][:],
                                                             in_=dx_own[tch]), 16)))

            @blk.vector
            def _(v, tch=tch):
                v.wait_ge(s_x[xtok[24 + tch][0]].h, xtok[24 + tch][1])
                if tch == 0:
                    v.wait_ge(s_v.h, ao_done)
                if tch >= 2:
                    v.wait_ge(s_sp[spill_tok[tch - 2][0]].h, spill_tok[tch - 2][1])
                dst = tmpA if tch % 2 == 0 else tmpB
                ao_pc = ao_tm_p[tch // 4][:, (tch % 4) * 2048:(tch % 4 + 1) * 2048]
                iv = v.tensor_add(dst[:], xst[tch % 2][:], ao_pc)
                add_v_tok.append(s_v.inc(iv))

            @blk.scalar
            def _(sc, tch=tch):
                sc.wait_ge(s_v.h, add_v_tok[tch])
                src = tmpA if tch % 2 == 0 else tmpB
                spill_tok.append((tch % 2,
                                  s_sp[tch % 2].inc(sc.dma_start(out=dx2[tch],
                                                                 in_=src[:]), 16)))

            src = tmpA if tch % 2 == 0 else tmpB
            def _w5(r, tch=tch):
                return (view3(ln2xT_p[2 * r], 1024)[:, 0:4,
                                                    tch * 128:(tch + 1) * 128],
                        view3(ln2xT_p[2 * r + 1], 1024)[:, 0:4,
                                                        tch * 128:(tch + 1) * 128])
            emit_ln_chunk(src[:], _w5, pre_v=add_v_tok[tch])
            apply_done.append(s_sc.n)
        ln2_done = psT_free[0]

        psa = [ps_y[:, 0:1024], ps_s[:, 0:1024]]
        sig_tok = {}
        tanh_tok = {}
        hstt_tok = {}
        for fo in range(16):
            for p in range(2):
                i = 2 * fo + p

                @blk.tensor
                def _(t, fo=fo, p=p, i=i):
                    t.wait_ge(s_wa[i % 2].h, 16 * (i // 2 + 1))
                    if i == 0:
                        t.wait_ge(s_v.h, ln2_done)
                    if fo >= 1:
                        t.wait_ge(s_sc.h, (sig_tok if p == 0 else tanh_tok)[fo - 1])
                    pa_ = psa[p]
                    for fc in range(16):
                        for hf in range(2):
                            mm = t.matmul(pa_[:, hf * 512:(hf + 1) * 512],
                                          war[i % 2][:, fc * 128:(fc + 1) * 128],
                                          ln2xT_p[fc // 4][:, (fc % 4) * 1024 + hf * 512:
                                                  (fc % 4) * 1024 + (hf + 1) * 512],
                                          start=(fc == 0), stop=(fc == 15))
                    s_t.inc(mm)
                free_tok["wa"].append(("t", s_t.n))
                tmm = s_t.n

                @blk.scalar
                def _(sc, fo=fo, p=p, tmm=tmm):
                    sc.wait_ge(s_t.h, tmm)
                    if p == 0:
                        if fo == 0:
                            sc.wait_ge(s_sp[spill_tok[6][0]].h, spill_tok[6][1])
                            sc.wait_ge(s_sp[spill_tok[7][0]].h, spill_tok[7][1])
                        if fo >= 1:
                            sc.wait_ge(s_v.h, hstt_tok[fo - 1])
                        ii = sc.activation(tmpA[:, 0:1024], psa[0], AF.Sigmoid,
                                           bias=params[:, fo:fo + 1])
                        sig_tok[fo] = s_sc.inc(ii)
                    else:
                        ii = sc.activation(tmpB[:, 0:1024], psa[1], AF.Tanh,
                                           bias=params[:, 16 + fo:17 + fo])
                        tanh_tok[fo] = s_sc.inc(ii)
            @blk.vector
            def _(v, fo=fo):
                v.wait_ge(s_sc.h, tanh_tok[fo])
                v.tensor_scalar(tmpB[:, 1024:2048], tmpB[:, 0:1024], float(PI),
                                float(PI / 4), ALU.mult, ALU.add)
                iv = v.tensor_single_scalar(tmpA[:, 1024:2048], tmpB[:, 1024:2048],
                                            float(PI), ALU.is_gt)
                iv = v.scalar_tensor_tensor(tmpB[:, 0:1024], tmpA[:, 1024:2048],
                                            float(-TWO_PI), tmpB[:, 1024:2048],
                                            ALU.mult, ALU.add)
                s_v.inc(iv)
            tmx = s_v.n

            @blk.scalar
            def _(sc, fo=fo, tmx=tmx):
                sc.wait_ge(s_v.h, tmx)
                ii = sc.activation(tmpB[:, 1024:2048], tmpB[:, 0:1024], AF.Sin)
                s_sc.inc(ii)
            tsn = s_sc.n

            @blk.vector
            def _(v, fo=fo, tsn=tsn):
                v.wait_ge(s_sc.h, tsn)
                iv = v.scalar_tensor_tensor(hbuf[:, fo * 1024:(fo + 1) * 1024],
                                            tmpB[:, 1024:2048], float(np.sqrt(2.0)),
                                            tmpA[:, 0:1024], ALU.mult, ALU.mult)
                hstt_tok[fo] = s_v.inc(iv)
        h_done = s_v.n

        pse = [ps_p[:, 0:512], ps_s[:, 1536:2048]]
        psent = [ps_y[:, 0:512], ps_y[:, 512:1024]]
        ne_used = [0]
        npw_used = [0]
        nh_used = [0]
        e_evac_tok = {}
        prev_ent_last = [0]
        prev_block_apply = [0]

        def emit_fm_ln(gcol, bcol, src_tok, tb, stats_wait_v, weave=None):
            sq_tok = {}
            stat_mm_tok = {}

            @blk.tensor
            def _(t):
                t.wait_ge(s_v.h, src_tok)
                if stats_wait_v[0]:
                    t.wait_ge(s_v.h, stats_wait_v[0])
                for fc in range(16):
                    mm = t.matmul(ps_s[:, 0:512], ones32[:], hintsl(fc),
                                  start=(fc == 0), stop=(fc == 15))
                s_t.inc(mm)
            for g4_ in range(4):
                @blk.scalar
                def _(sc, g4_=g4_):
                    if g4_ == 0:
                        sc.wait_ge(s_v.h, src_tok)
                    if g4_ >= 1:
                        sc.wait_ge(s_t.h, stat_mm_tok[4 * g4_ - 1])
                    for fc in range(4 * g4_, 4 * g4_ + 4):
                        ii = sc.activation(tmpB[:, (fc % 4) * 512:(fc % 4 + 1) * 512],
                                           hintsl(fc), AF.Square)
                        sq_tok[fc] = s_sc.inc(ii)

                @blk.tensor
                def _(t, g4_=g4_):
                    for fc in range(4 * g4_, 4 * g4_ + 4):
                        t.wait_ge(s_sc.h, sq_tok[fc])
                        mm = t.matmul(ps_s[:, 512:1024], ones32[:],
                                      tmpB[:, (fc % 4) * 512:(fc % 4 + 1) * 512],
                                      start=(fc == 0), stop=(fc == 15))
                        stat_mm_tok[fc] = s_t.inc(mm)
                if weave:
                    weave(1)
            tstat = s_t.n

            @blk.vector
            def _(v):
                v.wait_ge(s_t.h, tstat)
                v.tensor_scalar_mul(tmpA[:, 0:512], ps_s[:, 0:512], float(1.0 / D))
                v.tensor_tensor(tmpA[:, 1024:1536], tmpA[:, 0:512],
                                tmpA[:, 0:512], ALU.mult)
                iv = v.scalar_tensor_tensor(tmpB[:, 0:512], ps_s[:, 512:1024],
                                            float(1.0 / D), tmpA[:, 1024:1536],
                                            ALU.mult, ALU.subtract)
                s_v.inc(iv)
            tvar = s_v.n
            stats_wait_v[0] = tvar

            @blk.scalar
            def _(sc):
                sc.wait_ge(s_v.h, tvar)
                ii = sc.activation(tmpB[:, 512:1024], tmpB[:, 0:512], AF.Ln,
                                   bias=np.float32(EPS))
                s_sc.inc(ii)
            tln = s_sc.n

            @blk.vector
            def _(v):
                v.wait_ge(s_sc.h, tln)
                iv = v.tensor_scalar_mul(tmpB[:, 0:512], tmpB[:, 512:1024],
                                         float(-0.5))
                s_v.inc(iv)
            trr = s_v.n

            @blk.scalar
            def _(sc):
                sc.wait_ge(s_v.h, trr)
                ii = sc.activation(tmpA[:, 512:1024], tmpB[:, 0:512], AF.Exp)
                s_sc.inc(ii)
            trstd = s_sc.n

            app_tok = {}
            for fc in range(16):
                @blk.vector
                def _(v, fc=fc):
                    if fc == 0:
                        v.wait_ge(s_sc.h, trstd)
                    pb = (fc % 2) * 512
                    pa = 1024 + (fc % 2) * 512
                    v.tensor_sub(tmpB[:, pb:pb + 512], hintsl(fc), tmpA[:, 0:512])
                    v.tensor_tensor(tmpB[:, pa:pa + 512], tmpB[:, pb:pb + 512],
                                    tmpA[:, 512:1024], ALU.mult)
                    iv = v.tensor_scalar(
                        hbuf[:, fc * 1024 + tb * 512:fc * 1024 + (tb + 1) * 512],
                        tmpB[:, pa:pa + 512],
                        params[:, gcol + fc:gcol + fc + 1],
                        params[:, bcol + fc:bcol + fc + 1],
                        ALU.mult, ALU.add)
                    app_tok[fc] = s_v.inc(iv)
                if weave and fc % 2 == 1:
                    weave(1)
            return app_tok[15]

        stats_wait_v = [0]
        BLOCKS = [(0, 0), (0, 1), (1, 0), (1, 1)]
        h_ready = {0: h_done, 1: h_done}

        def make_eprod(bi):
            l, tb = BLOCKS[bi]
            closures = []
            for ec in range(64):
                def emit_ec(ec=ec, l=l, tb=tb, bi=bi):
                    i = ne_used[0]
                    ffn_order.append(("we", i, l, ec))

                    @blk.tensor
                    def _(t, l=l, tb=tb, ec=ec, i=i, bi=bi):
                        t.wait_ge(s_we[i % 3].h, 16 * (i // 3 + 1))
                        if ec == 0:
                            t.wait_ge(s_v.h, h_ready[bi])
                        if i >= 2:
                            t.wait_ge(s_v.h, e_evac_tok[i - 2])
                        pe = pse[ec % 2]
                        for fc in range(16):
                            mm = t.matmul(pe,
                                          ewr[i % 3][:, fc * 128:(fc + 1) * 128],
                                          hbuf[:, fc * 1024 + tb * 512:
                                               fc * 1024 + (tb + 1) * 512],
                                          start=(fc == 0), stop=(fc == 15))
                        s_t.inc(mm)
                    te = s_t.n
                    free_tok["we"].append(("t", te))

                    @blk.vector
                    def _(v, l=l, ec=ec, i=i, te=te):
                        v.wait_ge(s_t.h, te)
                        if ec == 0 and prev_ent_last[0]:
                            v.wait_ge(s_t.h, prev_ent_last[0])
                        ii = v.tensor_scalar_add(etile(ec), pse[ec % 2],
                                                 params[:, 32 + 64 * l + ec:
                                                        33 + 64 * l + ec])
                        e_evac_tok[i] = s_v.inc(ii)
                    ne_used[0] += 1
                closures.append(emit_ec)
            return closures

        pending_ep = [None]

        def weave(n):
            c = pending_ep[0]
            if not c:
                return
            k = 0
            while c and k < n:
                c.pop(0)()
                k += 1

        for bi, (l, tb) in enumerate(BLOCKS):
            if bi == 0:
                for fn in make_eprod(0):
                    fn()
            else:
                weave(64)
            e_done = s_v.n
            if bi + 1 < 4:
                pending_ep[0] = make_eprod(bi + 1)

            ent_evac = {}
            for fc in range(16):
                for hf in range(4):
                    i = npw_used[0]
                    ffn_order.append(("wp", i, l, fc, hf))

                    @blk.tensor
                    def _(t, fc=fc, hf=hf, i=i, e_done=e_done):
                        t.wait_ge(s_wp[i % 2].h, 16 * (i // 2 + 1))
                        if fc == 0 and hf == 0:
                            t.wait_ge(s_v.h, e_done)
                        if fc >= 2 and hf == 0:
                            t.wait_ge(s_v.h, ent_evac[fc - 2])
                        pp = psent[fc % 2]
                        for k in range(16):
                            ec = hf * 16 + k
                            mm = t.matmul(pp,
                                          pwr[i % 2][:, k * 128:(k + 1) * 128],
                                          etile(ec),
                                          start=(ec == 0), stop=(ec == 63))
                        s_t.inc(mm)
                    free_tok["wp"].append(("t", s_t.n))
                    npw_used[0] += 1
                tent = s_t.n

                @blk.vector
                def _(v, l=l, tb=tb, fc=fc, tent=tent):
                    v.wait_ge(s_t.h, tent)
                    iv = v.scalar_tensor_tensor(
                        hintsl(fc), psent[fc % 2],
                        params[:, 160 + 16 * l + fc:161 + 16 * l + fc],
                        hbuf[:, fc * 1024 + tb * 512:fc * 1024 + (tb + 1) * 512],
                        ALU.add, ALU.add)
                    ent_evac[fc] = s_v.inc(iv)
            prev_ent_last[0] = s_t.n
            hent_done = s_v.n

            h3_done = emit_fm_ln(320 + 32 * l, 336 + 32 * l, hent_done, tb,
                                 stats_wait_v, weave)

            psq0 = [ps_s[:, 0:512], ps_y[:, 0:512]]
            psq1 = [ps_s[:, 512:1024], ps_y[:, 512:1024]]
            path_evac = {}
            for fo in range(16):
                for p in range(2):
                    i = nh_used[0]
                    ffn_order.append(("wh", i, l, p, fo))

                    @blk.tensor
                    def _(t, fo=fo, p=p, i=i, h3_done=h3_done):
                        t.wait_ge(s_wh[i % 3].h, 16 * (i // 3 + 1))
                        if fo == 0 and p == 0:
                            t.wait_ge(s_v.h, h3_done)
                            t.wait_ge(s_v.h, stats_wait_v[0])
                        if fo >= 2 and p == 0:
                            t.wait_ge(s_v.h, path_evac[fo - 2])
                        pp = (psq0 if p == 0 else psq1)[fo % 2]
                        for dc in range(16):
                            mm = t.matmul(pp,
                                          par[i % 3][:, dc * 128:(dc + 1) * 128],
                                          hbuf[:, dc * 1024 + tb * 512:
                                               dc * 1024 + (tb + 1) * 512],
                                          start=(dc == 0), stop=(dc == 15))
                        s_t.inc(mm)
                    free_tok["wh"].append(("t", s_t.n))
                    nh_used[0] += 1
                tpa = s_t.n
                if fo % 2 == 1:
                    weave(2)

                @blk.vector
                def _(v, l=l, tb=tb, fo=fo, tpa=tpa):
                    v.wait_ge(s_t.h, tpa)
                    m0 = params[:, 512 + 32 * l + fo:513 + 32 * l + fo]
                    m1 = params[:, 528 + 32 * l + fo:529 + 32 * l + fo]
                    bc = params[:, 576 + 16 * l + fo:577 + 16 * l + fo]
                    v.scalar_tensor_tensor(tmpA[:, 1024:1536], psq0[fo % 2], m0,
                                           hbuf[:, fo * 1024 + tb * 512:
                                                fo * 1024 + (tb + 1) * 512],
                                           ALU.mult, ALU.add)
                    v.scalar_tensor_tensor(tmpA[:, 1536:2048], psq1[fo % 2], m1,
                                           tmpA[:, 1024:1536], ALU.mult, ALU.add)
                    iv = v.tensor_scalar_add(hintsl(fo),
                                             tmpA[:, 1536:2048], bc)
                    path_evac[fo] = s_v.inc(iv)
            interf_done = s_v.n

            h_ready[bi + 2] = emit_fm_ln(384 + 32 * l, 400 + 32 * l,
                                         interf_done, tb, stats_wait_v, weave)
            prev_block_apply[0] = h_ready[bi + 2]
        ffn_done = prev_block_apply[0]

        reload_tok = []
        add_tok = []
        outdma_tok = []
        for tch in range(8):
            @blk.scalar
            def _(sc, tch=tch):
                if tch < 2:
                    sc.wait_ge(s_v.h, ffn_done)
                else:
                    sc.wait_ge(s_v.h, add_tok[tch - 2])
                reload_tok.append((tch % 2,
                                   s_x[tch % 2].inc(sc.dma_start(out=xst[tch % 2][:],
                                                                 in_=dx2[tch]), 16)))

            @blk.tensor
            def _(t, tch=tch):
                if tch == 0:
                    t.wait_ge(s_v.h, ffn_done)
                if psT_free[0]:
                    t.wait_ge(s_v.h, psT_free[0])
                for k in range(8):
                    tr = t.transpose(psT[:, k * 128:(k + 1) * 128],
                                     hbuf[:, k * 1024 + tch * 128:
                                          k * 1024 + (tch + 1) * 128], ident[:])
                s_t.inc(tr)
            th1 = s_t.n

            @blk.vector
            def _(v, tch=tch, th1=th1):
                v.wait_ge(s_t.h, th1)
                o = view3(lnout, 128)[:, 0:8, :]
                iv = v.tensor_copy(o, view3(psT, 128))
                psT_free[0] = s_v.inc(iv)
            tev1 = psT_free[0]

            @blk.tensor
            def _(t, tch=tch, tev1=tev1):
                t.wait_ge(s_v.h, tev1)
                for k in range(8):
                    tr = t.transpose(psT[:, k * 128:(k + 1) * 128],
                                     hbuf[:, (k + 8) * 1024 + tch * 128:
                                          (k + 8) * 1024 + (tch + 1) * 128],
                                     ident[:])
                s_t.inc(tr)
            th2 = s_t.n

            @blk.vector
            def _(v, tch=tch, th2=th2):
                v.wait_ge(s_t.h, th2)
                o = view3(lnout, 128)[:, 8:16, :]
                iv = v.tensor_copy(o, view3(psT, 128))
                psT_free[0] = s_v.inc(iv)
                v.wait_ge(s_x[reload_tok[tch][0]].h, reload_tok[tch][1])
                if tch >= 2:
                    v.wait_ge(s_o[outdma_tok[tch - 2][0]].h,
                              outdma_tok[tch - 2][1])
                dst = tmpA if tch % 2 == 0 else tmpB
                iv = v.tensor_add(dst[:], xst[tch % 2][:], lnout[:])
                add_tok.append(s_v.inc(iv))

            @blk.sync
            def _(sync, tch=tch):
                sync.wait_ge(s_v.h, add_tok[tch])
                src = tmpA if tch % 2 == 0 else tmpB
                outdma_tok.append((tch % 2,
                                   s_o[tch % 2].inc(sync.dma_start(out=dout[tch],
                                                                   in_=src[:]), 16)))

        @blk.sync
        def _(sync):
            sync.wait_ge(s_o[0].h, s_o[0].n)
            sync.wait_ge(s_o[1].h, s_o[1].n)

        sem_of = {"v": s_v, "t": s_t, "sc": s_sc}

        @blk.gpsimd
        def _(g):
            for h in range(16):
                if h >= 2:
                    kind, tok = free_tok["wq"][h - 2]
                    g.wait_ge(sem_of[kind].h, tok)
                s_wq[h % 2].inc(g.dma_start(
                    out=view3(wqr[h % 2], 128),
                    in_=dwq[h].rearrange("(k p) e -> p k e", p=128)), 16)
            g.wait_ge(s_v.h, kv_done)
            for i in range(16):
                if i >= 2:
                    kind, tok = free_tok["wo"][i - 2]
                    g.wait_ge(sem_of[kind].h, tok)
                s_wo[i % 2].inc(g.dma_start(
                    out=view3(wor[i % 2], 128),
                    in_=dwo[i].rearrange("(k p) e -> p k e", p=128)), 16)
            g.wait_ge(s_v.h, ao_done)
            for i in range(32):
                src = dwamp[i // 2] if i % 2 == 0 else dwph[i // 2]
                if i >= 2:
                    kind, tok = free_tok["wa"][i - 2]
                    g.wait_ge(sem_of[kind].h, tok)
                s_wa[i % 2].inc(g.dma_start(
                    out=view3(war[i % 2], 128),
                    in_=src.rearrange("(k p) e -> p k e", p=128)), 16)
            ring = {"we": 3, "wp": 2, "wh": 3}
            sems = {"we": s_we, "wp": s_wp, "wh": s_wh}
            for rec in ffn_order:
                kind, i = rec[0], rec[1]
                if i >= ring[kind]:
                    fk, tok = free_tok[kind][i - ring[kind]]
                    g.wait_ge(sem_of[fk].h, tok)
                if kind == "we":
                    _, _, l, ec = rec
                    src = dwe[l, ec]
                    dst = view3(ewr[i % 3], 128)
                elif kind == "wp":
                    _, _, l, fc, hf = rec
                    src = dwpw[l, fc, hf * 2048:(hf + 1) * 2048]
                    dst = view3(pwr[i % 2], 128)
                else:
                    _, _, l, p, fo = rec
                    src = dwpa[l, p, fo]
                    dst = view3(par[i % 3], 128)
                sems[kind][i % ring[kind]].inc(g.dma_start(
                    out=dst, in_=src.rearrange("(k p) e -> p k e", p=128)), 16)

    return nc


_NC_CACHE = {}


def _get_nc():
    if "nc" not in _NC_CACHE:
        _NC_CACHE["nc"] = build_program()
    return _NC_CACHE["nc"]


def kernel(L0_attn_x, L0_attn_Wq, L0_attn_Wk, L0_attn_Wv, L0_attn_Wo,
           L0_ffn_amp_w, L0_ffn_amp_b, L0_ffn_ph_w, L0_ffn_ph_b,
           L0_ffn_ent_w, L0_ffn_ent_b, L0_ffn_ent_pw, L0_ffn_ent_pb,
           L0_ffn_path_w, L0_ffn_path_b, L0_ffn_phase,
           L0_ffn_ln3_g, L0_ffn_ln3_b, L0_ffn_ln4_g, L0_ffn_ln4_b):
    x = np.ascontiguousarray(np.asarray(L0_attn_x, dtype=np.float32))

    def stage_cols(w, nb):
        w = np.asarray(w, dtype=np.float32)
        return np.ascontiguousarray(
            w.reshape(w.shape[0], nb, 128).transpose(1, 0, 2))

    wq_s = stage_cols(L0_attn_Wq, 16)
    wo_s = stage_cols(L0_attn_Wo, 16)
    ampw_s = stage_cols(L0_ffn_amp_w, 16)
    phw_s = stage_cols(L0_ffn_ph_w, 16)
    entw_s = np.stack([stage_cols(np.asarray(L0_ffn_ent_w)[l], 64)
                       for l in range(2)])
    pw_s = np.stack([stage_cols(np.asarray(L0_ffn_ent_pw)[l], 16)
                     for l in range(2)])
    path_s = np.stack([np.stack([stage_cols(np.asarray(L0_ffn_path_w)[l, p], 16)
                                 for p in range(2)]) for l in range(2)])
    wk = np.ascontiguousarray(np.asarray(L0_attn_Wk, dtype=np.float32))
    wv = np.ascontiguousarray(np.asarray(L0_attn_Wv, dtype=np.float32))

    pv = np.zeros((512, 128), np.float32)

    def put(vec, c0):
        v = np.asarray(vec, dtype=np.float32).reshape(-1, 128)
        pv[c0:c0 + v.shape[0]] = v

    put(L0_ffn_amp_b, 0)
    put(L0_ffn_ph_b, 16)
    put(np.asarray(L0_ffn_ent_b)[0], 32)
    put(np.asarray(L0_ffn_ent_b)[1], 96)
    put(np.asarray(L0_ffn_ent_pb)[0], 160)
    put(np.asarray(L0_ffn_ent_pb)[1], 176)
    put(np.asarray(L0_ffn_path_b)[0, 0], 192)
    put(np.asarray(L0_ffn_path_b)[0, 1], 208)
    put(np.asarray(L0_ffn_path_b)[1, 0], 224)
    put(np.asarray(L0_ffn_path_b)[1, 1], 240)
    put(np.asarray(L0_ffn_phase)[0, 0], 256)
    put(np.asarray(L0_ffn_phase)[0, 1], 272)
    put(np.asarray(L0_ffn_phase)[1, 0], 288)
    put(np.asarray(L0_ffn_phase)[1, 1], 304)
    put(np.asarray(L0_ffn_ln3_g)[0], 320)
    put(np.asarray(L0_ffn_ln3_b)[0], 336)
    put(np.asarray(L0_ffn_ln3_g)[1], 352)
    put(np.asarray(L0_ffn_ln3_b)[1], 368)
    put(np.asarray(L0_ffn_ln4_g)[0], 384)
    put(np.asarray(L0_ffn_ln4_b)[0], 400)
    put(np.asarray(L0_ffn_ln4_g)[1], 416)
    put(np.asarray(L0_ffn_ln4_b)[1], 432)
    pvec = np.ascontiguousarray(pv.reshape(4, 128, 128))

    nc = _get_nc()
    in_maps = []
    for c in range(8):
        b, par_ = c // 2, c % 2
        gbl = GBLK[par_]
        rows = np.concatenate([np.arange(gb * 128, (gb + 1) * 128) for gb in gbl])
        x_own = np.ascontiguousarray(x[b][rows]).reshape(8, 128, D)
        x_seq = np.ascontiguousarray(x[b]).reshape(16, 128, D)
        qpos = np.empty((128, 8), np.float32)
        qrow = np.empty((128, 1024), np.float32)
        for j, gb in enumerate(gbl):
            qpos[:, j] = gb * 128 + np.arange(128)
            qrow[:, j * 128:(j + 1) * 128] = gb * 128 + np.arange(128)[None, :]
        in_maps.append({
            "x_own": x_own, "x_seq": x_seq, "qpos": qpos, "qposrow": qrow,
            "pvec": pvec, "wq_s": wq_s, "wk": wk, "wv": wv, "wo_s": wo_s,
            "ampw_s": ampw_s, "phw_s": phw_s, "entw_s": entw_s,
            "pw_s": pw_s, "path_s": path_s,
        })

    res = run_bass_kernel_spmd(nc, in_maps, core_ids=list(range(8)))

    out = np.empty((4, 2048, D), np.float32)
    for c in range(8):
        b, par_ = c // 2, c % 2
        gbl = GBLK[par_]
        o = np.asarray(res.results[c]["out"]).reshape(1024, D)
        for j, gb in enumerate(gbl):
            out[b, gb * 128:(gb + 1) * 128] = o[j * 128:(j + 1) * 128]
    return out


# version 13
# speedup vs baseline: 1.0294x; 1.0294x over previous
import sys
import os

for _p in ("/opt/trn_rl_repo",):
    if _p not in sys.path and os.path.isdir(_p):
        sys.path.insert(0, _p)

import numpy as np
import concourse.bass as bass
import concourse.mybir as mybir
from concourse.bass_utils import run_bass_kernel_spmd
from contextlib import ExitStack

F32 = mybir.dt.float32
BF16 = mybir.dt.bfloat16
I32 = mybir.dt.int32
AF = mybir.ActivationFunctionType
ALU = mybir.AluOpType
AX = mybir.AxisListType

D = 2048
NF = 16
EPS = 1e-5
PI = float(np.pi)
TWO_PI = float(2 * np.pi)
E_SLOT = [2, 4, 6, 8, 10, 12, 14, 16]
GBLK = {0: [0, 3, 4, 7, 8, 11, 12, 15],
        1: [1, 2, 5, 6, 9, 10, 13, 14]}
QSCALE = float(1.0 / np.sqrt(128.0))
f32 = np.float32


class Sem:
    def __init__(self, nc, ctx, name):
        self.h = ctx.enter_context(nc.semaphore(name))
        self.n = 0

    def inc(self, instr, k=1):
        instr.then_inc(self.h, k)
        self.n += k
        return self.n


class SbufMap:
    def __init__(self, nc, base, top):
        self.nc = nc
        self.base = base
        self.top = top
        self.placed = []

    def alloc(self, name, cols, dtype, phases):
        size = cols * mybir.dt.size(dtype)
        ph = set(phases)
        placed = sorted([p for p in self.placed if p[3] & ph], key=lambda p: p[1])
        cands = sorted(set([self.base] + [o + s for (_, o, s, _) in placed]))
        off = None
        for c in cands:
            c = (c + 63) & ~63
            if c + size > self.top:
                continue
            if all(c + size <= o or c >= o + s for (_, o, s, _) in placed):
                off = c
                break
        if off is None:
            lines = [f"{n}@{o}+{sz}" for (n, o, sz, p) in placed]
            raise AssertionError(
                f"SBUF alloc failed: {name} ({size}B) phases={sorted(ph)}\n"
                + "\n".join(lines))
        self.placed.append((name, off, size, ph))
        return self.nc.alloc_sbuf_tensor_at(name, [128, cols], dtype, offset=off)


def view3(ap_tensor, inner):
    return ap_tensor[:].rearrange("p (f c) -> p f c", c=inner)


def build_program():
    nc = bass.Bass("TRN2", target_bir_lowering=False, debug=False)

    dx_own = nc.dram_tensor("x_own", [8, 128, D], F32, kind="ExternalInput").ap()
    dx_seq = nc.dram_tensor("x_seq", [16, 128, D], F32, kind="ExternalInput").ap()
    dqpos = nc.dram_tensor("qpos", [128, 8], F32, kind="ExternalInput").ap()
    dqrow = nc.dram_tensor("qposrow", [128, 1024], F32, kind="ExternalInput").ap()
    dpvec = nc.dram_tensor("pvec", [4, 128, 128], F32, kind="ExternalInput").ap()
    dwq = nc.dram_tensor("wq_s", [16, 2048, 128], F32, kind="ExternalInput").ap()
    dwk = nc.dram_tensor("wk", [2048, 512], F32, kind="ExternalInput").ap()
    dwv = nc.dram_tensor("wv", [2048, 512], F32, kind="ExternalInput").ap()
    dwo = nc.dram_tensor("wo_s", [16, 2048, 128], F32, kind="ExternalInput").ap()
    dwamp = nc.dram_tensor("ampw_s", [16, 2048, 128], F32, kind="ExternalInput").ap()
    dwph = nc.dram_tensor("phw_s", [16, 2048, 128], F32, kind="ExternalInput").ap()
    dwe = nc.dram_tensor("entw_s", [2, 64, 2048, 128], F32, kind="ExternalInput").ap()
    dwpw = nc.dram_tensor("pw_s", [2, 16, 8192, 128], F32, kind="ExternalInput").ap()
    dwpa = nc.dram_tensor("path_s", [2, 2, 16, 2048, 128], F32,
                          kind="ExternalInput").ap()
    dout = nc.dram_tensor("out", [8, 128, D], F32, kind="ExternalOutput").ap()
    dx2 = nc.dram_tensor("dbg_x2", [8, 128, D], F32, kind="ExternalOutput").ap()

    with ExitStack() as ctx:
        ps_s = ctx.enter_context(nc.psum_tensor("ps_s", [128, 2048], F32))
        ps_y = ctx.enter_context(nc.psum_tensor("ps_y", [128, 1024], F32))
        ps_p = ctx.enter_context(nc.psum_tensor("ps_p", [128, 512], F32))
        psT = ctx.enter_context(nc.psum_tensor("psT", [128, 1024], BF16))

        M = SbufMap(nc, base=35840, top=229340)
        ALL = list(range(12))
        params = M.alloc("params", 1024, F32, ALL)
        statsb = M.alloc("statsb", 128, F32, ALL)
        ident = M.alloc("ident", 128, BF16, ALL)
        ident32 = M.alloc("ident32", 128, F32, [0])
        ones32 = M.alloc("ones32", 128, F32, ALL)
        tmpA = M.alloc("tmpA", 2048, F32, [0, 1, 2, 3, 5, 6, 7, 8, 9, 10, 11])
        tmpB = M.alloc("tmpB", 2048, F32, [0, 1, 2, 3, 5, 6, 7, 8, 9, 10, 11])
        lnout2_bufs = [M.alloc("lnout", 2048, BF16, [1, 2, 3, 5, 11]),
                       M.alloc("lnoutB", 2048, BF16, [1, 2, 3, 5, 11])]
        lnout = lnout2_bufs[0]
        dumpb = M.alloc("dumpb", 2048, BF16, [1, 2, 3, 5])
        xst = [M.alloc("xstage0", 2048, F32, [0, 1, 2, 3, 5, 11]),
               M.alloc("xstage1", 2048, F32, [0, 1, 2, 3, 5, 11])]
        s_iota_i = M.alloc("s_iota_i", 2048, I32, [0])
        s_iota = M.alloc("s_iota", 2048, F32, list(range(5)))
        qposrow = M.alloc("qposrow", 1024, F32, [0])
        qpos = M.alloc("qpos", 8, F32, list(range(5)))
        ii0 = M.alloc("ii0", 128, I32, [0])
        ii1 = M.alloc("ii1", 128, I32, [0])
        pvst = [M.alloc("pvst0", 128, F32, [0]), M.alloc("pvst1", 128, F32, [0])]
        tbs = [M.alloc("tbs0", 2048, F32, [0]), M.alloc("tbs1", 2048, F32, [0])]
        qsin = M.alloc("qsin", 1024, F32, [0, 1, 2])
        qcos = M.alloc("qcos", 1024, F32, [0, 1, 2])
        ksin = M.alloc("ksin", 2048, F32, [0, 1, 2, 3])
        kcos = M.alloc("kcos", 2048, F32, [0, 1, 2, 3])
        ln1ownT_p = [M.alloc(f"ln1ownT{i}", 8192, BF16, [1, 2]) for i in range(2)]
        wqr = [M.alloc("wqr0", 2048, BF16, [0, 1, 2]), M.alloc("wqr1", 2048, BF16, [0, 1, 2])]
        wk_sb = M.alloc("wk_sb", 8192, BF16, [0, 1, 2, 3])
        wv_sb = M.alloc("wv_sb", 8192, BF16, [0, 1, 2, 3])
        lnT = M.alloc("lnT", 8192, BF16, [3])
        Q_p = [M.alloc(f"Q{i}", 4096, BF16, [2, 3, 4]) for i in range(4)]
        K_p = [M.alloc(f"K{i}", 2048, BF16, [3, 4]) for i in range(4)]
        V_p = [M.alloc(f"V{i}", 2048, BF16, [3, 4]) for i in range(4)]
        masks = M.alloc("masks", 4096, BF16, [4])
        PbE = [M.alloc(f"PbE{i}", 2048, BF16, [4]) for i in range(4)]
        PbM = M.alloc("PbM", 2048, BF16, [4])
        PT_ap = [M.alloc(f"PTa{i}", 4096, BF16, [4]) for i in range(2)]
        PT_bp = [M.alloc(f"PTb{i}", 4096, BF16, [4]) for i in range(2)]
        y_p = [M.alloc(f"y{i}", 4096, BF16, [4, 5]) for i in range(4)]
        wor = [M.alloc("wor0", 2048, BF16, [4, 5]),
               M.alloc("wor1", 2048, BF16, [4, 5])]
        ao_tm_p = [M.alloc(f"ao_tm{i}", 8192, BF16, [5]) for i in range(2)]
        ao_fm = M.alloc("ao_fm", 1024, BF16, [5])
        ln2xT_p = [M.alloc(f"ln2xT{i}", 4096, BF16, [5, 6]) for i in range(4)]
        war = [M.alloc("war0", 2048, BF16, [5, 6]), M.alloc("war1", 2048, BF16, [5, 6])]
        hbuf = M.alloc("hbuf", 16384, BF16, list(range(6, 12)))
        ewr = [M.alloc(f"ewr{i}", 2048, BF16, list(range(6, 11))) for i in range(3)]
        pwr = [M.alloc(f"pwr{i}", 2048, BF16, list(range(7, 11))) for i in range(2)]
        par = [M.alloc(f"par{i}", 2048, BF16, list(range(7, 11))) for i in range(3)]
        ebuf = [M.alloc(f"ebuf{i}", 8192, BF16, list(range(7, 11)))
                for i in range(4)]
        hint_p = [M.alloc(f"hint{i}", 2048, F32, list(range(7, 11)))
                  for i in range(4)]
        peak = max(o + s for (_, o, s, _) in M.placed)
        assert peak <= 229340, f"SBUF peak {peak}"

        def ln1ownT(fc, c0, c1):
            t = ln1ownT_p[fc // 8]
            base = (fc % 8) * 1024
            return t[:, base + c0:base + c1]

        def Qsl(qh, c0, c1):
            t = Q_p[qh // 4]
            base = (qh % 4) * 1024
            return t[:, base + c0:base + c1]

        def Ksl(g, c0, c1):
            return K_p[g][:, c0:c1]

        def Vsl(sc):
            return V_p[sc // 4][:, (sc % 4) * 512:(sc % 4 + 1) * 512]

        def Vtile(sc, g):
            return V_p[sc // 4][:, (sc % 4) * 512 + g * 128:
                                (sc % 4) * 512 + (g + 1) * 128]

        def PTsl(which, sc_piece):
            return (PT_ap if which == 0 else PT_bp)[sc_piece]

        def ysl(qh_group):
            return y_p[qh_group]

        def hintsl(fc):
            return hint_p[fc // 4][:, (fc % 4) * 512:(fc % 4 + 1) * 512]

        def etile(ec):
            t = ebuf[ec // 16]
            c = (ec % 16) * 512
            return t[:, c:c + 512]

        s_x = [Sem(nc, ctx, f"s_x{k}") for k in range(2)]
        s_qp = Sem(nc, ctx, "s_qp")
        s_sp = [Sem(nc, ctx, f"s_sp{k}") for k in range(2)]
        s_o = [Sem(nc, ctx, f"s_o{k}") for k in range(2)]
        s_g = Sem(nc, ctx, "s_g")
        s_wm = [Sem(nc, ctx, f"s_wm{k}") for k in range(2)]
        s_kv = Sem(nc, ctx, "s_kv")
        s_t = Sem(nc, ctx, "s_t")
        s_v = Sem(nc, ctx, "s_v")
        s_sc = Sem(nc, ctx, "s_sc")
        s_wq = [Sem(nc, ctx, f"s_wq{k}") for k in range(2)]
        s_wo = [Sem(nc, ctx, f"s_wo{k}") for k in range(2)]
        s_wa = [Sem(nc, ctx, f"s_wa{k}") for k in range(2)]
        s_we = [Sem(nc, ctx, f"s_we{k}") for k in range(3)]
        s_wp = [Sem(nc, ctx, f"s_wp{k}") for k in range(2)]
        s_wh = [Sem(nc, ctx, f"s_wh{k}") for k in range(3)]

        free_tok = {"wq": [], "wo": [], "wa": [], "we": [], "wp": [], "wh": []}
        ffn_order = []

        blk = ctx.enter_context(nc.Block())

        @blk.gpsimd
        def _(g):
            g.iota(s_iota_i[:], pattern=[[1, 2048]], base=0, channel_multiplier=0)
            g.iota(ii0[:], pattern=[[0, 128]], base=0, channel_multiplier=1)
            last = g.iota(ii1[:], pattern=[[1, 128]], base=0, channel_multiplier=0)
            s_g.inc(last)
            for i in range(4):
                if i >= 2:
                    g.wait_ge(s_t.h, i - 1)
                s_wm[i % 2].inc(g.dma_start(out=pvst[i % 2][:], in_=dpvec[i]), 16)
            s_kv.inc(g.dma_start(
                out=view3(wk_sb, 512), in_=dwk.rearrange("(k p) e -> p k e", p=128)), 16)
            s_kv.inc(g.dma_start(
                out=view3(wv_sb, 512), in_=dwv.rearrange("(k p) e -> p k e", p=128)), 16)
        tok_g = s_g.n

        @blk.vector
        def _(v):
            v.wait_ge(s_g.h, tok_g)
            v.tensor_copy(s_iota[:], s_iota_i[:])
            v.tensor_tensor(ident[:], ii0[:], ii1[:], ALU.is_equal)
            v.tensor_tensor(ident32[:], ii0[:], ii1[:], ALU.is_equal)
            v.memset(ones32[:], 1.0)
            i0 = v.tensor_copy(statsb[:, 0:1], ii0[:, 0:1])
            tw = s_v.inc(i0)
            v.wait_ge(s_v.h, tw)
            iv = v.tensor_scalar_mul(statsb[:, 1:2], statsb[:, 0:1],
                                     float(-np.log(10000.0) / 64.0))
            s_v.inc(iv)
        tok_v0 = s_v.n

        @blk.scalar
        def _(sc):
            sc.wait_ge(s_v.h, tok_v0)
            ii = sc.activation(statsb[:, 2:3], statsb[:, 1:2], AF.Exp)
            s_sc.inc(ii)
        tok_if = s_sc.n

        pv_ev = []
        for i in range(4):
            @blk.tensor
            def _(t, i=i):
                t.wait_ge(s_wm[i % 2].h, 16 * (i // 2 + 1))
                if i == 0:
                    t.wait_ge(s_v.h, tok_v0)
                if i >= 1:
                    t.wait_ge(s_v.h, pv_ev[i - 1])
                tr = t.transpose(ps_p[:, 0:128], pvst[i % 2][:], ident32[:])
                s_t.inc(tr)
            tt = s_t.n

            @blk.vector
            def _(v, i=i, tt=tt):
                v.wait_ge(s_t.h, tt)
                iv = v.tensor_copy(params[:, i * 128:(i + 1) * 128], ps_p[:, 0:128])
                pv_ev.append(s_v.inc(iv))
        tok_params = pv_ev[-1]

        def table_pipeline(src_rows64_ap, width, dst_sin, dst_cos, scale):
            for (dst, shift) in ((dst_sin, 0.0), (dst_cos, PI / 2)):
                sA = tmpA[0:64, 0:width]
                sB = tmpB[0:64, 0:width]
                sC = tbs[0][0:64, 0:width]
                sD = tbs[1][0:64, 0:width]
                i32s = s_iota_i[0:64, 0:width]

                @blk.vector
                def _(v, dst=dst, shift=shift, sA=sA, sB=sB, sC=sC, sD=sD, i32s=i32s):
                    v.wait_ge(s_sc.h, tok_if)
                    v.tensor_scalar_mul(sA, src_rows64_ap, statsb[0:64, 2:3])
                    v.tensor_scalar(sB, sA, float(1.0), float(shift + 16 * PI),
                                    ALU.mult, ALU.add)
                    v.tensor_scalar_mul(sC, sB, float(1.0 / TWO_PI))
                    v.tensor_copy(i32s, sC)
                    v.tensor_copy(sC, i32s)
                    v.scalar_tensor_tensor(sD, sC, float(-TWO_PI), sB, ALU.mult, ALU.add)
                    v.tensor_single_scalar(sC, sD, float(PI), ALU.is_gt)
                    iv = v.scalar_tensor_tensor(sB, sC, float(-TWO_PI), sD,
                                                ALU.mult, ALU.add)
                    s_v.inc(iv)
                tv = s_v.n

                @blk.scalar
                def _(sc, sA=sA, sB=sB, tv=tv):
                    sc.wait_ge(s_v.h, tv)
                    ii = sc.activation(sA, sB, AF.Sin)
                    s_sc.inc(ii)
                ts_ = s_sc.n

                @blk.vector
                def _(v, dst=dst, sA=sA, ts_=ts_):
                    v.wait_ge(s_sc.h, ts_)
                    v.tensor_scalar_mul(dst[0:64, 0:width], sA, float(scale))
                    iv = v.tensor_copy(dst[64:128, 0:width], dst[0:64, 0:width])
                    s_v.inc(iv)

        table_pipeline(s_iota[0:64, :], 2048, ksin, kcos, 1.0)
        @blk.sync
        def _(sync):
            s_qp.inc(sync.dma_start(out=qposrow[:], in_=dqrow), 16)
            s_qp.inc(sync.dma_start(out=qpos[:], in_=dqpos), 16)
        tok_qr = s_qp.n

        @blk.vector
        def _(v):
            v.wait_ge(s_qp.h, tok_qr)
        table_pipeline(qposrow[0:64, :], 1024, qsin, qcos, QSCALE)

        @blk.vector
        def _(v):
            v.wait_ge(s_v.h, tok_params)
            ph_ap = params[:, 256:320]
            sA = tmpA[:, 0:64]
            sB = tmpB[:, 0:64]
            sC = tmpA[:, 64:128]
            sD = tmpB[:, 64:128]
            i32s = s_iota_i[:, 0:64]
            v.tensor_scalar(sB, ph_ap, float(1.0), float(PI / 4 + 16 * PI),
                            ALU.mult, ALU.add)
            v.tensor_scalar_mul(sC, sB, float(1.0 / TWO_PI))
            v.tensor_copy(i32s, sC)
            v.tensor_copy(sC, i32s)
            v.scalar_tensor_tensor(sD, sC, float(-TWO_PI), sB, ALU.mult, ALU.add)
            v.tensor_single_scalar(sC, sD, float(PI), ALU.is_gt)
            iv = v.scalar_tensor_tensor(sB, sC, float(-TWO_PI), sD, ALU.mult, ALU.add)
            s_v.inc(iv)
        tvm = s_v.n

        @blk.scalar
        def _(sc):
            sc.wait_ge(s_v.h, tvm)
            ii = sc.activation(tmpA[:, 0:64], tmpB[:, 0:64], AF.Sin)
            s_sc.inc(ii)
        tsm = s_sc.n

        @blk.vector
        def _(v):
            v.wait_ge(s_sc.h, tsm)
            im = v.tensor_scalar_mul(params[:, 512:576], tmpA[:, 0:64],
                                     float(np.sqrt(2.0)))
            tw = s_v.inc(im)
            v.wait_ge(s_v.h, tw)
            for l in range(2):
                pb0 = params[:, 192 + 32 * l:208 + 32 * l]
                pb1 = params[:, 208 + 32 * l:224 + 32 * l]
                m0 = params[:, 512 + 32 * l:528 + 32 * l]
                m1 = params[:, 528 + 32 * l:544 + 32 * l]
                v.tensor_tensor(tmpA[:, 0:16], pb0, m0, ALU.mult)
                i1 = v.tensor_tensor(tmpA[:, 16:32], pb1, m1, ALU.mult)
                tw = s_v.inc(i1)
                v.wait_ge(s_v.h, tw)
                iv = v.tensor_add(params[:, 576 + 16 * l:592 + 16 * l],
                                  tmpA[:, 0:16], tmpA[:, 16:32])
            s_v.inc(iv)
        tok_tables = s_v.n

        psT_free = [0]
        lnout_free = [0, 0]

        def heat(t, n):
            for _ in range(n):
                t.matmul(ps_p[:, 0:128], ident[:], ident[:], start=True, stop=True)

        lnpar = {"i": 0}

        def emit_ln_chunk(x_src_ap, dst_writer, pre_v=None, pre_x=None,
                          heat_ok=True):
            par = lnpar["i"] % 2
            lnpar["i"] += 1
            C = 8 + 32 * par
            lnout = lnout2_bufs[par]

            def sb_(k):
                return statsb[:, C + k:C + k + 1]
            """LN over free dim of staged f32 [128,2048]; bf16 result transposed
            into dstT tiles at [f, within_off:within_off+128], f=0..15."""
            @blk.vector
            def _(v):
                if pre_x is not None:
                    v.wait_ge(s_x[pre_x[0]].h, pre_x[1])
                if pre_v is not None:
                    v.wait_ge(s_v.h, pre_v)
                iv = v.reduce_sum(sb_(0), x_src_ap, axis=AX.X)
                s_v.inc(iv)
            t1 = s_v.n

            V0 = s_v.n
            t3, t5, t7 = V0 + 3, V0 + 4, V0 + 6
            tk_ = {}

            @blk.scalar
            def _(sc):
                if pre_x is not None:
                    sc.wait_ge(s_x[pre_x[0]].h, pre_x[1])
                if pre_v is not None:
                    sc.wait_ge(s_v.h, pre_v)
                ii = sc.activation(dumpb[:], x_src_ap, AF.Square,
                                   accum_out=sb_(1))
                tk_["t2"] = s_sc.inc(ii)
                sc.wait_ge(s_v.h, t3)
                ii = sc.activation(sb_(5), sb_(4), AF.Ln,
                                   bias=np.float32(EPS))
                tk_["t4"] = s_sc.inc(ii)
                sc.wait_ge(s_v.h, t5)
                ii = sc.activation(sb_(7), sb_(6), AF.Exp)
                tk_["t6"] = s_sc.inc(ii)
                sc.wait_ge(s_v.h, t7)
                if lnout_free[par]:
                    sc.wait_ge(s_t.h, lnout_free[par])
                ii = sc.activation(lnout[:], x_src_ap, AF.Identity,
                                   bias=sb_(9), scale=sb_(7))
                s_sc.inc(ii)
            t2, t4, t6 = tk_["t2"], tk_["t4"], tk_["t6"]
            t8 = s_sc.n

            @blk.vector
            def _(v):
                v.wait_ge(s_sc.h, t2)
                i0 = v.tensor_scalar_mul(sb_(2), sb_(0),
                                         float(1.0 / D))
                tw = s_v.inc(i0)
                v.wait_ge(s_v.h, tw)
                i1 = v.tensor_tensor(sb_(3), sb_(2),
                                     sb_(2), ALU.mult)
                tw = s_v.inc(i1)
                v.wait_ge(s_v.h, tw)
                iv = v.scalar_tensor_tensor(sb_(4), sb_(1),
                                            float(1.0 / D), sb_(3),
                                            ALU.mult, ALU.subtract)
                s_v.inc(iv)
            assert s_v.n == t3, (s_v.n, t3)

            @blk.vector
            def _(v):
                v.wait_ge(s_sc.h, t4)
                iv = v.tensor_scalar_mul(sb_(6), sb_(5),
                                         float(-0.5))
                s_v.inc(iv)
            assert s_v.n == t5, (s_v.n, t5)

            @blk.vector
            def _(v):
                v.wait_ge(s_sc.h, t6)
                i0 = v.tensor_tensor(sb_(8), sb_(2),
                                     sb_(7), ALU.mult)
                tw = s_v.inc(i0)
                v.wait_ge(s_v.h, tw)
                iv = v.tensor_scalar_mul(sb_(9), sb_(8),
                                         float(-1.0))
                s_v.inc(iv)
            assert s_v.n == t7, (s_v.n, t7)

            for r in range(2):
                @blk.tensor
                def _(t, r=r, t8=t8, t3=t3, t6=t6):
                    if r == 0 and heat_ok:
                        t.wait_ge(s_v.h, t3)
                        heat(t, 12)
                        t.wait_ge(s_sc.h, t6)
                        heat(t, 12)
                    t.wait_ge(s_sc.h, t8)
                    if psT_free[0]:
                        t.wait_ge(s_v.h, psT_free[0])
                    for f in range(8):
                        tr = t.transpose(
                            psT[:, f * 128:(f + 1) * 128],
                            lnout[:, (8 * r + f) * 128:(8 * r + f + 1) * 128],
                            ident[:])
                    s_t.inc(tr)
                tt = s_t.n
                lnout_free[par] = tt

                @blk.vector
                def _(v, r=r, tt=tt):
                    v.wait_ge(s_t.h, tt)
                    o = dst_writer(r)
                    if isinstance(o, tuple):
                        v.tensor_copy(o[0], view3(psT, 128)[:, 0:4, :])
                        iv = v.tensor_copy(o[1], view3(psT, 128)[:, 4:8, :])
                    else:
                        iv = v.tensor_copy(o, view3(psT, 128))
                    psT_free[0] = s_v.inc(iv)
            return t8

        xtok = []
        apply_done = []
        for tch in range(8):
            @blk.sync
            def _(sync, tch=tch):
                if tch >= 2:
                    sync.wait_ge(s_sc.h, apply_done[tch - 2])
                xtok.append((tch % 2,
                             s_x[tch % 2].inc(sync.dma_start(out=xst[tch % 2][:],
                                                             in_=dx_own[tch]), 16)))
            def _w1(r, tch=tch):
                t = ln1ownT_p[r]
                return view3(t, 1024)[:, 0:8, tch * 128:(tch + 1) * 128]
            emit_ln_chunk(xst[tch % 2][:], _w1, pre_x=xtok[tch])
            apply_done.append(s_sc.n)
        ln1_done_v = psT_free[0]

        psq = [ps_y[:, 0:1024], ps_s[:, 0:1024]]
        rope_done = []
        q_mm_tok = []
        for h in range(16):
            @blk.tensor
            def _(t, h=h):
                t.wait_ge(s_wq[h % 2].h, 16 * (h // 2 + 1))
                if h == 0:
                    t.wait_ge(s_v.h, ln1_done_v)
                if h >= 2:
                    t.wait_ge(s_v.h, rope_done[h - 2])
                pq = psq[h % 2]
                for fc in range(16):
                    for hf in range(2):
                        mm = t.matmul(pq[:, hf * 512:(hf + 1) * 512],
                                      wqr[h % 2][:, fc * 128:(fc + 1) * 128],
                                      ln1ownT(fc, hf * 512, (hf + 1) * 512),
                                      start=(fc == 0), stop=(fc == 15))
                s_t.inc(mm)
            q_mm_tok.append(s_t.n)

            @blk.vector
            def _(v, h=h):
                v.wait_ge(s_t.h, q_mm_tok[h])
                pq = psq[h % 2]
                lo, hi = slice(0, 64), slice(64, 128)
                a = tmpA[0:64, 0:1024]
                b = tmpB[0:64, 0:1024]
                qc = Qsl(h, 0, 1024)
                v.tensor_tensor(a, pq[lo, :], qcos[lo, :], ALU.mult)
                v.tensor_tensor(b, pq[hi, :], qsin[hi, :], ALU.mult)
                v.tensor_sub(qc[lo, :], a, b)
                v.tensor_tensor(tmpA[64:128, 0:1024], pq[lo, :], qsin[lo, :],
                                ALU.mult)
                v.tensor_tensor(tmpB[64:128, 0:1024], pq[hi, :], qcos[hi, :],
                                ALU.mult)
                iv = v.tensor_add(qc[hi, :], tmpA[64:128, 0:1024],
                                  tmpB[64:128, 0:1024])
                rope_done.append(s_v.inc(iv))
            free_tok["wq"].append(("v", rope_done[h]))
        q_done = rope_done[-1]

        psk = [ps_p[:, 0:512], ps_s[:, 0:512]]
        psv = [ps_y[:, 0:512], ps_s[:, 512:1024]]
        kpipe = {"k": 0, "v": 0}
        kv_free = {}
        for pc in range(4):
            for tsb in range(4):
                gi = 8 + pc * 4 + tsb

                @blk.sync
                def _(sync, gi=gi, pc=pc, tsb=tsb):
                    sync.wait_ge(s_sc.h, apply_done[gi - 2])
                    xtok.append((gi % 2,
                                 s_x[gi % 2].inc(sync.dma_start(
                                     out=xst[gi % 2][:],
                                     in_=dx_seq[pc * 4 + tsb]), 16)))
                def _w3(r, tsb=tsb):
                    return view3(lnT, 512)[:, 8 * r:8 * r + 8,
                                           tsb * 128:(tsb + 1) * 128]
                emit_ln_chunk(xst[gi % 2][:], _w3, pre_x=xtok[gi], heat_ok=False)
                apply_done.append(s_sc.n)
            lnT_tok = psT_free[0]

            for g4 in range(4):
                i = kpipe["k"]

                @blk.tensor
                def _(t, pc=pc, g4=g4, i=i, lnT_tok=lnT_tok):
                    t.wait_ge(s_v.h, lnT_tok)
                    if pc == 0 and g4 == 0:
                        t.wait_ge(s_kv.h, 32)
                        t.wait_ge(s_v.h, q_done)
                    if i >= 2:
                        t.wait_ge(s_v.h, kv_free[("k", i - 2)])
                    pk = psk[i % 2]
                    for fc in range(16):
                        mm = t.matmul(
                            pk,
                            wk_sb[:, fc * 512 + g4 * 128:fc * 512 + (g4 + 1) * 128],
                            lnT[:, fc * 512:(fc + 1) * 512],
                            start=(fc == 0), stop=(fc == 15))
                    s_t.inc(mm)
                tk = s_t.n

                @blk.vector
                def _(v, pc=pc, g4=g4, i=i, tk=tk):
                    v.wait_ge(s_t.h, tk)
                    pk = psk[i % 2]
                    lo, hi = slice(0, 64), slice(64, 128)
                    cw = slice(pc * 512, (pc + 1) * 512)
                    a = tmpA[0:64, 0:512]
                    b = tmpB[0:64, 0:512]
                    kc = Ksl(g4, pc * 512, (pc + 1) * 512)
                    v.tensor_tensor(a, pk[lo, :], kcos[lo, cw], ALU.mult)
                    v.tensor_tensor(b, pk[hi, :], ksin[hi, cw], ALU.mult)
                    v.tensor_sub(kc[lo, :], a, b)
                    v.tensor_tensor(tmpA[64:128, 0:512], pk[lo, :], ksin[lo, cw],
                                    ALU.mult)
                    v.tensor_tensor(tmpB[64:128, 0:512], pk[hi, :], kcos[hi, cw],
                                    ALU.mult)
                    iv = v.tensor_add(kc[hi, :], tmpA[64:128, 0:512],
                                      tmpB[64:128, 0:512])
                    kv_free[("k", i)] = s_v.inc(iv)
                kpipe["k"] += 1

            for tsb in range(4):
                i = kpipe["v"]

                @blk.tensor
                def _(t, pc=pc, tsb=tsb, i=i, lnT_tok=lnT_tok):
                    t.wait_ge(s_v.h, lnT_tok)
                    if i >= 2:
                        t.wait_ge(s_v.h, kv_free[("v", i - 2)])
                    pv = psv[i % 2]
                    for fc in range(16):
                        mm = t.matmul(
                            pv,
                            lnT[:, fc * 512 + tsb * 128:fc * 512 + (tsb + 1) * 128],
                            wv_sb[:, fc * 512:(fc + 1) * 512],
                            start=(fc == 0), stop=(fc == 15))
                    s_t.inc(mm)
                tv2 = s_t.n

                @blk.vector
                def _(v, pc=pc, tsb=tsb, i=i, tv2=tv2):
                    v.wait_ge(s_t.h, tv2)
                    sc_ = pc * 4 + tsb
                    iv = v.tensor_copy(Vsl(sc_), psv[i % 2])
                    kv_free[("v", i)] = s_v.inc(iv)
                kpipe["v"] += 1
        kv_done = s_v.n

        EXPC = 24.0
        S0 = s_sc.n
        git = [0]
        pread_grp = [0]
        ptbuf_free = {0: 0, 1: 0}
        psy_free = {0: 0, 1: 0}
        for sp in range(4):
            ja, jb = 2 * sp, 2 * sp + 1
            Ea, Eb = E_SLOT[ja], E_SLOT[jb]

            @blk.vector
            def _(v, sp=sp, ja=ja, jb=jb, Ea=Ea, Eb=Eb):
                if sp == 0:
                    v.wait_ge(s_v.h, kv_done)
                v.tensor_single_scalar(masks[:, 0:Ea * 128], s_iota[:, 0:Ea * 128],
                                       qpos[:, ja:ja + 1], ALU.is_le)
                iv = v.tensor_single_scalar(masks[:, 2048:2048 + Eb * 128],
                                            s_iota[:, 0:Eb * 128],
                                            qpos[:, jb:jb + 1], ALU.is_le)
                s_v.inc(iv)
            mask_tok = s_v.n

            for g4 in range(4):
                for (sub, j, E, moff) in ((0, ja, Ea, 0),
                                          (1, jb, Eb, 2048)):
                    yoff = sub * 512
                    cols = E * 128
                    twobuf = cols <= 1024
                    base_it = git[0]
                    tqk = []
                    for qi in range(4):
                        qh = 4 * g4 + qi
                        it = base_it + qi
                        if twobuf:
                            base = (it % 2) * 1024
                            freedep = it - 2
                        else:
                            base = 0
                            freedep = it - 1

                        @blk.tensor
                        def _(t, j=j, qh=qh, cols=cols, g4=g4, base=base,
                              freedep=freedep, mask_tok=mask_tok):
                            t.wait_ge(s_v.h, mask_tok)
                            if freedep >= 0:
                                t.wait_ge(s_sc.h, S0 + freedep + 1)
                            c = 0
                            while c < cols:
                                w = min(512, cols - c)
                                mm = t.matmul(
                                    ps_s[:, base + c:base + c + w],
                                    Qsl(qh, j * 128, (j + 1) * 128),
                                    Ksl(g4, c, c + w),
                                    start=True, stop=True)
                                c += w
                            s_t.inc(mm)
                        tqk.append(s_t.n)

                    @blk.scalar
                    def _(sc, cols=cols, base_it=base_it, tqk=tuple(tqk),
                          prg=pread_grp[0]):
                        if prg:
                            sc.wait_ge(s_t.h, prg)
                        for qi in range(4):
                            it = base_it + qi
                            base = (it % 2) * 1024 if cols <= 1024 else 0
                            sc.wait_ge(s_t.h, tqk[qi])
                            ii = sc.activation(PbE[qi][:, 0:cols],
                                               ps_s[:, base:base + cols],
                                               AF.Exp, bias=np.float32(-EXPC))
                            s_sc.inc(ii)
                    assert s_sc.n == S0 + base_it + 4, (s_sc.n, S0, base_it)

                    tnorm = []

                    @blk.vector
                    def _(v, cols=cols, moff=moff, base_it=base_it):
                        for qi in range(4):
                            SC = 20 + 3 * (qi % 2)
                            v.wait_ge(s_sc.h, S0 + base_it + qi + 1)
                            i0 = v.scalar_tensor_tensor(
                                PbM[:, 0:cols], PbE[qi][:, 0:cols], float(1.0),
                                masks[:, moff:moff + cols],
                                ALU.mult, ALU.mult,
                                accum_out=statsb[:, SC + 1:SC + 2])
                            t0 = s_v.inc(i0)
                            v.wait_ge(s_v.h, t0)
                            i1 = v.reciprocal(statsb[:, SC + 2:SC + 3],
                                              statsb[:, SC + 1:SC + 2])
                            t1 = s_v.inc(i1)
                            v.wait_ge(s_v.h, t1)
                            iv = v.tensor_scalar_mul(PbE[qi][:, 0:cols],
                                                     PbM[:, 0:cols],
                                                     statsb[:, SC + 2:SC + 3])
                            tnorm.append(s_v.inc(iv))

                    for qi in range(4):
                        nrounds = (E + 7) // 8
                        for r in range(nrounds):
                            n_in_r = min(8, E - 8 * r)

                            @blk.tensor
                            def _(t, qi=qi, r=r, n_in_r=n_in_r, sub=sub,
                                  tn=tnorm[qi] if False else None):
                                t.wait_ge(s_v.h, tnorm[qi])
                                if psT_free[0]:
                                    t.wait_ge(s_v.h, psT_free[0])
                                if ptbuf_free[sub]:
                                    t.wait_ge(s_t.h, ptbuf_free[sub])
                                for k in range(n_in_r):
                                    sc_ = 8 * r + k
                                    tr = t.transpose(
                                        psT[:, k * 128:(k + 1) * 128],
                                        PbE[qi][:, sc_ * 128:(sc_ + 1) * 128],
                                        ident[:])
                                s_t.inc(tr)
                            ttr = s_t.n

                            @blk.vector
                            def _(v, r=r, n_in_r=n_in_r, qi=qi, sub=sub, ttr=ttr):
                                v.wait_ge(s_t.h, ttr)
                                o = view3(PTsl(sub, r), 512)[:, 0:n_in_r,
                                                             qi * 128:(qi + 1) * 128]
                                iv = v.tensor_copy(
                                    o, view3(psT, 128)[:, 0:n_in_r, :])
                                psT_free[0] = s_v.inc(iv)
                    pread_grp[0] = s_t.n
                    git[0] += 4
                    pt_tok = psT_free[0]

                    @blk.tensor
                    def _(t, g4=g4, E=E, yoff=yoff, sub=sub, pt_tok=pt_tok):
                        t.wait_ge(s_v.h, pt_tok)
                        if psy_free[sub]:
                            t.wait_ge(s_v.h, psy_free[sub])
                        for sc_ in range(E):
                            pt = PTsl(sub, sc_ // 8)
                            mm = t.matmul(
                                ps_y[:, yoff:yoff + 512],
                                Vtile(sc_, g4),
                                pt[:, (sc_ % 8) * 512:(sc_ % 8 + 1) * 512],
                                start=(sc_ == 0), stop=(sc_ == E - 1))
                        ptbuf_free[sub] = s_t.inc(mm)
                    tav = s_t.n

                    @blk.vector
                    def _(v, g4=g4, j=j, yoff=yoff, sub=sub, tav=tav):
                        v.wait_ge(s_t.h, tav)
                        o = view3(ysl(g4), 1024)[:, 0:4, j * 128:(j + 1) * 128]
                        i_ = view3(ps_y, 128)[:, 4 * sub:4 * sub + 4, :]
                        iv = v.tensor_copy(o, i_)
                        psy_free[sub] = s_v.inc(iv)
        y_done = s_v.n

        pso = [ps_y[:, 0:1024], ps_s[:, 0:1024]]
        wo_psum_free = {}
        aofm_free = [0]
        wo_mm_tok = []
        for fo in range(16):
            @blk.tensor
            def _(t, fo=fo):
                t.wait_ge(s_wo[fo % 2].h, 16 * (fo // 2 + 1))
                if fo == 0:
                    t.wait_ge(s_v.h, y_done)
                if fo >= 2:
                    t.wait_ge(s_v.h, wo_psum_free[fo % 2])
                po = pso[fo % 2]
                for dc in range(16):
                    yb = y_p[dc // 4]
                    for hf in range(2):
                        mm = t.matmul(po[:, hf * 512:(hf + 1) * 512],
                                      wor[fo % 2][:, dc * 128:(dc + 1) * 128],
                                      yb[:, (dc % 4) * 1024 + hf * 512:
                                         (dc % 4) * 1024 + (hf + 1) * 512],
                                      start=(dc == 0), stop=(dc == 15))
                s_t.inc(mm)
            wo_mm_tok.append(s_t.n)
            free_tok["wo"].append(("t", wo_mm_tok[fo]))

            @blk.vector
            def _(v, fo=fo):
                v.wait_ge(s_t.h, wo_mm_tok[fo])
                if aofm_free[0]:
                    v.wait_ge(s_t.h, aofm_free[0])
                iv = v.tensor_copy(ao_fm[:], pso[fo % 2][:, 0:1024])
                wo_psum_free[fo % 2] = s_v.inc(iv)
            tev = s_v.n

            @blk.tensor
            def _(t, fo=fo, tev=tev):
                t.wait_ge(s_v.h, tev)
                if psT_free[0]:
                    t.wait_ge(s_v.h, psT_free[0])
                for k in range(8):
                    tr = t.transpose(psT[:, k * 128:(k + 1) * 128],
                                     ao_fm[:, k * 128:(k + 1) * 128], ident[:])
                aofm_free[0] = s_t.inc(tr)
            ttr2 = s_t.n

            @blk.vector
            def _(v, fo=fo, ttr2=ttr2):
                v.wait_ge(s_t.h, ttr2)
                o0 = view3(ao_tm_p[0], 2048)[:, 0:4, fo * 128:(fo + 1) * 128]
                o1 = view3(ao_tm_p[1], 2048)[:, 0:4, fo * 128:(fo + 1) * 128]
                v.tensor_copy(o0, view3(psT, 128)[:, 0:4, :])
                iv = v.tensor_copy(o1, view3(psT, 128)[:, 4:8, :])
                psT_free[0] = s_v.inc(iv)
        ao_done = psT_free[0]

        spill_tok = []
        add_v_tok = []
        for tch in range(8):
            @blk.sync
            def _(sync, tch=tch):
                if tch < 2:
                    sync.wait_ge(s_v.h, ao_done)
                else:
                    sync.wait_ge(s_sc.h, apply_done[24 + tch - 2])
                xtok.append((tch % 2,
                             s_x[tch % 2].inc(sync.dma_start(out=xst[tch % 2][:],
                                                             in_=dx_own[tch]), 16)))

            @blk.vector
            def _(v, tch=tch):
                v.wait_ge(s_x[xtok[24 + tch][0]].h, xtok[24 + tch][1])
                if tch == 0:
                    v.wait_ge(s_v.h, ao_done)
                if tch >= 2:
                    v.wait_ge(s_sp[spill_tok[tch - 2][0]].h, spill_tok[tch - 2][1])
                dst = tmpA if tch % 2 == 0 else tmpB
                ao_pc = ao_tm_p[tch // 4][:, (tch % 4) * 2048:(tch % 4 + 1) * 2048]
                iv = v.tensor_add(dst[:], xst[tch % 2][:], ao_pc)
                add_v_tok.append(s_v.inc(iv))

            @blk.scalar
            def _(sc, tch=tch):
                sc.wait_ge(s_v.h, add_v_tok[tch])
                src = tmpA if tch % 2 == 0 else tmpB
                spill_tok.append((tch % 2,
                                  s_sp[tch % 2].inc(sc.dma_start(out=dx2[tch],
                                                                 in_=src[:]), 16)))

            src = tmpA if tch % 2 == 0 else tmpB
            def _w5(r, tch=tch):
                return (view3(ln2xT_p[2 * r], 1024)[:, 0:4,
                                                    tch * 128:(tch + 1) * 128],
                        view3(ln2xT_p[2 * r + 1], 1024)[:, 0:4,
                                                        tch * 128:(tch + 1) * 128])
            emit_ln_chunk(src[:], _w5, pre_v=add_v_tok[tch])
            apply_done.append(s_sc.n)
        ln2_done = psT_free[0]

        psa = [ps_y[:, 0:1024], ps_s[:, 0:1024]]
        sig_tok = {}
        tanh_tok = {}
        hstt_tok = {}
        for fo in range(16):
            tmm_p = {}
            for p in range(2):
                i = 2 * fo + p

                @blk.tensor
                def _(t, fo=fo, p=p, i=i):
                    t.wait_ge(s_wa[i % 2].h, 16 * (i // 2 + 1))
                    if i == 0:
                        t.wait_ge(s_v.h, ln2_done)
                    if fo >= 1:
                        t.wait_ge(s_sc.h, (sig_tok if p == 0 else tanh_tok)[fo - 1])
                    pa_ = psa[p]
                    for fc in range(16):
                        for hf in range(2):
                            mm = t.matmul(pa_[:, hf * 512:(hf + 1) * 512],
                                          war[i % 2][:, fc * 128:(fc + 1) * 128],
                                          ln2xT_p[fc // 4][:, (fc % 4) * 1024 + hf * 512:
                                                  (fc % 4) * 1024 + (hf + 1) * 512],
                                          start=(fc == 0), stop=(fc == 15))
                    s_t.inc(mm)
                free_tok["wa"].append(("t", s_t.n))
                tmm_p[p] = s_t.n

            tmx = s_v.n + 1

            @blk.scalar
            def _(sc, fo=fo, tmm_p=dict(tmm_p), tmx=tmx):
                sc.wait_ge(s_t.h, tmm_p[0])
                if fo == 0:
                    sc.wait_ge(s_sp[spill_tok[6][0]].h, spill_tok[6][1])
                    sc.wait_ge(s_sp[spill_tok[7][0]].h, spill_tok[7][1])
                if fo >= 1:
                    sc.wait_ge(s_v.h, hstt_tok[fo - 1])
                ii = sc.activation(tmpA[:, 0:1024], psa[0], AF.Sigmoid,
                                   bias=params[:, fo:fo + 1])
                sig_tok[fo] = s_sc.inc(ii)
                sc.wait_ge(s_t.h, tmm_p[1])
                ii = sc.activation(tmpB[:, 0:1024], psa[1], AF.Tanh,
                                   bias=params[:, 16 + fo:17 + fo])
                tanh_tok[fo] = s_sc.inc(ii)
                sc.wait_ge(s_v.h, tmx)
                ii = sc.activation(tmpB[:, 1024:2048], tmpB[:, 0:1024], AF.Sin)
                s_sc.inc(ii)
            tsn = s_sc.n

            @blk.vector
            def _(v, fo=fo):
                v.wait_ge(s_sc.h, tanh_tok[fo])
                v.tensor_scalar(tmpB[:, 1024:2048], tmpB[:, 0:1024], float(PI),
                                float(PI / 4), ALU.mult, ALU.add)
                iv = v.tensor_single_scalar(tmpA[:, 1024:2048], tmpB[:, 1024:2048],
                                            float(PI), ALU.is_gt)
                iv = v.scalar_tensor_tensor(tmpB[:, 0:1024], tmpA[:, 1024:2048],
                                            float(-TWO_PI), tmpB[:, 1024:2048],
                                            ALU.mult, ALU.add)
                s_v.inc(iv)
            assert s_v.n == tmx, (s_v.n, tmx)

            @blk.vector
            def _(v, fo=fo, tsn=tsn):
                v.wait_ge(s_sc.h, tsn)
                iv = v.scalar_tensor_tensor(hbuf[:, fo * 1024:(fo + 1) * 1024],
                                            tmpB[:, 1024:2048], float(np.sqrt(2.0)),
                                            tmpA[:, 0:1024], ALU.mult, ALU.mult)
                hstt_tok[fo] = s_v.inc(iv)
        h_done = s_v.n

        pse = [ps_p[:, 0:512], ps_s[:, 1536:2048]]
        psent = [ps_y[:, 0:512], ps_y[:, 512:1024]]
        ne_used = [0]
        npw_used = [0]
        nh_used = [0]
        e_evac_tok = {}
        prev_ent_last = [0]
        prev_block_apply = [0]

        def emit_fm_ln(gcol, bcol, src_tok, tb, stats_wait_v, weave=None):
            sq_tok = {}
            stat_mm_tok = {}

            @blk.tensor
            def _(t):
                t.wait_ge(s_v.h, src_tok)
                if stats_wait_v[0]:
                    t.wait_ge(s_v.h, stats_wait_v[0])
                for fc in range(16):
                    mm = t.matmul(ps_s[:, 0:512], ones32[:], hintsl(fc),
                                  start=(fc == 0), stop=(fc == 15))
                s_t.inc(mm)
            SQ0 = s_sc.n
            for fc in range(16):
                sq_tok[fc] = SQ0 + fc + 1
            for g4_ in range(4):
                @blk.tensor
                def _(t, g4_=g4_):
                    for fc in range(4 * g4_, 4 * g4_ + 4):
                        t.wait_ge(s_sc.h, sq_tok[fc])
                        mm = t.matmul(ps_s[:, 512:1024], ones32[:],
                                      tmpB[:, (fc % 4) * 512:(fc % 4 + 1) * 512],
                                      start=(fc == 0), stop=(fc == 15))
                        stat_mm_tok[fc] = s_t.inc(mm)
                if weave:
                    weave(1)
            tstat = s_t.n
            tvar = s_v.n + 1
            trr = s_v.n + 2

            @blk.scalar
            def _(sc):
                sc.wait_ge(s_v.h, src_tok)
                for g4_ in range(4):
                    if g4_ >= 1:
                        sc.wait_ge(s_t.h, stat_mm_tok[4 * g4_ - 1])
                    for fc in range(4 * g4_, 4 * g4_ + 4):
                        ii = sc.activation(tmpB[:, (fc % 4) * 512:(fc % 4 + 1) * 512],
                                           hintsl(fc), AF.Square)
                        assert s_sc.inc(ii) == sq_tok[fc]
                sc.wait_ge(s_v.h, tvar)
                ii = sc.activation(tmpB[:, 512:1024], tmpB[:, 0:512], AF.Ln,
                                   bias=np.float32(EPS))
                s_sc.inc(ii)
                sc.wait_ge(s_v.h, trr)
                ii = sc.activation(tmpA[:, 512:1024], tmpB[:, 0:512], AF.Exp)
                s_sc.inc(ii)
            tln = s_sc.n - 1
            trstd = s_sc.n

            @blk.vector
            def _(v):
                v.wait_ge(s_t.h, tstat)
                v.tensor_scalar_mul(tmpA[:, 0:512], ps_s[:, 0:512], float(1.0 / D))
                v.tensor_tensor(tmpA[:, 1024:1536], tmpA[:, 0:512],
                                tmpA[:, 0:512], ALU.mult)
                iv = v.scalar_tensor_tensor(tmpB[:, 0:512], ps_s[:, 512:1024],
                                            float(1.0 / D), tmpA[:, 1024:1536],
                                            ALU.mult, ALU.subtract)
                s_v.inc(iv)
            assert s_v.n == tvar, (s_v.n, tvar)
            stats_wait_v[0] = tvar

            @blk.vector
            def _(v):
                v.wait_ge(s_sc.h, tln)
                iv = v.tensor_scalar_mul(tmpB[:, 0:512], tmpB[:, 512:1024],
                                         float(-0.5))
                s_v.inc(iv)
            assert s_v.n == trr, (s_v.n, trr)

            app_tok = {}
            for fc in range(16):
                @blk.vector
                def _(v, fc=fc):
                    if fc == 0:
                        v.wait_ge(s_sc.h, trstd)
                    pb = (fc % 2) * 512
                    pa = 1024 + (fc % 2) * 512
                    v.tensor_sub(tmpB[:, pb:pb + 512], hintsl(fc), tmpA[:, 0:512])
                    v.tensor_tensor(tmpB[:, pa:pa + 512], tmpB[:, pb:pb + 512],
                                    tmpA[:, 512:1024], ALU.mult)
                    iv = v.tensor_scalar(
                        hbuf[:, fc * 1024 + tb * 512:fc * 1024 + (tb + 1) * 512],
                        tmpB[:, pa:pa + 512],
                        params[:, gcol + fc:gcol + fc + 1],
                        params[:, bcol + fc:bcol + fc + 1],
                        ALU.mult, ALU.add)
                    app_tok[fc] = s_v.inc(iv)
                if weave and fc % 2 == 1:
                    weave(1)
            return app_tok[15]

        stats_wait_v = [0]
        BLOCKS = [(0, 0), (0, 1), (1, 0), (1, 1)]
        h_ready = {0: h_done, 1: h_done}

        def make_eprod(bi):
            l, tb = BLOCKS[bi]
            closures = []
            for ec in range(64):
                def emit_ec(ec=ec, l=l, tb=tb, bi=bi):
                    i = ne_used[0]
                    ffn_order.append(("we", i, l, ec))

                    @blk.tensor
                    def _(t, l=l, tb=tb, ec=ec, i=i, bi=bi):
                        t.wait_ge(s_we[i % 3].h, 16 * (i // 3 + 1))
                        if ec == 0:
                            t.wait_ge(s_v.h, h_ready[bi])
                        if i >= 2:
                            t.wait_ge(s_v.h, e_evac_tok[i - 2])
                        pe = pse[ec % 2]
                        for fc in range(16):
                            mm = t.matmul(pe,
                                          ewr[i % 3][:, fc * 128:(fc + 1) * 128],
                                          hbuf[:, fc * 1024 + tb * 512:
                                               fc * 1024 + (tb + 1) * 512],
                                          start=(fc == 0), stop=(fc == 15))
                        s_t.inc(mm)
                    te = s_t.n
                    free_tok["we"].append(("t", te))

                    @blk.vector
                    def _(v, l=l, ec=ec, i=i, te=te):
                        v.wait_ge(s_t.h, te)
                        if ec == 0 and prev_ent_last[0]:
                            v.wait_ge(s_t.h, prev_ent_last[0])
                        ii = v.tensor_scalar_add(etile(ec), pse[ec % 2],
                                                 params[:, 32 + 64 * l + ec:
                                                        33 + 64 * l + ec])
                        e_evac_tok[i] = s_v.inc(ii)
                    ne_used[0] += 1
                closures.append(emit_ec)
            return closures

        pending_ep = [None]

        def weave(n):
            c = pending_ep[0]
            if not c:
                return
            k = 0
            while c and k < n:
                c.pop(0)()
                k += 1

        for bi, (l, tb) in enumerate(BLOCKS):
            if bi == 0:
                for fn in make_eprod(0):
                    fn()
            else:
                weave(64)
            e_done = s_v.n
            if bi + 1 < 4:
                pending_ep[0] = make_eprod(bi + 1)

            ent_evac = {}
            for fc in range(16):
                for hf in range(4):
                    i = npw_used[0]
                    ffn_order.append(("wp", i, l, fc, hf))

                    @blk.tensor
                    def _(t, fc=fc, hf=hf, i=i, e_done=e_done):
                        t.wait_ge(s_wp[i % 2].h, 16 * (i // 2 + 1))
                        if fc == 0 and hf == 0:
                            t.wait_ge(s_v.h, e_done)
                        if fc >= 2 and hf == 0:
                            t.wait_ge(s_v.h, ent_evac[fc - 2])
                        pp = psent[fc % 2]
                        for k in range(16):
                            ec = hf * 16 + k
                            mm = t.matmul(pp,
                                          pwr[i % 2][:, k * 128:(k + 1) * 128],
                                          etile(ec),
                                          start=(ec == 0), stop=(ec == 63))
                        s_t.inc(mm)
                    free_tok["wp"].append(("t", s_t.n))
                    npw_used[0] += 1
                tent = s_t.n

                @blk.vector
                def _(v, l=l, tb=tb, fc=fc, tent=tent):
                    v.wait_ge(s_t.h, tent)
                    iv = v.scalar_tensor_tensor(
                        hintsl(fc), psent[fc % 2],
                        params[:, 160 + 16 * l + fc:161 + 16 * l + fc],
                        hbuf[:, fc * 1024 + tb * 512:fc * 1024 + (tb + 1) * 512],
                        ALU.add, ALU.add)
                    ent_evac[fc] = s_v.inc(iv)
            prev_ent_last[0] = s_t.n
            hent_done = s_v.n

            h3_done = emit_fm_ln(320 + 32 * l, 336 + 32 * l, hent_done, tb,
                                 stats_wait_v, weave)

            psq0 = [ps_s[:, 0:512], ps_y[:, 0:512]]
            psq1 = [ps_s[:, 512:1024], ps_y[:, 512:1024]]
            path_evac = {}
            for fo in range(16):
                for p in range(2):
                    i = nh_used[0]
                    ffn_order.append(("wh", i, l, p, fo))

                    @blk.tensor
                    def _(t, fo=fo, p=p, i=i, h3_done=h3_done):
                        t.wait_ge(s_wh[i % 3].h, 16 * (i // 3 + 1))
                        if fo == 0 and p == 0:
                            t.wait_ge(s_v.h, h3_done)
                            t.wait_ge(s_v.h, stats_wait_v[0])
                        if fo >= 2 and p == 0:
                            t.wait_ge(s_v.h, path_evac[fo - 2])
                        pp = (psq0 if p == 0 else psq1)[fo % 2]
                        for dc in range(16):
                            mm = t.matmul(pp,
                                          par[i % 3][:, dc * 128:(dc + 1) * 128],
                                          hbuf[:, dc * 1024 + tb * 512:
                                               dc * 1024 + (tb + 1) * 512],
                                          start=(dc == 0), stop=(dc == 15))
                        s_t.inc(mm)
                    free_tok["wh"].append(("t", s_t.n))
                    nh_used[0] += 1
                tpa = s_t.n
                if fo % 2 == 1:
                    weave(2)

                @blk.vector
                def _(v, l=l, tb=tb, fo=fo, tpa=tpa):
                    v.wait_ge(s_t.h, tpa)
                    m0 = params[:, 512 + 32 * l + fo:513 + 32 * l + fo]
                    m1 = params[:, 528 + 32 * l + fo:529 + 32 * l + fo]
                    bc = params[:, 576 + 16 * l + fo:577 + 16 * l + fo]
                    v.scalar_tensor_tensor(tmpA[:, 1024:1536], psq0[fo % 2], m0,
                                           hbuf[:, fo * 1024 + tb * 512:
                                                fo * 1024 + (tb + 1) * 512],
                                           ALU.mult, ALU.add)
                    v.scalar_tensor_tensor(tmpA[:, 1536:2048], psq1[fo % 2], m1,
                                           tmpA[:, 1024:1536], ALU.mult, ALU.add)
                    iv = v.tensor_scalar_add(hintsl(fo),
                                             tmpA[:, 1536:2048], bc)
                    path_evac[fo] = s_v.inc(iv)
            interf_done = s_v.n

            h_ready[bi + 2] = emit_fm_ln(384 + 32 * l, 400 + 32 * l,
                                         interf_done, tb, stats_wait_v, weave)
            prev_block_apply[0] = h_ready[bi + 2]
        ffn_done = prev_block_apply[0]

        reload_tok = []
        add_tok = []
        outdma_tok = []
        for tch in range(8):
            @blk.scalar
            def _(sc, tch=tch):
                if tch < 2:
                    sc.wait_ge(s_v.h, ffn_done)
                else:
                    sc.wait_ge(s_v.h, add_tok[tch - 2])
                reload_tok.append((tch % 2,
                                   s_x[tch % 2].inc(sc.dma_start(out=xst[tch % 2][:],
                                                                 in_=dx2[tch]), 16)))

            @blk.tensor
            def _(t, tch=tch):
                if tch == 0:
                    t.wait_ge(s_v.h, ffn_done)
                if psT_free[0]:
                    t.wait_ge(s_v.h, psT_free[0])
                for k in range(8):
                    tr = t.transpose(psT[:, k * 128:(k + 1) * 128],
                                     hbuf[:, k * 1024 + tch * 128:
                                          k * 1024 + (tch + 1) * 128], ident[:])
                s_t.inc(tr)
            th1 = s_t.n

            @blk.vector
            def _(v, tch=tch, th1=th1):
                v.wait_ge(s_t.h, th1)
                o = view3(lnout, 128)[:, 0:8, :]
                iv = v.tensor_copy(o, view3(psT, 128))
                psT_free[0] = s_v.inc(iv)
            tev1 = psT_free[0]

            @blk.tensor
            def _(t, tch=tch, tev1=tev1):
                t.wait_ge(s_v.h, tev1)
                for k in range(8):
                    tr = t.transpose(psT[:, k * 128:(k + 1) * 128],
                                     hbuf[:, (k + 8) * 1024 + tch * 128:
                                          (k + 8) * 1024 + (tch + 1) * 128],
                                     ident[:])
                s_t.inc(tr)
            th2 = s_t.n

            @blk.vector
            def _(v, tch=tch, th2=th2):
                v.wait_ge(s_t.h, th2)
                o = view3(lnout, 128)[:, 8:16, :]
                iv = v.tensor_copy(o, view3(psT, 128))
                psT_free[0] = s_v.inc(iv)
                v.wait_ge(s_x[reload_tok[tch][0]].h, reload_tok[tch][1])
                if tch >= 2:
                    v.wait_ge(s_o[outdma_tok[tch - 2][0]].h,
                              outdma_tok[tch - 2][1])
                dst = tmpA if tch % 2 == 0 else tmpB
                iv = v.tensor_add(dst[:], xst[tch % 2][:], lnout[:])
                add_tok.append(s_v.inc(iv))

            @blk.sync
            def _(sync, tch=tch):
                sync.wait_ge(s_v.h, add_tok[tch])
                src = tmpA if tch % 2 == 0 else tmpB
                outdma_tok.append((tch % 2,
                                   s_o[tch % 2].inc(sync.dma_start(out=dout[tch],
                                                                   in_=src[:]), 16)))

        @blk.sync
        def _(sync):
            sync.wait_ge(s_o[0].h, s_o[0].n)
            sync.wait_ge(s_o[1].h, s_o[1].n)

        sem_of = {"v": s_v, "t": s_t, "sc": s_sc}

        @blk.gpsimd
        def _(g):
            for h in range(16):
                if h >= 2:
                    kind, tok = free_tok["wq"][h - 2]
                    g.wait_ge(sem_of[kind].h, tok)
                s_wq[h % 2].inc(g.dma_start(
                    out=view3(wqr[h % 2], 128),
                    in_=dwq[h].rearrange("(k p) e -> p k e", p=128)), 16)
            g.wait_ge(s_v.h, kv_done)
            for i in range(16):
                if i >= 2:
                    kind, tok = free_tok["wo"][i - 2]
                    g.wait_ge(sem_of[kind].h, tok)
                s_wo[i % 2].inc(g.dma_start(
                    out=view3(wor[i % 2], 128),
                    in_=dwo[i].rearrange("(k p) e -> p k e", p=128)), 16)
            g.wait_ge(s_v.h, ao_done)
            for i in range(32):
                src = dwamp[i // 2] if i % 2 == 0 else dwph[i // 2]
                if i >= 2:
                    kind, tok = free_tok["wa"][i - 2]
                    g.wait_ge(sem_of[kind].h, tok)
                s_wa[i % 2].inc(g.dma_start(
                    out=view3(war[i % 2], 128),
                    in_=src.rearrange("(k p) e -> p k e", p=128)), 16)
            ring = {"we": 3, "wp": 2, "wh": 3}
            sems = {"we": s_we, "wp": s_wp, "wh": s_wh}
            for rec in ffn_order:
                kind, i = rec[0], rec[1]
                if i >= ring[kind]:
                    fk, tok = free_tok[kind][i - ring[kind]]
                    g.wait_ge(sem_of[fk].h, tok)
                if kind == "we":
                    _, _, l, ec = rec
                    src = dwe[l, ec]
                    dst = view3(ewr[i % 3], 128)
                elif kind == "wp":
                    _, _, l, fc, hf = rec
                    src = dwpw[l, fc, hf * 2048:(hf + 1) * 2048]
                    dst = view3(pwr[i % 2], 128)
                else:
                    _, _, l, p, fo = rec
                    src = dwpa[l, p, fo]
                    dst = view3(par[i % 3], 128)
                sems[kind][i % ring[kind]].inc(g.dma_start(
                    out=dst, in_=src.rearrange("(k p) e -> p k e", p=128)), 16)

    return nc


_NC_CACHE = {}


def _get_nc():
    if "nc" not in _NC_CACHE:
        _NC_CACHE["nc"] = build_program()
    return _NC_CACHE["nc"]


def kernel(L0_attn_x, L0_attn_Wq, L0_attn_Wk, L0_attn_Wv, L0_attn_Wo,
           L0_ffn_amp_w, L0_ffn_amp_b, L0_ffn_ph_w, L0_ffn_ph_b,
           L0_ffn_ent_w, L0_ffn_ent_b, L0_ffn_ent_pw, L0_ffn_ent_pb,
           L0_ffn_path_w, L0_ffn_path_b, L0_ffn_phase,
           L0_ffn_ln3_g, L0_ffn_ln3_b, L0_ffn_ln4_g, L0_ffn_ln4_b):
    x = np.ascontiguousarray(np.asarray(L0_attn_x, dtype=np.float32))

    def stage_cols(w, nb):
        w = np.asarray(w, dtype=np.float32)
        return np.ascontiguousarray(
            w.reshape(w.shape[0], nb, 128).transpose(1, 0, 2))

    wq_s = stage_cols(L0_attn_Wq, 16)
    wo_s = stage_cols(L0_attn_Wo, 16)
    ampw_s = stage_cols(L0_ffn_amp_w, 16)
    phw_s = stage_cols(L0_ffn_ph_w, 16)
    entw_s = np.stack([stage_cols(np.asarray(L0_ffn_ent_w)[l], 64)
                       for l in range(2)])
    pw_s = np.stack([stage_cols(np.asarray(L0_ffn_ent_pw)[l], 16)
                     for l in range(2)])
    path_s = np.stack([np.stack([stage_cols(np.asarray(L0_ffn_path_w)[l, p], 16)
                                 for p in range(2)]) for l in range(2)])
    wk = np.ascontiguousarray(np.asarray(L0_attn_Wk, dtype=np.float32))
    wv = np.ascontiguousarray(np.asarray(L0_attn_Wv, dtype=np.float32))

    pv = np.zeros((512, 128), np.float32)

    def put(vec, c0):
        v = np.asarray(vec, dtype=np.float32).reshape(-1, 128)
        pv[c0:c0 + v.shape[0]] = v

    put(L0_ffn_amp_b, 0)
    put(L0_ffn_ph_b, 16)
    put(np.asarray(L0_ffn_ent_b)[0], 32)
    put(np.asarray(L0_ffn_ent_b)[1], 96)
    put(np.asarray(L0_ffn_ent_pb)[0], 160)
    put(np.asarray(L0_ffn_ent_pb)[1], 176)
    put(np.asarray(L0_ffn_path_b)[0, 0], 192)
    put(np.asarray(L0_ffn_path_b)[0, 1], 208)
    put(np.asarray(L0_ffn_path_b)[1, 0], 224)
    put(np.asarray(L0_ffn_path_b)[1, 1], 240)
    put(np.asarray(L0_ffn_phase)[0, 0], 256)
    put(np.asarray(L0_ffn_phase)[0, 1], 272)
    put(np.asarray(L0_ffn_phase)[1, 0], 288)
    put(np.asarray(L0_ffn_phase)[1, 1], 304)
    put(np.asarray(L0_ffn_ln3_g)[0], 320)
    put(np.asarray(L0_ffn_ln3_b)[0], 336)
    put(np.asarray(L0_ffn_ln3_g)[1], 352)
    put(np.asarray(L0_ffn_ln3_b)[1], 368)
    put(np.asarray(L0_ffn_ln4_g)[0], 384)
    put(np.asarray(L0_ffn_ln4_b)[0], 400)
    put(np.asarray(L0_ffn_ln4_g)[1], 416)
    put(np.asarray(L0_ffn_ln4_b)[1], 432)
    pvec = np.ascontiguousarray(pv.reshape(4, 128, 128))

    nc = _get_nc()
    in_maps = []
    for c in range(8):
        b, par_ = c // 2, c % 2
        gbl = GBLK[par_]
        rows = np.concatenate([np.arange(gb * 128, (gb + 1) * 128) for gb in gbl])
        x_own = np.ascontiguousarray(x[b][rows]).reshape(8, 128, D)
        x_seq = np.ascontiguousarray(x[b]).reshape(16, 128, D)
        qpos = np.empty((128, 8), np.float32)
        qrow = np.empty((128, 1024), np.float32)
        for j, gb in enumerate(gbl):
            qpos[:, j] = gb * 128 + np.arange(128)
            qrow[:, j * 128:(j + 1) * 128] = gb * 128 + np.arange(128)[None, :]
        in_maps.append({
            "x_own": x_own, "x_seq": x_seq, "qpos": qpos, "qposrow": qrow,
            "pvec": pvec, "wq_s": wq_s, "wk": wk, "wv": wv, "wo_s": wo_s,
            "ampw_s": ampw_s, "phw_s": phw_s, "entw_s": entw_s,
            "pw_s": pw_s, "path_s": path_s,
        })

    res = run_bass_kernel_spmd(nc, in_maps, core_ids=list(range(8)))

    out = np.empty((4, 2048, D), np.float32)
    for c in range(8):
        b, par_ = c // 2, c % 2
        gbl = GBLK[par_]
        o = np.asarray(res.results[c]["out"]).reshape(1024, D)
        for j, gb in enumerate(gbl):
            out[b, gb * 128:(gb + 1) * 128] = o[j * 128:(j + 1) * 128]
    return out


# version 14
# speedup vs baseline: 1.0336x; 1.0041x over previous
import sys
import os

for _p in ("/opt/trn_rl_repo",):
    if _p not in sys.path and os.path.isdir(_p):
        sys.path.insert(0, _p)

import numpy as np
import concourse.bass as bass
import concourse.mybir as mybir
from concourse.bass_utils import run_bass_kernel_spmd
from contextlib import ExitStack

F32 = mybir.dt.float32
BF16 = mybir.dt.bfloat16
I32 = mybir.dt.int32
AF = mybir.ActivationFunctionType
ALU = mybir.AluOpType
AX = mybir.AxisListType

D = 2048
NF = 16
EPS = 1e-5
PI = float(np.pi)
TWO_PI = float(2 * np.pi)
E_SLOT = [2, 4, 6, 8, 10, 12, 14, 16]
GBLK = {0: [0, 3, 4, 7, 8, 11, 12, 15],
        1: [1, 2, 5, 6, 9, 10, 13, 14]}
QSCALE = float(1.0 / np.sqrt(128.0))
f32 = np.float32


class Sem:
    def __init__(self, nc, ctx, name):
        self.h = ctx.enter_context(nc.semaphore(name))
        self.n = 0

    def inc(self, instr, k=1):
        instr.then_inc(self.h, k)
        self.n += k
        return self.n


class SbufMap:
    def __init__(self, nc, base, top):
        self.nc = nc
        self.base = base
        self.top = top
        self.placed = []

    def alloc(self, name, cols, dtype, phases):
        size = cols * mybir.dt.size(dtype)
        ph = set(phases)
        placed = sorted([p for p in self.placed if p[3] & ph], key=lambda p: p[1])
        cands = sorted(set([self.base] + [o + s for (_, o, s, _) in placed]))
        off = None
        for c in cands:
            c = (c + 63) & ~63
            if c + size > self.top:
                continue
            if all(c + size <= o or c >= o + s for (_, o, s, _) in placed):
                off = c
                break
        if off is None:
            lines = [f"{n}@{o}+{sz}" for (n, o, sz, p) in placed]
            raise AssertionError(
                f"SBUF alloc failed: {name} ({size}B) phases={sorted(ph)}\n"
                + "\n".join(lines))
        self.placed.append((name, off, size, ph))
        return self.nc.alloc_sbuf_tensor_at(name, [128, cols], dtype, offset=off)


def view3(ap_tensor, inner):
    return ap_tensor[:].rearrange("p (f c) -> p f c", c=inner)


def build_program():
    nc = bass.Bass("TRN2", target_bir_lowering=False, debug=False)

    dx_own = nc.dram_tensor("x_own", [8, 128, D], F32, kind="ExternalInput").ap()
    dx_seq = nc.dram_tensor("x_seq", [16, 128, D], F32, kind="ExternalInput").ap()
    dqpos = nc.dram_tensor("qpos", [128, 8], F32, kind="ExternalInput").ap()
    dqrow = nc.dram_tensor("qposrow", [128, 1024], F32, kind="ExternalInput").ap()
    dpvec = nc.dram_tensor("pvec", [4, 128, 128], F32, kind="ExternalInput").ap()
    dwq = nc.dram_tensor("wq_s", [16, 2048, 128], F32, kind="ExternalInput").ap()
    dwk = nc.dram_tensor("wk", [2048, 512], F32, kind="ExternalInput").ap()
    dwv = nc.dram_tensor("wv", [2048, 512], F32, kind="ExternalInput").ap()
    dwo = nc.dram_tensor("wo_s", [16, 2048, 128], F32, kind="ExternalInput").ap()
    dwamp = nc.dram_tensor("ampw_s", [16, 2048, 128], F32, kind="ExternalInput").ap()
    dwph = nc.dram_tensor("phw_s", [16, 2048, 128], F32, kind="ExternalInput").ap()
    dwe = nc.dram_tensor("entw_s", [2, 64, 2048, 128], F32, kind="ExternalInput").ap()
    dwpw = nc.dram_tensor("pw_s", [2, 16, 8192, 128], F32, kind="ExternalInput").ap()
    dwpa = nc.dram_tensor("path_s", [2, 2, 16, 2048, 128], F32,
                          kind="ExternalInput").ap()
    dout = nc.dram_tensor("out", [8, 128, D], F32, kind="ExternalOutput").ap()
    dx2 = nc.dram_tensor("dbg_x2", [8, 128, D], F32, kind="ExternalOutput").ap()

    with ExitStack() as ctx:
        ps_s = ctx.enter_context(nc.psum_tensor("ps_s", [128, 2048], F32))
        ps_y = ctx.enter_context(nc.psum_tensor("ps_y", [128, 1024], F32))
        ps_p = ctx.enter_context(nc.psum_tensor("ps_p", [128, 512], F32))
        psT = ctx.enter_context(nc.psum_tensor("psT", [128, 1024], BF16))

        M = SbufMap(nc, base=35840, top=229340)
        ALL = list(range(12))
        params = M.alloc("params", 1024, F32, ALL)
        statsb = M.alloc("statsb", 128, F32, ALL)
        ident = M.alloc("ident", 128, BF16, ALL)
        ident32 = M.alloc("ident32", 128, F32, [0])
        ones32 = M.alloc("ones32", 128, F32, ALL)
        tmpA = M.alloc("tmpA", 2048, F32, [0, 1, 2, 3, 5, 6, 7, 8, 9, 10, 11])
        tmpB = M.alloc("tmpB", 2048, F32, [0, 1, 2, 3, 5, 6, 7, 8, 9, 10, 11])
        lnout2_bufs = [M.alloc("lnout", 2048, BF16, [1, 2, 3, 5, 11]),
                       M.alloc("lnoutB", 2048, BF16, [1, 2, 3, 5, 11])]
        lnout = lnout2_bufs[0]
        dumpb = M.alloc("dumpb", 2048, BF16, [1, 2, 3, 5])
        xst = [M.alloc("xstage0", 2048, F32, [0, 1, 2, 3, 5, 11]),
               M.alloc("xstage1", 2048, F32, [0, 1, 2, 3, 5, 11])]
        s_iota_i = M.alloc("s_iota_i", 2048, I32, [0])
        s_iota = M.alloc("s_iota", 2048, F32, list(range(5)))
        qposrow = M.alloc("qposrow", 1024, F32, [0])
        qpos = M.alloc("qpos", 8, F32, list(range(5)))
        ii0 = M.alloc("ii0", 128, I32, [0])
        ii1 = M.alloc("ii1", 128, I32, [0])
        pvst = [M.alloc("pvst0", 128, F32, [0]), M.alloc("pvst1", 128, F32, [0])]
        tbs = [M.alloc("tbs0", 2048, F32, [0]), M.alloc("tbs1", 2048, F32, [0])]
        qsin = M.alloc("qsin", 1024, F32, [0, 1, 2])
        qcos = M.alloc("qcos", 1024, F32, [0, 1, 2])
        ksin = M.alloc("ksin", 2048, F32, [0, 1, 2, 3])
        kcos = M.alloc("kcos", 2048, F32, [0, 1, 2, 3])
        ln1ownT_p = [M.alloc(f"ln1ownT{i}", 8192, BF16, [1, 2]) for i in range(2)]
        wqr = [M.alloc("wqr0", 2048, BF16, [0, 1, 2]), M.alloc("wqr1", 2048, BF16, [0, 1, 2])]
        wk_sb = M.alloc("wk_sb", 8192, BF16, [0, 1, 2, 3])
        wv_sb = M.alloc("wv_sb", 8192, BF16, [0, 1, 2, 3])
        lnT = M.alloc("lnT", 8192, BF16, [3])
        Q_p = [M.alloc(f"Q{i}", 4096, BF16, [2, 3, 4]) for i in range(4)]
        K_p = [M.alloc(f"K{i}", 2048, BF16, [3, 4]) for i in range(4)]
        V_p = [M.alloc(f"V{i}", 2048, BF16, [3, 4]) for i in range(4)]
        masks = M.alloc("masks", 4096, BF16, [4])
        PbE = [M.alloc(f"PbE{i}", 2048, BF16, [4]) for i in range(4)]
        PbM = M.alloc("PbM", 2048, BF16, [4])
        PT_ap = [M.alloc(f"PTa{i}", 4096, BF16, [4]) for i in range(2)]
        PT_bp = [M.alloc(f"PTb{i}", 4096, BF16, [4]) for i in range(2)]
        y_p = [M.alloc(f"y{i}", 4096, BF16, [4, 5]) for i in range(4)]
        wor = [M.alloc("wor0", 2048, BF16, [4, 5]),
               M.alloc("wor1", 2048, BF16, [4, 5])]
        ao_tm_p = [M.alloc(f"ao_tm{i}", 8192, BF16, [5]) for i in range(2)]
        ao_fm = M.alloc("ao_fm", 1024, BF16, [5])
        ln2xT_p = [M.alloc(f"ln2xT{i}", 4096, BF16, [5, 6]) for i in range(4)]
        war = [M.alloc("war0", 2048, BF16, [5, 6]), M.alloc("war1", 2048, BF16, [5, 6])]
        hbuf = M.alloc("hbuf", 16384, BF16, list(range(6, 12)))
        ewr = [M.alloc(f"ewr{i}", 2048, BF16, list(range(6, 11))) for i in range(3)]
        pwr = [M.alloc(f"pwr{i}", 2048, BF16, list(range(7, 11))) for i in range(2)]
        par = [M.alloc(f"par{i}", 2048, BF16, list(range(7, 11))) for i in range(3)]
        ebuf = [M.alloc(f"ebuf{i}", 8192, BF16, list(range(7, 11)))
                for i in range(4)]
        hint_p = [M.alloc(f"hint{i}", 2048, F32, list(range(7, 11)))
                  for i in range(4)]
        peak = max(o + s for (_, o, s, _) in M.placed)
        assert peak <= 229340, f"SBUF peak {peak}"

        def ln1ownT(fc, c0, c1):
            t = ln1ownT_p[fc // 8]
            base = (fc % 8) * 1024
            return t[:, base + c0:base + c1]

        def Qsl(qh, c0, c1):
            t = Q_p[qh // 4]
            base = (qh % 4) * 1024
            return t[:, base + c0:base + c1]

        def Ksl(g, c0, c1):
            return K_p[g][:, c0:c1]

        def Vsl(sc):
            return V_p[sc // 4][:, (sc % 4) * 512:(sc % 4 + 1) * 512]

        def Vtile(sc, g):
            return V_p[sc // 4][:, (sc % 4) * 512 + g * 128:
                                (sc % 4) * 512 + (g + 1) * 128]

        def PTsl(which, sc_piece):
            return (PT_ap if which == 0 else PT_bp)[sc_piece]

        def ysl(qh_group):
            return y_p[qh_group]

        def hintsl(fc):
            return hint_p[fc // 4][:, (fc % 4) * 512:(fc % 4 + 1) * 512]

        def etile(ec):
            t = ebuf[ec // 16]
            c = (ec % 16) * 512
            return t[:, c:c + 512]

        s_x = [Sem(nc, ctx, f"s_x{k}") for k in range(2)]
        s_qp = Sem(nc, ctx, "s_qp")
        s_sp = [Sem(nc, ctx, f"s_sp{k}") for k in range(2)]
        s_o = [Sem(nc, ctx, f"s_o{k}") for k in range(2)]
        s_g = Sem(nc, ctx, "s_g")
        s_wm = [Sem(nc, ctx, f"s_wm{k}") for k in range(2)]
        s_kv = Sem(nc, ctx, "s_kv")
        s_t = Sem(nc, ctx, "s_t")
        s_v = Sem(nc, ctx, "s_v")
        s_sc = Sem(nc, ctx, "s_sc")
        s_wq = [Sem(nc, ctx, f"s_wq{k}") for k in range(2)]
        s_wo = [Sem(nc, ctx, f"s_wo{k}") for k in range(2)]
        s_wa = [Sem(nc, ctx, f"s_wa{k}") for k in range(2)]
        s_we = [Sem(nc, ctx, f"s_we{k}") for k in range(3)]
        s_wp = [Sem(nc, ctx, f"s_wp{k}") for k in range(2)]
        s_wh = [Sem(nc, ctx, f"s_wh{k}") for k in range(3)]

        free_tok = {"wq": [], "wo": [], "wa": [], "we": [], "wp": [], "wh": []}
        ffn_order = []

        blk = ctx.enter_context(nc.Block())

        @blk.gpsimd
        def _(g):
            g.iota(s_iota_i[:], pattern=[[1, 2048]], base=0, channel_multiplier=0)
            g.iota(ii0[:], pattern=[[0, 128]], base=0, channel_multiplier=1)
            last = g.iota(ii1[:], pattern=[[1, 128]], base=0, channel_multiplier=0)
            s_g.inc(last)
            for i in range(4):
                if i >= 2:
                    g.wait_ge(s_t.h, i - 1)
                s_wm[i % 2].inc(g.dma_start(out=pvst[i % 2][:], in_=dpvec[i]), 16)
            s_kv.inc(g.dma_start(
                out=view3(wk_sb, 512), in_=dwk.rearrange("(k p) e -> p k e", p=128)), 16)
            s_kv.inc(g.dma_start(
                out=view3(wv_sb, 512), in_=dwv.rearrange("(k p) e -> p k e", p=128)), 16)
        tok_g = s_g.n

        @blk.vector
        def _(v):
            v.wait_ge(s_g.h, tok_g)
            v.tensor_copy(s_iota[:], s_iota_i[:])
            v.tensor_tensor(ident[:], ii0[:], ii1[:], ALU.is_equal)
            v.tensor_tensor(ident32[:], ii0[:], ii1[:], ALU.is_equal)
            v.memset(ones32[:], 1.0)
            i0 = v.tensor_copy(statsb[:, 0:1], ii0[:, 0:1])
            tw = s_v.inc(i0)
            v.wait_ge(s_v.h, tw)
            iv = v.tensor_scalar_mul(statsb[:, 1:2], statsb[:, 0:1],
                                     float(-np.log(10000.0) / 64.0))
            s_v.inc(iv)
        tok_v0 = s_v.n

        @blk.scalar
        def _(sc):
            sc.wait_ge(s_v.h, tok_v0)
            ii = sc.activation(statsb[:, 2:3], statsb[:, 1:2], AF.Exp)
            s_sc.inc(ii)
        tok_if = s_sc.n

        pv_ev = []
        for i in range(4):
            @blk.tensor
            def _(t, i=i):
                t.wait_ge(s_wm[i % 2].h, 16 * (i // 2 + 1))
                if i == 0:
                    t.wait_ge(s_v.h, tok_v0)
                if i >= 1:
                    t.wait_ge(s_v.h, pv_ev[i - 1])
                tr = t.transpose(ps_p[:, 0:128], pvst[i % 2][:], ident32[:])
                s_t.inc(tr)
            tt = s_t.n

            @blk.vector
            def _(v, i=i, tt=tt):
                v.wait_ge(s_t.h, tt)
                iv = v.tensor_copy(params[:, i * 128:(i + 1) * 128], ps_p[:, 0:128])
                pv_ev.append(s_v.inc(iv))
        tok_params = pv_ev[-1]

        def table_pipeline(src_rows64_ap, width, dst_sin, dst_cos, scale):
            for (dst, shift) in ((dst_sin, 0.0), (dst_cos, PI / 2)):
                sA = tmpA[0:64, 0:width]
                sB = tmpB[0:64, 0:width]
                sC = tbs[0][0:64, 0:width]
                sD = tbs[1][0:64, 0:width]
                i32s = s_iota_i[0:64, 0:width]

                @blk.vector
                def _(v, dst=dst, shift=shift, sA=sA, sB=sB, sC=sC, sD=sD, i32s=i32s):
                    v.wait_ge(s_sc.h, tok_if)
                    v.tensor_scalar_mul(sA, src_rows64_ap, statsb[0:64, 2:3])
                    v.tensor_scalar(sB, sA, float(1.0), float(shift + 16 * PI),
                                    ALU.mult, ALU.add)
                    v.tensor_scalar_mul(sC, sB, float(1.0 / TWO_PI))
                    v.tensor_copy(i32s, sC)
                    v.tensor_copy(sC, i32s)
                    v.scalar_tensor_tensor(sD, sC, float(-TWO_PI), sB, ALU.mult, ALU.add)
                    v.tensor_single_scalar(sC, sD, float(PI), ALU.is_gt)
                    iv = v.scalar_tensor_tensor(sB, sC, float(-TWO_PI), sD,
                                                ALU.mult, ALU.add)
                    s_v.inc(iv)
                tv = s_v.n

                @blk.scalar
                def _(sc, sA=sA, sB=sB, tv=tv):
                    sc.wait_ge(s_v.h, tv)
                    ii = sc.activation(sA, sB, AF.Sin)
                    s_sc.inc(ii)
                ts_ = s_sc.n

                @blk.vector
                def _(v, dst=dst, sA=sA, ts_=ts_):
                    v.wait_ge(s_sc.h, ts_)
                    v.tensor_scalar_mul(dst[0:64, 0:width], sA, float(scale))
                    iv = v.tensor_copy(dst[64:128, 0:width], dst[0:64, 0:width])
                    s_v.inc(iv)

        table_pipeline(s_iota[0:64, :], 2048, ksin, kcos, 1.0)
        @blk.sync
        def _(sync):
            s_qp.inc(sync.dma_start(out=qposrow[:], in_=dqrow), 16)
            s_qp.inc(sync.dma_start(out=qpos[:], in_=dqpos), 16)
        tok_qr = s_qp.n

        @blk.vector
        def _(v):
            v.wait_ge(s_qp.h, tok_qr)
        table_pipeline(qposrow[0:64, :], 1024, qsin, qcos, QSCALE)

        @blk.vector
        def _(v):
            v.wait_ge(s_v.h, tok_params)
            ph_ap = params[:, 256:320]
            sA = tmpA[:, 0:64]
            sB = tmpB[:, 0:64]
            sC = tmpA[:, 64:128]
            sD = tmpB[:, 64:128]
            i32s = s_iota_i[:, 0:64]
            v.tensor_scalar(sB, ph_ap, float(1.0), float(PI / 4 + 16 * PI),
                            ALU.mult, ALU.add)
            v.tensor_scalar_mul(sC, sB, float(1.0 / TWO_PI))
            v.tensor_copy(i32s, sC)
            v.tensor_copy(sC, i32s)
            v.scalar_tensor_tensor(sD, sC, float(-TWO_PI), sB, ALU.mult, ALU.add)
            v.tensor_single_scalar(sC, sD, float(PI), ALU.is_gt)
            iv = v.scalar_tensor_tensor(sB, sC, float(-TWO_PI), sD, ALU.mult, ALU.add)
            s_v.inc(iv)
        tvm = s_v.n

        @blk.scalar
        def _(sc):
            sc.wait_ge(s_v.h, tvm)
            ii = sc.activation(tmpA[:, 0:64], tmpB[:, 0:64], AF.Sin)
            s_sc.inc(ii)
        tsm = s_sc.n

        @blk.vector
        def _(v):
            v.wait_ge(s_sc.h, tsm)
            im = v.tensor_scalar_mul(params[:, 512:576], tmpA[:, 0:64],
                                     float(np.sqrt(2.0)))
            tw = s_v.inc(im)
            v.wait_ge(s_v.h, tw)
            for l in range(2):
                pb0 = params[:, 192 + 32 * l:208 + 32 * l]
                pb1 = params[:, 208 + 32 * l:224 + 32 * l]
                m0 = params[:, 512 + 32 * l:528 + 32 * l]
                m1 = params[:, 528 + 32 * l:544 + 32 * l]
                v.tensor_tensor(tmpA[:, 0:16], pb0, m0, ALU.mult)
                i1 = v.tensor_tensor(tmpA[:, 16:32], pb1, m1, ALU.mult)
                tw = s_v.inc(i1)
                v.wait_ge(s_v.h, tw)
                iv = v.tensor_add(params[:, 576 + 16 * l:592 + 16 * l],
                                  tmpA[:, 0:16], tmpA[:, 16:32])
            s_v.inc(iv)
        tok_tables = s_v.n

        psT_free = [0]
        lnout_free = [0, 0]

        def heat(t, n):
            for _ in range(n):
                t.matmul(ps_p[:, 0:128], ident[:], ident[:], start=True, stop=True)

        lnpar = {"i": 0}

        def emit_ln_chunk(x_src_ap, dst_writer, pre_v=None, pre_x=None,
                          heat_ok=True):
            par = lnpar["i"] % 2
            lnpar["i"] += 1
            C = 8 + 32 * par
            lnout = lnout2_bufs[par]

            def sb_(k):
                return statsb[:, C + k:C + k + 1]
            """LN over free dim of staged f32 [128,2048]; bf16 result transposed
            into dstT tiles at [f, within_off:within_off+128], f=0..15."""
            @blk.vector
            def _(v):
                if pre_x is not None:
                    v.wait_ge(s_x[pre_x[0]].h, pre_x[1])
                if pre_v is not None:
                    v.wait_ge(s_v.h, pre_v)
                iv = v.reduce_sum(sb_(0), x_src_ap, axis=AX.X)
                s_v.inc(iv)
            t1 = s_v.n

            V0 = s_v.n
            t3, t5, t7 = V0 + 3, V0 + 4, V0 + 6
            tk_ = {}

            @blk.scalar
            def _(sc):
                if pre_x is not None:
                    sc.wait_ge(s_x[pre_x[0]].h, pre_x[1])
                if pre_v is not None:
                    sc.wait_ge(s_v.h, pre_v)
                ii = sc.activation(dumpb[:], x_src_ap, AF.Square,
                                   accum_out=sb_(1))
                tk_["t2"] = s_sc.inc(ii)
                sc.wait_ge(s_v.h, t3)
                ii = sc.activation(sb_(5), sb_(4), AF.Ln,
                                   bias=np.float32(EPS))
                tk_["t4"] = s_sc.inc(ii)
                sc.wait_ge(s_v.h, t5)
                ii = sc.activation(sb_(7), sb_(6), AF.Exp)
                tk_["t6"] = s_sc.inc(ii)
                sc.wait_ge(s_v.h, t7)
                if lnout_free[par]:
                    sc.wait_ge(s_t.h, lnout_free[par])
                ii = sc.activation(lnout[:], x_src_ap, AF.Identity,
                                   bias=sb_(9), scale=sb_(7))
                s_sc.inc(ii)
            t2, t4, t6 = tk_["t2"], tk_["t4"], tk_["t6"]
            t8 = s_sc.n

            @blk.vector
            def _(v):
                v.wait_ge(s_sc.h, t2)
                i0 = v.tensor_scalar_mul(sb_(2), sb_(0),
                                         float(1.0 / D))
                tw = s_v.inc(i0)
                v.wait_ge(s_v.h, tw)
                i1 = v.tensor_tensor(sb_(3), sb_(2),
                                     sb_(2), ALU.mult)
                tw = s_v.inc(i1)
                v.wait_ge(s_v.h, tw)
                iv = v.scalar_tensor_tensor(sb_(4), sb_(1),
                                            float(1.0 / D), sb_(3),
                                            ALU.mult, ALU.subtract)
                s_v.inc(iv)
            assert s_v.n == t3, (s_v.n, t3)

            @blk.vector
            def _(v):
                v.wait_ge(s_sc.h, t4)
                iv = v.tensor_scalar_mul(sb_(6), sb_(5),
                                         float(-0.5))
                s_v.inc(iv)
            assert s_v.n == t5, (s_v.n, t5)

            @blk.vector
            def _(v):
                v.wait_ge(s_sc.h, t6)
                i0 = v.tensor_tensor(sb_(8), sb_(2),
                                     sb_(7), ALU.mult)
                tw = s_v.inc(i0)
                v.wait_ge(s_v.h, tw)
                iv = v.tensor_scalar_mul(sb_(9), sb_(8),
                                         float(-1.0))
                s_v.inc(iv)
            assert s_v.n == t7, (s_v.n, t7)

            for r in range(2):
                @blk.tensor
                def _(t, r=r, t8=t8, t3=t3, t6=t6):
                    if r == 0 and heat_ok:
                        t.wait_ge(s_v.h, t3)
                        heat(t, 12)
                        t.wait_ge(s_sc.h, t6)
                        heat(t, 12)
                    t.wait_ge(s_sc.h, t8)
                    if psT_free[0]:
                        t.wait_ge(s_v.h, psT_free[0])
                    for f in range(8):
                        tr = t.transpose(
                            psT[:, f * 128:(f + 1) * 128],
                            lnout[:, (8 * r + f) * 128:(8 * r + f + 1) * 128],
                            ident[:])
                    s_t.inc(tr)
                tt = s_t.n
                lnout_free[par] = tt

                @blk.vector
                def _(v, r=r, tt=tt):
                    v.wait_ge(s_t.h, tt)
                    o = dst_writer(r)
                    if isinstance(o, tuple):
                        v.tensor_copy(o[0], view3(psT, 128)[:, 0:4, :])
                        iv = v.tensor_copy(o[1], view3(psT, 128)[:, 4:8, :])
                    else:
                        iv = v.tensor_copy(o, view3(psT, 128))
                    psT_free[0] = s_v.inc(iv)
            return t8

        xtok = []
        apply_done = []
        for tch in range(8):
            @blk.sync
            def _(sync, tch=tch):
                if tch >= 2:
                    sync.wait_ge(s_sc.h, apply_done[tch - 2])
                xtok.append((tch % 2,
                             s_x[tch % 2].inc(sync.dma_start(out=xst[tch % 2][:],
                                                             in_=dx_own[tch]), 16)))
            def _w1(r, tch=tch):
                t = ln1ownT_p[r]
                return view3(t, 1024)[:, 0:8, tch * 128:(tch + 1) * 128]
            emit_ln_chunk(xst[tch % 2][:], _w1, pre_x=xtok[tch])
            apply_done.append(s_sc.n)
        ln1_done_v = psT_free[0]

        psq = [ps_y[:, 0:1024], ps_s[:, 0:1024]]
        rope_done = []
        q_mm_tok = []
        for h in range(16):
            @blk.tensor
            def _(t, h=h):
                t.wait_ge(s_wq[h % 2].h, 16 * (h // 2 + 1))
                if h == 0:
                    t.wait_ge(s_v.h, ln1_done_v)
                if h >= 2:
                    t.wait_ge(s_v.h, rope_done[h - 2])
                pq = psq[h % 2]
                for fc in range(16):
                    for hf in range(2):
                        mm = t.matmul(pq[:, hf * 512:(hf + 1) * 512],
                                      wqr[h % 2][:, fc * 128:(fc + 1) * 128],
                                      ln1ownT(fc, hf * 512, (hf + 1) * 512),
                                      start=(fc == 0), stop=(fc == 15))
                s_t.inc(mm)
            q_mm_tok.append(s_t.n)

            @blk.vector
            def _(v, h=h):
                v.wait_ge(s_t.h, q_mm_tok[h])
                pq = psq[h % 2]
                lo, hi = slice(0, 64), slice(64, 128)
                a = tmpA[0:64, 0:1024]
                b = tmpB[0:64, 0:1024]
                qc = Qsl(h, 0, 1024)
                v.tensor_tensor(a, pq[lo, :], qcos[lo, :], ALU.mult)
                v.tensor_tensor(b, pq[hi, :], qsin[hi, :], ALU.mult)
                v.tensor_sub(qc[lo, :], a, b)
                v.tensor_tensor(tmpA[64:128, 0:1024], pq[lo, :], qsin[lo, :],
                                ALU.mult)
                v.tensor_tensor(tmpB[64:128, 0:1024], pq[hi, :], qcos[hi, :],
                                ALU.mult)
                iv = v.tensor_add(qc[hi, :], tmpA[64:128, 0:1024],
                                  tmpB[64:128, 0:1024])
                rope_done.append(s_v.inc(iv))
            free_tok["wq"].append(("v", rope_done[h]))
        q_done = rope_done[-1]

        psk = [ps_p[:, 0:512], ps_s[:, 0:512]]
        psv = [ps_y[:, 0:512], ps_s[:, 512:1024]]
        kpipe = {"k": 0, "v": 0}
        kv_free = {}
        for pc in range(4):
            for tsb in range(4):
                gi = 8 + pc * 4 + tsb

                @blk.sync
                def _(sync, gi=gi, pc=pc, tsb=tsb):
                    sync.wait_ge(s_sc.h, apply_done[gi - 2])
                    xtok.append((gi % 2,
                                 s_x[gi % 2].inc(sync.dma_start(
                                     out=xst[gi % 2][:],
                                     in_=dx_seq[pc * 4 + tsb]), 16)))
                def _w3(r, tsb=tsb):
                    return view3(lnT, 512)[:, 8 * r:8 * r + 8,
                                           tsb * 128:(tsb + 1) * 128]
                emit_ln_chunk(xst[gi % 2][:], _w3, pre_x=xtok[gi], heat_ok=False)
                apply_done.append(s_sc.n)
            lnT_tok = psT_free[0]

            for g4 in range(4):
                i = kpipe["k"]

                @blk.tensor
                def _(t, pc=pc, g4=g4, i=i, lnT_tok=lnT_tok):
                    t.wait_ge(s_v.h, lnT_tok)
                    if pc == 0 and g4 == 0:
                        t.wait_ge(s_kv.h, 32)
                        t.wait_ge(s_v.h, q_done)
                    if i >= 2:
                        t.wait_ge(s_v.h, kv_free[("k", i - 2)])
                    pk = psk[i % 2]
                    for fc in range(16):
                        mm = t.matmul(
                            pk,
                            wk_sb[:, fc * 512 + g4 * 128:fc * 512 + (g4 + 1) * 128],
                            lnT[:, fc * 512:(fc + 1) * 512],
                            start=(fc == 0), stop=(fc == 15))
                    s_t.inc(mm)
                tk = s_t.n

                @blk.vector
                def _(v, pc=pc, g4=g4, i=i, tk=tk):
                    v.wait_ge(s_t.h, tk)
                    pk = psk[i % 2]
                    lo, hi = slice(0, 64), slice(64, 128)
                    cw = slice(pc * 512, (pc + 1) * 512)
                    a = tmpA[0:64, 0:512]
                    b = tmpB[0:64, 0:512]
                    kc = Ksl(g4, pc * 512, (pc + 1) * 512)
                    v.tensor_tensor(a, pk[lo, :], kcos[lo, cw], ALU.mult)
                    v.tensor_tensor(b, pk[hi, :], ksin[hi, cw], ALU.mult)
                    v.tensor_sub(kc[lo, :], a, b)
                    v.tensor_tensor(tmpA[64:128, 0:512], pk[lo, :], ksin[lo, cw],
                                    ALU.mult)
                    v.tensor_tensor(tmpB[64:128, 0:512], pk[hi, :], kcos[hi, cw],
                                    ALU.mult)
                    iv = v.tensor_add(kc[hi, :], tmpA[64:128, 0:512],
                                      tmpB[64:128, 0:512])
                    kv_free[("k", i)] = s_v.inc(iv)
                kpipe["k"] += 1

            for tsb in range(4):
                i = kpipe["v"]

                @blk.tensor
                def _(t, pc=pc, tsb=tsb, i=i, lnT_tok=lnT_tok):
                    t.wait_ge(s_v.h, lnT_tok)
                    if i >= 2:
                        t.wait_ge(s_v.h, kv_free[("v", i - 2)])
                    pv = psv[i % 2]
                    for fc in range(16):
                        mm = t.matmul(
                            pv,
                            lnT[:, fc * 512 + tsb * 128:fc * 512 + (tsb + 1) * 128],
                            wv_sb[:, fc * 512:(fc + 1) * 512],
                            start=(fc == 0), stop=(fc == 15))
                    s_t.inc(mm)
                tv2 = s_t.n

                @blk.vector
                def _(v, pc=pc, tsb=tsb, i=i, tv2=tv2):
                    v.wait_ge(s_t.h, tv2)
                    sc_ = pc * 4 + tsb
                    iv = v.tensor_copy(Vsl(sc_), psv[i % 2])
                    kv_free[("v", i)] = s_v.inc(iv)
                kpipe["v"] += 1
        kv_done = s_v.n

        EXPC = 24.0
        S0 = s_sc.n
        git = [0]
        pread_grp = [0]
        ptbuf_free = {0: 0, 1: 0}
        psy_free = {0: 0, 1: 0}
        for sp in range(4):
            ja, jb = 2 * sp, 2 * sp + 1
            Ea, Eb = E_SLOT[ja], E_SLOT[jb]

            @blk.vector
            def _(v, sp=sp, ja=ja, jb=jb, Ea=Ea, Eb=Eb):
                if sp == 0:
                    v.wait_ge(s_v.h, kv_done)
                v.tensor_single_scalar(masks[:, 0:Ea * 128], s_iota[:, 0:Ea * 128],
                                       qpos[:, ja:ja + 1], ALU.is_le)
                iv = v.tensor_single_scalar(masks[:, 2048:2048 + Eb * 128],
                                            s_iota[:, 0:Eb * 128],
                                            qpos[:, jb:jb + 1], ALU.is_le)
                s_v.inc(iv)
            mask_tok = s_v.n

            for g4 in range(4):
                for (sub, j, E, moff) in ((0, ja, Ea, 0),
                                          (1, jb, Eb, 2048)):
                    yoff = sub * 512
                    cols = E * 128
                    twobuf = cols <= 1024
                    base_it = git[0]
                    tqk = []
                    for qi in range(4):
                        qh = 4 * g4 + qi
                        it = base_it + qi
                        if twobuf:
                            base = (it % 2) * 1024
                            freedep = it - 2
                        else:
                            base = 0
                            freedep = it - 1

                        @blk.tensor
                        def _(t, j=j, qh=qh, cols=cols, g4=g4, base=base,
                              freedep=freedep, mask_tok=mask_tok):
                            t.wait_ge(s_v.h, mask_tok)
                            if freedep >= 0:
                                t.wait_ge(s_sc.h, S0 + freedep + 1)
                            c = 0
                            while c < cols:
                                w = min(512, cols - c)
                                mm = t.matmul(
                                    ps_s[:, base + c:base + c + w],
                                    Qsl(qh, j * 128, (j + 1) * 128),
                                    Ksl(g4, c, c + w),
                                    start=True, stop=True)
                                c += w
                            s_t.inc(mm)
                        tqk.append(s_t.n)

                    @blk.scalar
                    def _(sc, cols=cols, base_it=base_it, tqk=tuple(tqk),
                          prg=pread_grp[0]):
                        if prg:
                            sc.wait_ge(s_t.h, prg)
                        for qi in range(4):
                            it = base_it + qi
                            base = (it % 2) * 1024 if cols <= 1024 else 0
                            sc.wait_ge(s_t.h, tqk[qi])
                            ii = sc.activation(PbE[qi][:, 0:cols],
                                               ps_s[:, base:base + cols],
                                               AF.Exp, bias=np.float32(-EXPC))
                            s_sc.inc(ii)
                    assert s_sc.n == S0 + base_it + 4, (s_sc.n, S0, base_it)

                    tnorm = []

                    @blk.vector
                    def _(v, cols=cols, moff=moff, base_it=base_it):
                        for qi in range(4):
                            SC = 20 + 3 * (qi % 2)
                            v.wait_ge(s_sc.h, S0 + base_it + qi + 1)
                            i0 = v.scalar_tensor_tensor(
                                PbM[:, 0:cols], PbE[qi][:, 0:cols], float(1.0),
                                masks[:, moff:moff + cols],
                                ALU.mult, ALU.mult,
                                accum_out=statsb[:, SC + 1:SC + 2])
                            t0 = s_v.inc(i0)
                            v.wait_ge(s_v.h, t0)
                            i1 = v.reciprocal(statsb[:, SC + 2:SC + 3],
                                              statsb[:, SC + 1:SC + 2])
                            t1 = s_v.inc(i1)
                            v.wait_ge(s_v.h, t1)
                            iv = v.tensor_scalar_mul(PbE[qi][:, 0:cols],
                                                     PbM[:, 0:cols],
                                                     statsb[:, SC + 2:SC + 3])
                            tnorm.append(s_v.inc(iv))

                    for qi in range(4):
                        nrounds = (E + 7) // 8
                        for r in range(nrounds):
                            n_in_r = min(8, E - 8 * r)

                            @blk.tensor
                            def _(t, qi=qi, r=r, n_in_r=n_in_r, sub=sub,
                                  tn=tnorm[qi] if False else None):
                                t.wait_ge(s_v.h, tnorm[qi])
                                if psT_free[0]:
                                    t.wait_ge(s_v.h, psT_free[0])
                                if ptbuf_free[sub]:
                                    t.wait_ge(s_t.h, ptbuf_free[sub])
                                for k in range(n_in_r):
                                    sc_ = 8 * r + k
                                    tr = t.transpose(
                                        psT[:, k * 128:(k + 1) * 128],
                                        PbE[qi][:, sc_ * 128:(sc_ + 1) * 128],
                                        ident[:])
                                s_t.inc(tr)
                            ttr = s_t.n

                            @blk.vector
                            def _(v, r=r, n_in_r=n_in_r, qi=qi, sub=sub, ttr=ttr):
                                v.wait_ge(s_t.h, ttr)
                                o = view3(PTsl(sub, r), 512)[:, 0:n_in_r,
                                                             qi * 128:(qi + 1) * 128]
                                iv = v.tensor_copy(
                                    o, view3(psT, 128)[:, 0:n_in_r, :])
                                psT_free[0] = s_v.inc(iv)
                    pread_grp[0] = s_t.n
                    git[0] += 4
                    pt_tok = psT_free[0]

                    @blk.tensor
                    def _(t, g4=g4, E=E, yoff=yoff, sub=sub, pt_tok=pt_tok):
                        t.wait_ge(s_v.h, pt_tok)
                        if psy_free[sub]:
                            t.wait_ge(s_v.h, psy_free[sub])
                        for sc_ in range(E):
                            pt = PTsl(sub, sc_ // 8)
                            mm = t.matmul(
                                ps_y[:, yoff:yoff + 512],
                                Vtile(sc_, g4),
                                pt[:, (sc_ % 8) * 512:(sc_ % 8 + 1) * 512],
                                start=(sc_ == 0), stop=(sc_ == E - 1))
                        ptbuf_free[sub] = s_t.inc(mm)
                    tav = s_t.n

                    @blk.vector
                    def _(v, g4=g4, j=j, yoff=yoff, sub=sub, tav=tav):
                        v.wait_ge(s_t.h, tav)
                        o = view3(ysl(g4), 1024)[:, 0:4, j * 128:(j + 1) * 128]
                        i_ = view3(ps_y, 128)[:, 4 * sub:4 * sub + 4, :]
                        iv = v.tensor_copy(o, i_)
                        psy_free[sub] = s_v.inc(iv)
        y_done = s_v.n

        pso = [ps_y[:, 0:1024], ps_s[:, 0:1024]]
        wo_psum_free = {}
        aofm_free = [0]
        wo_mm_tok = []
        for fo in range(16):
            @blk.tensor
            def _(t, fo=fo):
                t.wait_ge(s_wo[fo % 2].h, 16 * (fo // 2 + 1))
                if fo == 0:
                    t.wait_ge(s_v.h, y_done)
                if fo >= 2:
                    t.wait_ge(s_v.h, wo_psum_free[fo % 2])
                po = pso[fo % 2]
                for dc in range(16):
                    yb = y_p[dc // 4]
                    for hf in range(2):
                        mm = t.matmul(po[:, hf * 512:(hf + 1) * 512],
                                      wor[fo % 2][:, dc * 128:(dc + 1) * 128],
                                      yb[:, (dc % 4) * 1024 + hf * 512:
                                         (dc % 4) * 1024 + (hf + 1) * 512],
                                      start=(dc == 0), stop=(dc == 15))
                s_t.inc(mm)
            wo_mm_tok.append(s_t.n)
            free_tok["wo"].append(("t", wo_mm_tok[fo]))

            @blk.vector
            def _(v, fo=fo):
                v.wait_ge(s_t.h, wo_mm_tok[fo])
                if aofm_free[0]:
                    v.wait_ge(s_t.h, aofm_free[0])
                iv = v.tensor_copy(ao_fm[:], pso[fo % 2][:, 0:1024])
                wo_psum_free[fo % 2] = s_v.inc(iv)
            tev = s_v.n

            @blk.tensor
            def _(t, fo=fo, tev=tev):
                t.wait_ge(s_v.h, tev)
                if psT_free[0]:
                    t.wait_ge(s_v.h, psT_free[0])
                for k in range(8):
                    tr = t.transpose(psT[:, k * 128:(k + 1) * 128],
                                     ao_fm[:, k * 128:(k + 1) * 128], ident[:])
                aofm_free[0] = s_t.inc(tr)
            ttr2 = s_t.n

            @blk.vector
            def _(v, fo=fo, ttr2=ttr2):
                v.wait_ge(s_t.h, ttr2)
                o0 = view3(ao_tm_p[0], 2048)[:, 0:4, fo * 128:(fo + 1) * 128]
                o1 = view3(ao_tm_p[1], 2048)[:, 0:4, fo * 128:(fo + 1) * 128]
                v.tensor_copy(o0, view3(psT, 128)[:, 0:4, :])
                iv = v.tensor_copy(o1, view3(psT, 128)[:, 4:8, :])
                psT_free[0] = s_v.inc(iv)
        ao_done = psT_free[0]

        spill_tok = []
        add_v_tok = []
        for tch in range(8):
            @blk.sync
            def _(sync, tch=tch):
                if tch < 2:
                    sync.wait_ge(s_v.h, ao_done)
                else:
                    sync.wait_ge(s_sc.h, apply_done[24 + tch - 2])
                xtok.append((tch % 2,
                             s_x[tch % 2].inc(sync.dma_start(out=xst[tch % 2][:],
                                                             in_=dx_own[tch]), 16)))

            @blk.vector
            def _(v, tch=tch):
                v.wait_ge(s_x[xtok[24 + tch][0]].h, xtok[24 + tch][1])
                if tch == 0:
                    v.wait_ge(s_v.h, ao_done)
                if tch >= 2:
                    v.wait_ge(s_sp[spill_tok[tch - 2][0]].h, spill_tok[tch - 2][1])
                dst = tmpA if tch % 2 == 0 else tmpB
                ao_pc = ao_tm_p[tch // 4][:, (tch % 4) * 2048:(tch % 4 + 1) * 2048]
                iv = v.tensor_add(dst[:], xst[tch % 2][:], ao_pc)
                add_v_tok.append(s_v.inc(iv))

            @blk.scalar
            def _(sc, tch=tch):
                sc.wait_ge(s_v.h, add_v_tok[tch])
                src = tmpA if tch % 2 == 0 else tmpB
                spill_tok.append((tch % 2,
                                  s_sp[tch % 2].inc(sc.dma_start(out=dx2[tch],
                                                                 in_=src[:]), 16)))

            src = tmpA if tch % 2 == 0 else tmpB
            def _w5(r, tch=tch):
                return (view3(ln2xT_p[2 * r], 1024)[:, 0:4,
                                                    tch * 128:(tch + 1) * 128],
                        view3(ln2xT_p[2 * r + 1], 1024)[:, 0:4,
                                                        tch * 128:(tch + 1) * 128])
            emit_ln_chunk(src[:], _w5, pre_v=add_v_tok[tch])
            apply_done.append(s_sc.n)
        ln2_done = psT_free[0]

        psa = [ps_y[:, 0:1024], ps_s[:, 0:1024]]
        sig_tok = {}
        tanh_tok = {}
        hstt_tok = {}
        for fo in range(16):
            tmm_p = {}
            for p in range(2):
                i = 2 * fo + p

                @blk.tensor
                def _(t, fo=fo, p=p, i=i):
                    t.wait_ge(s_wa[i % 2].h, 16 * (i // 2 + 1))
                    if i == 0:
                        t.wait_ge(s_v.h, ln2_done)
                    if fo >= 1:
                        t.wait_ge(s_sc.h, (sig_tok if p == 0 else tanh_tok)[fo - 1])
                    pa_ = psa[p]
                    for fc in range(16):
                        for hf in range(2):
                            mm = t.matmul(pa_[:, hf * 512:(hf + 1) * 512],
                                          war[i % 2][:, fc * 128:(fc + 1) * 128],
                                          ln2xT_p[fc // 4][:, (fc % 4) * 1024 + hf * 512:
                                                  (fc % 4) * 1024 + (hf + 1) * 512],
                                          start=(fc == 0), stop=(fc == 15))
                    s_t.inc(mm)
                free_tok["wa"].append(("t", s_t.n))
                tmm_p[p] = s_t.n

            tmx = s_v.n + 1

            @blk.scalar
            def _(sc, fo=fo, tmm_p=dict(tmm_p), tmx=tmx):
                sc.wait_ge(s_t.h, tmm_p[0])
                if fo == 0:
                    sc.wait_ge(s_sp[spill_tok[6][0]].h, spill_tok[6][1])
                    sc.wait_ge(s_sp[spill_tok[7][0]].h, spill_tok[7][1])
                if fo >= 1:
                    sc.wait_ge(s_v.h, hstt_tok[fo - 1])
                ii = sc.activation(tmpA[:, 0:1024], psa[0], AF.Sigmoid,
                                   bias=params[:, fo:fo + 1])
                sig_tok[fo] = s_sc.inc(ii)
                sc.wait_ge(s_t.h, tmm_p[1])
                ii = sc.activation(tmpB[:, 0:1024], psa[1], AF.Tanh,
                                   bias=params[:, 16 + fo:17 + fo])
                tanh_tok[fo] = s_sc.inc(ii)
                sc.wait_ge(s_v.h, tmx)
                ii = sc.activation(tmpB[:, 1024:2048], tmpB[:, 0:1024], AF.Sin)
                s_sc.inc(ii)
            tsn = s_sc.n

            @blk.vector
            def _(v, fo=fo):
                v.wait_ge(s_sc.h, tanh_tok[fo])
                v.tensor_scalar(tmpB[:, 1024:2048], tmpB[:, 0:1024], float(PI),
                                float(PI / 4), ALU.mult, ALU.add)
                iv = v.tensor_single_scalar(tmpA[:, 1024:2048], tmpB[:, 1024:2048],
                                            float(PI), ALU.is_gt)
                iv = v.scalar_tensor_tensor(tmpB[:, 0:1024], tmpA[:, 1024:2048],
                                            float(-TWO_PI), tmpB[:, 1024:2048],
                                            ALU.mult, ALU.add)
                s_v.inc(iv)
            assert s_v.n == tmx, (s_v.n, tmx)

            @blk.vector
            def _(v, fo=fo, tsn=tsn):
                v.wait_ge(s_sc.h, tsn)
                iv = v.scalar_tensor_tensor(hbuf[:, fo * 1024:(fo + 1) * 1024],
                                            tmpB[:, 1024:2048], float(np.sqrt(2.0)),
                                            tmpA[:, 0:1024], ALU.mult, ALU.mult)
                hstt_tok[fo] = s_v.inc(iv)
        h_done = s_v.n

        pse = [ps_p[:, 0:512], ps_s[:, 1536:2048]]
        psent = [ps_y[:, 0:512], ps_y[:, 512:1024]]
        ne_used = [0]
        npw_used = [0]
        nh_used = [0]
        e_evac_tok = {}
        prev_ent_last = [0]
        prev_block_apply = [0]

        def emit_fm_ln(gcol, bcol, src_tok, tb, stats_wait_v, weave=None):
            sq_tok = {}
            stat_mm_tok = {}

            @blk.tensor
            def _(t):
                t.wait_ge(s_v.h, src_tok)
                if stats_wait_v[0]:
                    t.wait_ge(s_v.h, stats_wait_v[0])
                for fc in range(16):
                    mm = t.matmul(ps_s[:, 0:512], ones32[:], hintsl(fc),
                                  start=(fc == 0), stop=(fc == 15))
                s_t.inc(mm)
            SQ0 = s_sc.n
            for fc in range(16):
                sq_tok[fc] = SQ0 + fc + 1
            for g4_ in range(4):
                @blk.tensor
                def _(t, g4_=g4_):
                    for fc in range(4 * g4_, 4 * g4_ + 4):
                        t.wait_ge(s_sc.h, sq_tok[fc])
                        mm = t.matmul(ps_s[:, 512:1024], ones32[:],
                                      tmpB[:, (fc % 4) * 512:(fc % 4 + 1) * 512],
                                      start=(fc == 0), stop=(fc == 15))
                        stat_mm_tok[fc] = s_t.inc(mm)
                if weave:
                    weave(1)
            tstat = s_t.n
            tvar = s_v.n + 1
            trr = s_v.n + 2

            @blk.scalar
            def _(sc):
                sc.wait_ge(s_v.h, src_tok)
                for g4_ in range(4):
                    if g4_ >= 1:
                        sc.wait_ge(s_t.h, stat_mm_tok[4 * g4_ - 1])
                    for fc in range(4 * g4_, 4 * g4_ + 4):
                        ii = sc.activation(tmpB[:, (fc % 4) * 512:(fc % 4 + 1) * 512],
                                           hintsl(fc), AF.Square)
                        assert s_sc.inc(ii) == sq_tok[fc]
                sc.wait_ge(s_v.h, tvar)
                ii = sc.activation(tmpB[:, 512:1024], tmpB[:, 0:512], AF.Ln,
                                   bias=np.float32(EPS))
                s_sc.inc(ii)
                sc.wait_ge(s_v.h, trr)
                ii = sc.activation(tmpA[:, 512:1024], tmpB[:, 0:512], AF.Exp)
                s_sc.inc(ii)
            tln = s_sc.n - 1
            trstd = s_sc.n

            @blk.vector
            def _(v):
                v.wait_ge(s_t.h, tstat)
                v.tensor_scalar_mul(tmpA[:, 0:512], ps_s[:, 0:512], float(1.0 / D))
                v.tensor_tensor(tmpA[:, 1024:1536], tmpA[:, 0:512],
                                tmpA[:, 0:512], ALU.mult)
                iv = v.scalar_tensor_tensor(tmpB[:, 0:512], ps_s[:, 512:1024],
                                            float(1.0 / D), tmpA[:, 1024:1536],
                                            ALU.mult, ALU.subtract)
                s_v.inc(iv)
            assert s_v.n == tvar, (s_v.n, tvar)
            stats_wait_v[0] = tvar

            @blk.vector
            def _(v):
                v.wait_ge(s_sc.h, tln)
                iv = v.tensor_scalar_mul(tmpB[:, 0:512], tmpB[:, 512:1024],
                                         float(-0.5))
                s_v.inc(iv)
            assert s_v.n == trr, (s_v.n, trr)

            app_tok = {}
            for fc in range(16):
                @blk.vector
                def _(v, fc=fc):
                    if fc == 0:
                        v.wait_ge(s_sc.h, trstd)
                    pb = (fc % 2) * 512
                    pa = 1024 + (fc % 2) * 512
                    v.tensor_sub(tmpB[:, pb:pb + 512], hintsl(fc), tmpA[:, 0:512])
                    v.tensor_tensor(tmpB[:, pa:pa + 512], tmpB[:, pb:pb + 512],
                                    tmpA[:, 512:1024], ALU.mult)
                    iv = v.tensor_scalar(
                        hbuf[:, fc * 1024 + tb * 512:fc * 1024 + (tb + 1) * 512],
                        tmpB[:, pa:pa + 512],
                        params[:, gcol + fc:gcol + fc + 1],
                        params[:, bcol + fc:bcol + fc + 1],
                        ALU.mult, ALU.add)
                    app_tok[fc] = s_v.inc(iv)
                if weave and fc % 2 == 1:
                    weave(1)
            return app_tok

        stats_wait_v = [0]
        BLOCKS = [(0, 0), (0, 1), (1, 0), (1, 1)]
        h_ready = {0: h_done, 1: h_done}

        def make_eprod(bi):
            l, tb = BLOCKS[bi]
            closures = []
            for ec in range(64):
                def emit_ec(ec=ec, l=l, tb=tb, bi=bi):
                    i = ne_used[0]
                    ffn_order.append(("we", i, l, ec))

                    @blk.tensor
                    def _(t, l=l, tb=tb, ec=ec, i=i, bi=bi):
                        t.wait_ge(s_we[i % 3].h, 16 * (i // 3 + 1))
                        hr = h_ready[bi]
                        if ec == 0 and not isinstance(hr, dict):
                            t.wait_ge(s_v.h, hr)
                        if i >= 2:
                            t.wait_ge(s_v.h, e_evac_tok[i - 2])
                        pe = pse[ec % 2]
                        for fc in range(16):
                            if ec == 0 and isinstance(hr, dict):
                                t.wait_ge(s_v.h, hr[fc])
                            mm = t.matmul(pe,
                                          ewr[i % 3][:, fc * 128:(fc + 1) * 128],
                                          hbuf[:, fc * 1024 + tb * 512:
                                               fc * 1024 + (tb + 1) * 512],
                                          start=(fc == 0), stop=(fc == 15))
                        s_t.inc(mm)
                    te = s_t.n
                    free_tok["we"].append(("t", te))

                    @blk.vector
                    def _(v, l=l, ec=ec, i=i, te=te):
                        v.wait_ge(s_t.h, te)
                        if ec == 0 and prev_ent_last[0]:
                            v.wait_ge(s_t.h, prev_ent_last[0])
                        ii = v.tensor_scalar_add(etile(ec), pse[ec % 2],
                                                 params[:, 32 + 64 * l + ec:
                                                        33 + 64 * l + ec])
                        e_evac_tok[i] = s_v.inc(ii)
                    ne_used[0] += 1
                closures.append(emit_ec)
            return closures

        pending_ep = [None]

        def weave(n):
            c = pending_ep[0]
            if not c:
                return
            k = 0
            while c and k < n:
                c.pop(0)()
                k += 1

        for bi, (l, tb) in enumerate(BLOCKS):
            if bi == 0:
                for fn in make_eprod(0):
                    fn()
            else:
                weave(64)
            e_done = s_v.n
            if bi + 1 < 4:
                pending_ep[0] = make_eprod(bi + 1)

            ent_evac = {}
            for fc in range(16):
                for hf in range(4):
                    i = npw_used[0]
                    ffn_order.append(("wp", i, l, fc, hf))

                    @blk.tensor
                    def _(t, fc=fc, hf=hf, i=i, e_done=e_done):
                        t.wait_ge(s_wp[i % 2].h, 16 * (i // 2 + 1))
                        if fc == 0 and hf == 0:
                            t.wait_ge(s_v.h, e_done)
                        if fc >= 2 and hf == 0:
                            t.wait_ge(s_v.h, ent_evac[fc - 2])
                        pp = psent[fc % 2]
                        for k in range(16):
                            ec = hf * 16 + k
                            mm = t.matmul(pp,
                                          pwr[i % 2][:, k * 128:(k + 1) * 128],
                                          etile(ec),
                                          start=(ec == 0), stop=(ec == 63))
                        s_t.inc(mm)
                    free_tok["wp"].append(("t", s_t.n))
                    npw_used[0] += 1
                tent = s_t.n

                @blk.vector
                def _(v, l=l, tb=tb, fc=fc, tent=tent):
                    v.wait_ge(s_t.h, tent)
                    iv = v.scalar_tensor_tensor(
                        hintsl(fc), psent[fc % 2],
                        params[:, 160 + 16 * l + fc:161 + 16 * l + fc],
                        hbuf[:, fc * 1024 + tb * 512:fc * 1024 + (tb + 1) * 512],
                        ALU.add, ALU.add)
                    ent_evac[fc] = s_v.inc(iv)
            prev_ent_last[0] = s_t.n
            hent_done = s_v.n

            h3_app = emit_fm_ln(320 + 32 * l, 336 + 32 * l, hent_done, tb,
                                stats_wait_v, weave)

            psq0 = [ps_s[:, 0:512], ps_y[:, 0:512]]
            psq1 = [ps_s[:, 512:1024], ps_y[:, 512:1024]]
            path_evac = {}
            for fo in range(16):
                for p in range(2):
                    i = nh_used[0]
                    ffn_order.append(("wh", i, l, p, fo))

                    @blk.tensor
                    def _(t, fo=fo, p=p, i=i, h3_app=h3_app):
                        t.wait_ge(s_wh[i % 3].h, 16 * (i // 3 + 1))
                        if fo == 0 and p == 0:
                            t.wait_ge(s_v.h, stats_wait_v[0])
                        if fo >= 2 and p == 0:
                            t.wait_ge(s_v.h, path_evac[fo - 2])
                        pp = (psq0 if p == 0 else psq1)[fo % 2]
                        for dc in range(16):
                            if fo == 0 and p == 0:
                                t.wait_ge(s_v.h, h3_app[dc])
                            mm = t.matmul(pp,
                                          par[i % 3][:, dc * 128:(dc + 1) * 128],
                                          hbuf[:, dc * 1024 + tb * 512:
                                               dc * 1024 + (tb + 1) * 512],
                                          start=(dc == 0), stop=(dc == 15))
                        s_t.inc(mm)
                    free_tok["wh"].append(("t", s_t.n))
                    nh_used[0] += 1
                tpa = s_t.n
                if fo % 2 == 1:
                    weave(2)

                @blk.vector
                def _(v, l=l, tb=tb, fo=fo, tpa=tpa):
                    v.wait_ge(s_t.h, tpa)
                    m0 = params[:, 512 + 32 * l + fo:513 + 32 * l + fo]
                    m1 = params[:, 528 + 32 * l + fo:529 + 32 * l + fo]
                    bc = params[:, 576 + 16 * l + fo:577 + 16 * l + fo]
                    v.scalar_tensor_tensor(tmpA[:, 1024:1536], psq0[fo % 2], m0,
                                           hbuf[:, fo * 1024 + tb * 512:
                                                fo * 1024 + (tb + 1) * 512],
                                           ALU.mult, ALU.add)
                    v.scalar_tensor_tensor(tmpA[:, 1536:2048], psq1[fo % 2], m1,
                                           tmpA[:, 1024:1536], ALU.mult, ALU.add)
                    iv = v.tensor_scalar_add(hintsl(fo),
                                             tmpA[:, 1536:2048], bc)
                    path_evac[fo] = s_v.inc(iv)
            interf_done = s_v.n

            h_ready[bi + 2] = emit_fm_ln(384 + 32 * l, 400 + 32 * l,
                                         interf_done, tb, stats_wait_v, weave)
            prev_block_apply[0] = h_ready[bi + 2][15]
        ffn_done = prev_block_apply[0]

        reload_tok = []
        add_tok = []
        outdma_tok = []
        for tch in range(8):
            @blk.scalar
            def _(sc, tch=tch):
                if tch < 2:
                    sc.wait_ge(s_v.h, ffn_done)
                else:
                    sc.wait_ge(s_v.h, add_tok[tch - 2])
                reload_tok.append((tch % 2,
                                   s_x[tch % 2].inc(sc.dma_start(out=xst[tch % 2][:],
                                                                 in_=dx2[tch]), 16)))

            @blk.tensor
            def _(t, tch=tch):
                if tch == 0:
                    t.wait_ge(s_v.h, ffn_done)
                if psT_free[0]:
                    t.wait_ge(s_v.h, psT_free[0])
                for k in range(8):
                    tr = t.transpose(psT[:, k * 128:(k + 1) * 128],
                                     hbuf[:, k * 1024 + tch * 128:
                                          k * 1024 + (tch + 1) * 128], ident[:])
                s_t.inc(tr)
            th1 = s_t.n

            @blk.vector
            def _(v, tch=tch, th1=th1):
                v.wait_ge(s_t.h, th1)
                o = view3(lnout, 128)[:, 0:8, :]
                iv = v.tensor_copy(o, view3(psT, 128))
                psT_free[0] = s_v.inc(iv)
            tev1 = psT_free[0]

            @blk.tensor
            def _(t, tch=tch, tev1=tev1):
                t.wait_ge(s_v.h, tev1)
                for k in range(8):
                    tr = t.transpose(psT[:, k * 128:(k + 1) * 128],
                                     hbuf[:, (k + 8) * 1024 + tch * 128:
                                          (k + 8) * 1024 + (tch + 1) * 128],
                                     ident[:])
                s_t.inc(tr)
            th2 = s_t.n

            @blk.vector
            def _(v, tch=tch, th2=th2):
                v.wait_ge(s_t.h, th2)
                o = view3(lnout, 128)[:, 8:16, :]
                iv = v.tensor_copy(o, view3(psT, 128))
                psT_free[0] = s_v.inc(iv)
                v.wait_ge(s_x[reload_tok[tch][0]].h, reload_tok[tch][1])
                if tch >= 2:
                    v.wait_ge(s_o[outdma_tok[tch - 2][0]].h,
                              outdma_tok[tch - 2][1])
                dst = tmpA if tch % 2 == 0 else tmpB
                iv = v.tensor_add(dst[:], xst[tch % 2][:], lnout[:])
                add_tok.append(s_v.inc(iv))

            @blk.sync
            def _(sync, tch=tch):
                sync.wait_ge(s_v.h, add_tok[tch])
                src = tmpA if tch % 2 == 0 else tmpB
                outdma_tok.append((tch % 2,
                                   s_o[tch % 2].inc(sync.dma_start(out=dout[tch],
                                                                   in_=src[:]), 16)))

        @blk.sync
        def _(sync):
            sync.wait_ge(s_o[0].h, s_o[0].n)
            sync.wait_ge(s_o[1].h, s_o[1].n)

        sem_of = {"v": s_v, "t": s_t, "sc": s_sc}

        @blk.gpsimd
        def _(g):
            for h in range(16):
                if h >= 2:
                    kind, tok = free_tok["wq"][h - 2]
                    g.wait_ge(sem_of[kind].h, tok)
                s_wq[h % 2].inc(g.dma_start(
                    out=view3(wqr[h % 2], 128),
                    in_=dwq[h].rearrange("(k p) e -> p k e", p=128)), 16)
            g.wait_ge(s_v.h, kv_done)
            for i in range(16):
                if i >= 2:
                    kind, tok = free_tok["wo"][i - 2]
                    g.wait_ge(sem_of[kind].h, tok)
                s_wo[i % 2].inc(g.dma_start(
                    out=view3(wor[i % 2], 128),
                    in_=dwo[i].rearrange("(k p) e -> p k e", p=128)), 16)
            g.wait_ge(s_v.h, ao_done)
            for i in range(32):
                src = dwamp[i // 2] if i % 2 == 0 else dwph[i // 2]
                if i >= 2:
                    kind, tok = free_tok["wa"][i - 2]
                    g.wait_ge(sem_of[kind].h, tok)
                s_wa[i % 2].inc(g.dma_start(
                    out=view3(war[i % 2], 128),
                    in_=src.rearrange("(k p) e -> p k e", p=128)), 16)
            ring = {"we": 3, "wp": 2, "wh": 3}
            sems = {"we": s_we, "wp": s_wp, "wh": s_wh}
            for rec in ffn_order:
                kind, i = rec[0], rec[1]
                if i >= ring[kind]:
                    fk, tok = free_tok[kind][i - ring[kind]]
                    g.wait_ge(sem_of[fk].h, tok)
                if kind == "we":
                    _, _, l, ec = rec
                    src = dwe[l, ec]
                    dst = view3(ewr[i % 3], 128)
                elif kind == "wp":
                    _, _, l, fc, hf = rec
                    src = dwpw[l, fc, hf * 2048:(hf + 1) * 2048]
                    dst = view3(pwr[i % 2], 128)
                else:
                    _, _, l, p, fo = rec
                    src = dwpa[l, p, fo]
                    dst = view3(par[i % 3], 128)
                sems[kind][i % ring[kind]].inc(g.dma_start(
                    out=dst, in_=src.rearrange("(k p) e -> p k e", p=128)), 16)

    return nc


_NC_CACHE = {}


def _get_nc():
    if "nc" not in _NC_CACHE:
        _NC_CACHE["nc"] = build_program()
    return _NC_CACHE["nc"]


def kernel(L0_attn_x, L0_attn_Wq, L0_attn_Wk, L0_attn_Wv, L0_attn_Wo,
           L0_ffn_amp_w, L0_ffn_amp_b, L0_ffn_ph_w, L0_ffn_ph_b,
           L0_ffn_ent_w, L0_ffn_ent_b, L0_ffn_ent_pw, L0_ffn_ent_pb,
           L0_ffn_path_w, L0_ffn_path_b, L0_ffn_phase,
           L0_ffn_ln3_g, L0_ffn_ln3_b, L0_ffn_ln4_g, L0_ffn_ln4_b):
    x = np.ascontiguousarray(np.asarray(L0_attn_x, dtype=np.float32))

    def stage_cols(w, nb):
        w = np.asarray(w, dtype=np.float32)
        return np.ascontiguousarray(
            w.reshape(w.shape[0], nb, 128).transpose(1, 0, 2))

    wq_s = stage_cols(L0_attn_Wq, 16)
    wo_s = stage_cols(L0_attn_Wo, 16)
    ampw_s = stage_cols(L0_ffn_amp_w, 16)
    phw_s = stage_cols(L0_ffn_ph_w, 16)
    entw_s = np.stack([stage_cols(np.asarray(L0_ffn_ent_w)[l], 64)
                       for l in range(2)])
    pw_s = np.stack([stage_cols(np.asarray(L0_ffn_ent_pw)[l], 16)
                     for l in range(2)])
    path_s = np.stack([np.stack([stage_cols(np.asarray(L0_ffn_path_w)[l, p], 16)
                                 for p in range(2)]) for l in range(2)])
    wk = np.ascontiguousarray(np.asarray(L0_attn_Wk, dtype=np.float32))
    wv = np.ascontiguousarray(np.asarray(L0_attn_Wv, dtype=np.float32))

    pv = np.zeros((512, 128), np.float32)

    def put(vec, c0):
        v = np.asarray(vec, dtype=np.float32).reshape(-1, 128)
        pv[c0:c0 + v.shape[0]] = v

    put(L0_ffn_amp_b, 0)
    put(L0_ffn_ph_b, 16)
    put(np.asarray(L0_ffn_ent_b)[0], 32)
    put(np.asarray(L0_ffn_ent_b)[1], 96)
    put(np.asarray(L0_ffn_ent_pb)[0], 160)
    put(np.asarray(L0_ffn_ent_pb)[1], 176)
    put(np.asarray(L0_ffn_path_b)[0, 0], 192)
    put(np.asarray(L0_ffn_path_b)[0, 1], 208)
    put(np.asarray(L0_ffn_path_b)[1, 0], 224)
    put(np.asarray(L0_ffn_path_b)[1, 1], 240)
    put(np.asarray(L0_ffn_phase)[0, 0], 256)
    put(np.asarray(L0_ffn_phase)[0, 1], 272)
    put(np.asarray(L0_ffn_phase)[1, 0], 288)
    put(np.asarray(L0_ffn_phase)[1, 1], 304)
    put(np.asarray(L0_ffn_ln3_g)[0], 320)
    put(np.asarray(L0_ffn_ln3_b)[0], 336)
    put(np.asarray(L0_ffn_ln3_g)[1], 352)
    put(np.asarray(L0_ffn_ln3_b)[1], 368)
    put(np.asarray(L0_ffn_ln4_g)[0], 384)
    put(np.asarray(L0_ffn_ln4_b)[0], 400)
    put(np.asarray(L0_ffn_ln4_g)[1], 416)
    put(np.asarray(L0_ffn_ln4_b)[1], 432)
    pvec = np.ascontiguousarray(pv.reshape(4, 128, 128))

    nc = _get_nc()
    in_maps = []
    for c in range(8):
        b, par_ = c // 2, c % 2
        gbl = GBLK[par_]
        rows = np.concatenate([np.arange(gb * 128, (gb + 1) * 128) for gb in gbl])
        x_own = np.ascontiguousarray(x[b][rows]).reshape(8, 128, D)
        x_seq = np.ascontiguousarray(x[b]).reshape(16, 128, D)
        qpos = np.empty((128, 8), np.float32)
        qrow = np.empty((128, 1024), np.float32)
        for j, gb in enumerate(gbl):
            qpos[:, j] = gb * 128 + np.arange(128)
            qrow[:, j * 128:(j + 1) * 128] = gb * 128 + np.arange(128)[None, :]
        in_maps.append({
            "x_own": x_own, "x_seq": x_seq, "qpos": qpos, "qposrow": qrow,
            "pvec": pvec, "wq_s": wq_s, "wk": wk, "wv": wv, "wo_s": wo_s,
            "ampw_s": ampw_s, "phw_s": phw_s, "entw_s": entw_s,
            "pw_s": pw_s, "path_s": path_s,
        })

    res = run_bass_kernel_spmd(nc, in_maps, core_ids=list(range(8)))

    out = np.empty((4, 2048, D), np.float32)
    for c in range(8):
        b, par_ = c // 2, c % 2
        gbl = GBLK[par_]
        o = np.asarray(res.results[c]["out"]).reshape(1024, D)
        for j, gb in enumerate(gbl):
            out[b, gb * 128:(gb + 1) * 128] = o[j * 128:(j + 1) * 128]
    return out
